# version 06
# baseline (speedup 1.0000x reference)
import math
from contextlib import ExitStack

import numpy as np
import ml_dtypes

import concourse.bass as bass
import concourse.mybir as mybir
from concourse.bass_utils import run_bass_kernel_spmd

F32 = mybir.dt.float32
BF16 = mybir.dt.bfloat16
AF = mybir.ActivationFunctionType
ALU = mybir.AluOpType
AX = mybir.AxisListType

T = 2048
D = 1024
TT = T // 128
DC = D // 128
NCORES = 8
BF = ml_dtypes.bfloat16

ENGINES = ("tensor", "vector", "scalar", "gpsimd", "sync")


class _Op:
    __slots__ = ("eng", "fn", "reads", "writes", "late", "kind", "dsem",
                 "dtarget", "waits", "tick", "idx")

    def __init__(self, eng, fn, reads, writes, late, kind, dsem):
        self.eng = eng
        self.fn = fn
        self.reads = reads
        self.writes = writes
        self.late = late
        self.kind = kind
        self.dsem = dsem
        self.dtarget = None
        self.waits = []
        self.tick = False
        self.idx = -1


def _overlap(a, b):
    return a[0] == b[0] and a[1] < b[2] and b[1] < a[2]


class TBuf:

    def __init__(self, ap, name, bpe, off=0):
        self._ap = ap
        self.name = name
        self.bpe = bpe
        self.off = off

    def __getitem__(self, idx):
        return self._ap[idx]

    def ap(self):
        return self._ap

    @property
    def shape(self):
        return self._ap.shape


def _bpe(t):
    if hasattr(t, "bpe"):
        return t.bpe
    d = getattr(t, "dtype", None)
    return 2 if d == BF16 else 4


class Prog:

    def __init__(self, nc, stack):
        self.nc = nc
        self.stack = stack
        self.ops = []
        self.last_w = {}
        self.readers = {}
        self.dgroups = {}
        self.psum = [stack.enter_context(nc.psum_tensor(f"pb{i}", [128, 512], F32))
                     for i in range(6)]
        self.psum += [stack.enter_context(nc.psum_tensor(f"pb{i}", [128, 1024], BF16))
                      for i in (6, 7)]
        self._sb = {}
        self._esem = {}

    GLOBALS = ("xs", "x_fm", "stats", "lnbuf", "ident", "identb", "lns_sq")

    def begin_scope(self):
        self._scope_cache = {}
        self._cursor = 0

    def sb(self, name, shape, dtype):
        if not getattr(self, "pooling", False) or name in self.GLOBALS:
            if name not in self._sb:
                self._sb[name] = self.stack.enter_context(
                    self.nc.sbuf_tensor(name, shape, dtype))
            return self._sb[name]
        if not hasattr(self, "_scope_cache"):
            self._scope_cache = {}
            self._cursor = 0
        if not hasattr(self, "_arena_hwm"):
            self._arena_hwm = 0
        if name in self._scope_cache:
            return self._scope_cache[name]
        bpe = 2 if dtype == BF16 else 4
        nbytes = -(-shape[1] * bpe // 512) * 512
        if not hasattr(self, "_arena"):
            self._arena = self.stack.enter_context(
                self.nc.sbuf_tensor("arena", [128, self.arena_bytes // 2], BF16))
        off = self._cursor
        self._cursor += nbytes
        self._arena_hwm = max(self._arena_hwm, self._cursor)
        assert self._cursor <= self.arena_bytes, (
            f"arena overflow at {name}: {self._cursor} > {self.arena_bytes}")
        base = self._arena.ap()
        if dtype == F32:
            base = base.bitcast(F32)
            view = base[0:shape[0], off // 4: off // 4 + shape[1]]
        else:
            view = base[0:shape[0], off // 2: off // 2 + shape[1]]
        tb = TBuf(view, "arena", bpe, off)
        self._scope_cache[name] = tb
        return tb

    def pr(self, i):
        return (f"@psum{i}", 0, 512)

    def dgroup(self, name):
        if name not in self.dgroups:
            h = self.stack.enter_context(self.nc.semaphore("dg_" + name))
            self.dgroups[name] = {"sem": h, "count": 0}
        return self.dgroups[name]

    def dma2(self, queue, fn, reads=(), writes=(), group="dflt"):
        g0 = self.dgroup(group + "#0")
        g1 = self.dgroup(group + "#1")
        n = g0["count"] + g1["count"]
        return self.dma(queue, fn, reads, writes, group=group + "#" + str(n % 2))

    def add(self, eng, fn, reads=(), writes=(), late=False):
        op = _Op(eng, fn, list(reads), list(writes), late, "c", None)
        self._resolve(op)
        self.ops.append(op)
        return op

    def dma(self, queue, fn, reads=(), writes=(), group="dflt"):
        g = self.dgroup(group)
        op = _Op(queue, fn, list(reads), list(writes), True, "d", group)
        self._resolve(op)
        g["count"] += 1
        op.dtarget = 16 * g["count"]
        self.ops.append(op)
        return op

    def _dep(self, op, prod):
        if prod is op:
            return
        if prod.kind == "d":
            op.waits.append(("D" + prod.dsem, prod.dtarget))
        else:
            if prod.eng == op.eng and op.kind == "c" and (
                    not prod.late or prod.eng == "tensor"):
                return
            prod.tick = True
            op.waits.append(("E" + prod.eng, prod))

    def _resolve(self, op):
        for r in op.reads:
            for reg, w in self.last_w.get(r[0], ()):
                if _overlap(r, reg):
                    self._dep(op, w)
        for wreg in op.writes:
            for reg, rd in self.readers.get(wreg[0], ()):
                if rd is op:
                    continue
                if _overlap(wreg, reg):
                    if rd.kind == "d":
                        op.waits.append(("D" + rd.dsem, rd.dtarget))
                    elif (rd.eng != op.eng or rd.late or op.kind == "d") and not (
                            rd.eng == op.eng == "tensor" and op.kind == "c"):
                        rd.tick = True
                        op.waits.append(("E" + rd.eng, rd))
            for reg, w in self.last_w.get(wreg[0], ()):
                if w is not op and _overlap(wreg, reg):
                    self._dep(op, w)
        for r in op.reads:
            self.readers.setdefault(r[0], []).append((r, op))
        for wreg in op.writes:
            self.last_w.setdefault(wreg[0], []).append((wreg, op))
            lst = self.last_w[wreg[0]]
            if len(lst) > 8:
                same = [e for e in lst[:-1] if e[0] == wreg]
                if same:
                    self.last_w[wreg[0]] = [e for e in lst if e[0] != wreg] + [(wreg, op)]
            self.readers[wreg[0]] = [e for e in self.readers.get(wreg[0], ())
                                     if not (_overlap(wreg, e[0]) and
                                             e[0][1] >= wreg[1] and e[0][2] <= wreg[2])
                                     or e[1] is op]

    def emit(self, block):
        counters = {e: 0 for e in ENGINES}
        tickval = {}
        for op in self.ops:
            if op.kind == "c" and op.tick:
                counters[op.eng] += 1
                tickval[id(op)] = counters[op.eng]
        esems = {}
        for e in ENGINES:
            if counters[e] > 0:
                esems[e] = self.stack.enter_context(self.nc.semaphore("es_" + e))

        byeng = {e: [] for e in ENGINES}
        for op in self.ops:
            byeng[op.eng].append(op)

        for engname in ENGINES:
            ops = byeng[engname]
            if not ops:
                continue

            def mk(engname, ops):
                def body(e):
                    cur = {}
                    for op in ops:
                        for key, val in op.waits:
                            if key.startswith("E"):
                                v = tickval[id(val)]
                                h = esems[key[1:]]
                            else:
                                v = val
                                h = self.dgroups[key[1:]]["sem"]
                            if cur.get(key, -1) < v:
                                e.wait_ge(h, v)
                                cur[key] = v
                        ins = op.fn(e)
                        if op.kind == "d":
                            ins.then_inc(self.dgroups[op.dsem]["sem"], 16)
                        elif op.tick:
                            ins.then_inc(esems[op.eng], 1)
                return body
            getattr(block, engname)(mk(engname, ops))


def rg(t, lo, hi):
    o = getattr(t, "off", 0)
    return (sbname(t), o + lo * _bpe(t), o + hi * _bpe(t))



def sbname(t):
    return t.name if hasattr(t, "name") else str(t)


class Ctx:

    def __init__(self, p):
        self.p = p
        self.ident = p.sb("ident", [128, 128], F32)
        self.identb = p.sb("identb", [128, 128], BF16)
        self.xs = p.sb("xs", [128, TT * D], F32)
        self.stats = p.sb("stats", [128, 4 * TT], F32)
        self.lnbuf = p.sb("lnbuf", [128, D], F32)

    @property
    def x_fm(self):
        return self.p.sb("x_fm", [128, DC * T], BF16)

    def R(self, t, lo=None, hi=None):
        n = sbname(t)
        o = getattr(t, "off", 0)
        if lo is None:
            if o:
                sh = t.shape
                return (n, o, o + sh[0] * 0 + sh[1] * _bpe(t)) if False else (n, o, o + t._ap.shape[-1] * _bpe(t))
            return (n, 0, 10 ** 9)
        b = _bpe(t)
        return (n, o + lo * b, o + hi * b)


def emit_identity(p, c):
    ident = c.ident
    p.add("gpsimd", lambda e: e.memset(ident[:, :], 0.0),
          writes=[c.R(ident)])
    p.add("gpsimd", lambda e: e.affine_select(
        out=ident[:, :], in_=ident[:, :], compare_op=ALU.not_equal, fill=1.0,
        base=0, pattern=[[-1, 128]], channel_multiplier=1),
        reads=[c.R(ident)], writes=[c.R(ident)], late=True)
    identb = c.identb
    p.add("gpsimd", lambda e: e.memset(identb[:, :], 0.0), writes=[c.R(identb)])
    p.add("gpsimd", lambda e: e.affine_select(
        out=identb[:, :], in_=identb[:, :], compare_op=ALU.not_equal, fill=1.0,
        base=0, pattern=[[-1, 128]], channel_multiplier=1),
        reads=[c.R(identb)], writes=[c.R(identb)], late=True)


def emit_load_x(p, c, x_hbm):
    xs = c.xs
    p.dma("sync", lambda e: e.dma_start(
        out=xs.ap().rearrange("p (t d) -> p t d", t=TT),
        in_=x_hbm.rearrange("(t p) d -> p t d", p=128)),
        writes=[c.R(xs)], group="ldx")


def emit_ln_to_fm(p, c, pref="ln", eps=1e-5, also_tm_f32=None):
    xs, stats, lnbuf, x_fm = c.xs, c.stats, c.lnbuf, c.x_fm
    nt = TT
    emit_ln_stats_only(p, c, eps)

    for t in range(nt):
        dst = lnbuf[:, 0:D]
        dreg = c.R(lnbuf, 0, D)
        p.add("scalar", lambda e, t=t, dst=dst: e.activation(
            out=dst, in_=xs[:, t * D:(t + 1) * D], func=AF.Identity,
            scale=stats[:, t:t + 1], bias=stats[:, nt + t:nt + t + 1]),
            reads=[c.R(xs, t * D, (t + 1) * D), c.R(stats, 0, 2 * nt)],
            writes=[dreg], late=True)
        if also_tm_f32 is not None:
            p.add("scalar", lambda e, t=t: e.activation(
                out=also_tm_f32[:, t * D:(t + 1) * D], in_=xs[:, t * D:(t + 1) * D],
                func=AF.Identity, scale=stats[:, t:t + 1], bias=stats[:, nt + t:nt + t + 1]),
                reads=[c.R(xs, t * D, (t + 1) * D), c.R(stats, 0, 2 * nt)],
                writes=[c.R(also_tm_f32, t * D, (t + 1) * D)])
        emit_tm_tile_to_fm(p, c, dst, dreg, x_fm, t, pref=pref)


def emit_tm_tile_to_fm(p, c, src, sreg, dst_fm, t, pref="tr", ncols=D,
                       fm_T=T, dst_late=False):
    nchunk = ncols // 128
    for h in range(0, nchunk, 4):
        n4 = min(4, nchunk - h)
        bank = p.psum[(h // 4) % 2]

        def tp(e, h=h, n4=n4, bank=bank):
            for j in range(n4):
                ins = e.transpose(bank[:, j * 128:(j + 1) * 128],
                                  src[:, (h + j) * 128:(h + j + 1) * 128],
                                  c.ident[:, :])
            return ins
        p.add("tensor", tp, reads=[sreg, c.R(c.ident)],
              writes=[p.pr((h // 4) % 2)], late=True)
        dfm = dst_fm.ap().rearrange("p (cc t) -> p cc t", t=fm_T)[
            :, h:h + n4, t * 128:(t + 1) * 128]
        p.add("vector", lambda e, bank=bank, dfm=dfm, n4=n4: e.tensor_copy(
            dfm, bank[:, 0:n4 * 128].rearrange("p (cc t) -> p cc t", t=128)),
            reads=[p.pr((h // 4) % 2)],
            writes=[rg(dst_fm, h * fm_T, (h + n4) * fm_T)], late=dst_late)


def load_w_inout(p, c, w_hbm, name, In, Out, queue="gpsimd", group=None):
    wb = p.sb(name, [128, (In // 128) * Out], BF16)
    p.dma(queue, lambda e: e.dma_start(
        out=wb.ap().rearrange("p (cc o) -> p cc o", o=Out),
        in_=w_hbm.rearrange("(cc p) o -> p cc o", p=128)),
        writes=[c.R(wb)], group=group or name)
    return wb


def load_w_outin_T(p, c, w_hbm, name, Out, In, group=None):
    nat = p.sb(name + "_nat", [128, (Out // 128) * In], BF16)
    wb = p.sb(name, [128, (In // 128) * Out], BF16)
    g = group or name
    p.dma("gpsimd", lambda e: e.dma_start(
        out=nat.ap().rearrange("p (cc i) -> p cc i", i=In),
        in_=w_hbm.rearrange("(cc p) i -> p cc i", p=128)),
        writes=[c.R(nat)], group=g + "_nat")
    for oc in range(Out // 128):
        for ic in range(In // 128):
            p.dma("sync", lambda e, oc=oc, ic=ic: e.dma_start_transpose(
                wb[:, ic * Out + oc * 128: ic * Out + (oc + 1) * 128],
                nat[:, oc * In + ic * 128: oc * In + (ic + 1) * 128]),
                reads=[c.R(nat, oc * In + ic * 128, oc * In + (ic + 1) * 128)],
                writes=[c.R(wb, ic * Out + oc * 128, ic * Out + (oc + 1) * 128)],
                group=g)
    return wb


def emit_store_xs(p, c, y_hbm):
    xs = c.xs
    p.dma("sync", lambda e: e.dma_start(
        out=y_hbm.rearrange("(t p) d -> p t d", p=128),
        in_=xs.ap().rearrange("p (t d) -> p t d", t=TT)),
        reads=[c.R(xs)], group="sty")
    g = p.dgroup("sty")
    p.add("sync", lambda e: e.wait_ge(g["sem"], 16 * g["count"]))


def mm_accum(p, c, bank, lhsT_list, rhs_list, reads, n=512, start_clear=True):
    K = len(lhsT_list)

    def fn(e):
        for i in range(K):
            ins = e.matmul(p.psum[bank][:, 0:n], lhsT_list[i], rhs_list[i],
                           start=(start_clear and i == 0), stop=(i == K - 1))
        return ins
    p.add("tensor", fn, reads=list(reads), writes=[p.pr(bank)], late=True)



def emit_l0ffn(p, c, win, wso, w1b, w2stream, wgb, gbias):
    xs, x_fm = c.xs, c.x_fm
    hid_fm = p.sb("hid_fm", [128, 8 * T], BF16)
    hs_fm = p.sb("hs_fm", [128, 8 * 512], BF16)
    sa = p.sb("sa_silu", [128, 2 * 512], F32)
    gate = p.sb("gate", [128, TT * 16], F32)
    wdense = p.sb("wdense", [128, TT * 8], F32)

    xfm_r = [c.R(x_fm, ic * T, (ic + 1) * T) for ic in range(DC)]

    for t in range(TT):
        lhs = [x_fm[:, ic * T + t * 128: ic * T + (t + 1) * 128] for ic in range(DC)]
        rhs = [wgb[:, ic * 8:(ic + 1) * 8] for ic in range(DC)]
        bank = 5
        p.add("tensor", lambda e, lhs=lhs, rhs=rhs: [
            e.matmul(p.psum[5][:, 0:8], lhs[i], rhs[i], start=(i == 0), stop=(i == 7))
            for i in range(8)][-1],
            reads=xfm_r + [c.R(wgb)], writes=[p.pr(5)], late=True)
        g0 = gate[:, t * 16:t * 16 + 8]
        g1 = gate[:, t * 16 + 8:t * 16 + 16]
        wd = wdense[:, t * 8:(t + 1) * 8]
        g0r = c.R(gate, t * 16, t * 16 + 8)
        g1r = c.R(gate, t * 16 + 8, t * 16 + 16)
        wdr = c.R(wdense, t * 8, (t + 1) * 8)
        p.add("vector", lambda e, g0=g0: e.tensor_tensor(
            out=g0, in0=p.psum[5][:, 0:8], in1=gbias[:, :], op=ALU.add),
            reads=[p.pr(5), c.R(gbias)], writes=[g0r], late=True)
        p.add("scalar", lambda e, g0=g0, g1=g1: e.activation(
            out=g0, in_=g0, func=AF.Exp, accum_out=g1[:, 0:1]),
            reads=[g0r], writes=[g0r, g1r], late=True)
        p.add("vector", lambda e, g1=g1: e.reciprocal(out=g1[:, 1:2], in_=g1[:, 0:1]),
            reads=[g1r], writes=[g1r], late=True)
        p.add("vector", lambda e, g0=g0, g1=g1: e.tensor_scalar(
            out=g0, in0=g0, scalar1=g1[:, 1:2], scalar2=None, op0=ALU.mult),
            reads=[g0r, g1r], writes=[g0r], late=True)
        p.add("vector", lambda e, g0=g0, g1=g1: e.tensor_reduce(
            out=g1[:, 2:3], in_=g0, axis=AX.X, op=ALU.max),
            reads=[g0r], writes=[g1r], late=True)
        p.add("vector", lambda e, g0=g0, g1=g1, wd=wd: e.tensor_scalar(
            out=wd, in0=g0, scalar1=g1[:, 2:3], scalar2=None, op0=ALU.is_lt),
            reads=[g0r, g1r], writes=[wdr], late=True)
        p.add("vector", lambda e, g0=g0, wd=wd: e.tensor_tensor(
            out=wd, in0=wd, in1=g0, op=ALU.mult),
            reads=[g0r, wdr], writes=[wdr], late=True)
        p.add("vector", lambda e, g1=g1, wd=wd: e.tensor_reduce(
            out=g1[:, 3:4], in_=wd, axis=AX.X, op=ALU.max),
            reads=[wdr], writes=[g1r], late=True)
        p.add("vector", lambda e, g0=g0, g1=g1, wd=wd: e.tensor_scalar(
            out=wd, in0=g0, scalar1=g1[:, 3:4], scalar2=None, op0=ALU.is_ge),
            reads=[g0r, g1r], writes=[wdr], late=True)
        p.add("vector", lambda e, g0=g0, wd=wd: e.tensor_tensor(
            out=wd, in0=wd, in1=g0, op=ALU.mult),
            reads=[g0r, wdr], writes=[wdr], late=True)

    def swiglu_path(wsrc, out_fm, out_name, fmT):
        for tg in range(T // 512):
            for oc in range(8):
                ba, bb = (tg * 8 + oc) % 2, 2 + (tg * 8 + oc) % 2
                lr = c.R(wsrc, 0, 10 ** 9)

                def mma(e, tg=tg, oc=oc, ba=ba, wsrc=wsrc):
                    for ic in range(DC):
                        ins = e.matmul(p.psum[ba][:, :],
                                       wsrc[:, ic * 2048 + oc * 128: ic * 2048 + (oc + 1) * 128],
                                       x_fm[:, ic * T + tg * 512: ic * T + (tg + 1) * 512],
                                       start=(ic == 0), stop=(ic == DC - 1))
                    return ins

                def mmb(e, tg=tg, oc=oc, bb=bb, wsrc=wsrc):
                    for ic in range(DC):
                        ins = e.matmul(p.psum[bb][:, :],
                                       wsrc[:, ic * 2048 + (oc + 8) * 128: ic * 2048 + (oc + 9) * 128],
                                       x_fm[:, ic * T + tg * 512: ic * T + (tg + 1) * 512],
                                       start=(ic == 0), stop=(ic == DC - 1))
                    return ins
                p.add("tensor", mma, reads=xfm_r + [lr], writes=[p.pr(ba)], late=True)
                p.add("tensor", mmb, reads=xfm_r + [lr], writes=[p.pr(bb)], late=True)
                sslice = sa[:, (oc % 2) * 512:(oc % 2 + 1) * 512]
                ssreg = c.R(sa, (oc % 2) * 512, (oc % 2 + 1) * 512)
                p.add("scalar", lambda e, ba=ba, sslice=sslice: e.activation(
                    out=sslice, in_=p.psum[ba][:, :], func=AF.Silu),
                    reads=[p.pr(ba)], writes=[ssreg], late=True)
                dst = out_fm.ap().rearrange("p (cc t) -> p cc t", t=fmT)[
                    :, oc, tg * 512:(tg + 1) * 512]
                p.add("vector", lambda e, bb=bb, sslice=sslice, dst=dst: e.tensor_tensor(
                    out=dst, in0=p.psum[bb][:, :], in1=sslice, op=ALU.mult),
                    reads=[p.pr(bb), ssreg],
                    writes=[(out_name, oc * fmT + tg * 512, oc * fmT + (tg + 1) * 512)],
                    late=True)

    for tg in range(T // 512):
        for oc in range(8):
            ba, bb = oc % 2, 2 + oc % 2

            def mma(e, tg=tg, oc=oc, ba=ba):
                for ic in range(DC):
                    ins = e.matmul(p.psum[ba][:, :],
                                   win[:, ic * 2048 + oc * 128: ic * 2048 + (oc + 1) * 128],
                                   x_fm[:, ic * T + tg * 512: ic * T + (tg + 1) * 512],
                                   start=(ic == 0), stop=(ic == DC - 1))
                return ins

            def mmb(e, tg=tg, oc=oc, bb=bb):
                for ic in range(DC):
                    ins = e.matmul(p.psum[bb][:, :],
                                   win[:, ic * 2048 + (oc + 8) * 128: ic * 2048 + (oc + 9) * 128],
                                   x_fm[:, ic * T + tg * 512: ic * T + (tg + 1) * 512],
                                   start=(ic == 0), stop=(ic == DC - 1))
                return ins
            p.add("tensor", mma, reads=xfm_r + [c.R(win)], writes=[p.pr(ba)], late=True)
            p.add("tensor", mmb, reads=xfm_r + [c.R(win)], writes=[p.pr(bb)], late=True)
            sslice = sa[:, (oc % 2) * 512:(oc % 2 + 1) * 512]
            ssreg = c.R(sa, (oc % 2) * 512, (oc % 2 + 1) * 512)
            p.add("scalar", lambda e, ba=ba, sslice=sslice: e.activation(
                out=sslice, in_=p.psum[ba][:, :], func=AF.Silu),
                reads=[p.pr(ba)], writes=[ssreg], late=True)
            dst = hs_fm[:, oc * 512:(oc + 1) * 512]
            p.add("vector", lambda e, bb=bb, sslice=sslice, dst=dst: e.tensor_tensor(
                out=dst, in0=p.psum[bb][:, :], in1=sslice, op=ALU.mult),
                reads=[p.pr(bb), ssreg], writes=[c.R(hs_fm, oc * 512, (oc + 1) * 512)],
                late=True)
        for tt4 in range(4):
            t128 = tg * 4 + tt4
            for oh in range(2):
                bank = 4 + oh

                def mmo(e, tt4=tt4, oh=oh, bank=bank):
                    for hc in range(8):
                        ins = e.matmul(p.psum[bank][:, :],
                                       hs_fm[:, hc * 512 + tt4 * 128: hc * 512 + (tt4 + 1) * 128],
                                       wso[:, hc * D + oh * 512: hc * D + (oh + 1) * 512],
                                       start=(hc == 0), stop=(hc == 7))
                    return ins
                p.add("tensor", mmo, reads=[c.R(hs_fm), c.R(wso)],
                      writes=[p.pr(bank)], late=True)
                dst = xs[:, t128 * D + oh * 512: t128 * D + (oh + 1) * 512]
                xreg = c.R(xs, t128 * D + oh * 512, t128 * D + (oh + 1) * 512)
                p.add("vector", lambda e, bank=bank, dst=dst: e.tensor_tensor(
                    out=dst, in0=p.psum[bank][:, :], in1=dst, op=ALU.add),
                    reads=[p.pr(bank), xreg], writes=[xreg], late=True)

    w1b = w1b()
    swiglu_path(w1b, hid_fm, sbname(hid_fm), T)

    for ei in range(8):
        wslice, wreg = w2stream(ei)
        for t in range(TT):
            for oh in range(2):
                bank = 2 + ((t * 2 + oh) % 4)

                def mme(e, t=t, oh=oh, bank=bank, wslice=wslice):
                    for hc in range(8):
                        ins = e.matmul(p.psum[bank][:, :],
                                       hid_fm[:, hc * T + t * 128: hc * T + (t + 1) * 128],
                                       wslice(hc, oh),
                                       start=(hc == 0), stop=(hc == 7))
                    return ins
                p.add("tensor", mme, reads=[c.R(hid_fm), wreg], writes=[p.pr(bank)],
                      late=True)
                dst = xs[:, t * D + oh * 512: t * D + (oh + 1) * 512]
                xreg = c.R(xs, t * D + oh * 512, t * D + (oh + 1) * 512)
                wdcol = wdense[:, t * 8 + ei:t * 8 + ei + 1]
                p.add("vector", lambda e, bank=bank, dst=dst, wdcol=wdcol:
                      e.scalar_tensor_tensor(out=dst, in0=p.psum[bank][:, :],
                                             scalar=wdcol, in1=dst,
                                             op0=ALU.mult, op1=ALU.add),
                      reads=[p.pr(bank), xreg, c.R(wdense, t * 8 + ei, t * 8 + ei + 1)],
                      writes=[xreg], late=True)


def emit_l0ffn_full(p, c, aps, uid=""):
    wgb = load_w_inout(p, c, aps["w_gate"], "wgb" + uid, D, 8)
    gbias = p.sb("gbias" + uid, [128, 8], F32)
    p.dma("gpsimd", lambda e: e.dma_start(
        out=gbias[:, :], in_=aps["gate_bias"][None, :].broadcast_to((128, 8))),
        writes=[c.R(gbias)], group="gbias" + uid)
    wfused = load_w_inout(p, c, aps["w_shared_in"], "wfused" + uid, D, 2048,
                          group="wfused_a" + uid)
    wso = load_w_inout(p, c, aps["w_shared_out"], "wso" + uid, 1024, D)

    def reload_w1():
        p.dma("gpsimd", lambda e: e.dma_start(
            out=wfused.ap().rearrange("p (cc o) -> p cc o", o=2048),
            in_=aps["w1_shared"].rearrange("(cc p) o -> p cc o", p=128)),
            writes=[c.R(wfused)], group="wfused_b" + uid)
        return wfused

    half = 8 * D

    def w2stream(ei):
        off = (ei % 2) * half
        p.dma("gpsimd", lambda e, ei=ei, off=off: e.dma_start(
            out=wfused.ap()[:, off:off + half].rearrange("p (cc o) -> p cc o", o=D),
            in_=aps["w2"][ei].rearrange("(cc p) o -> p cc o", p=128)),
            writes=[c.R(wfused, off, off + half)], group=f"w2_{ei % 2}" + uid)
        return (lambda hc, oh, off=off: wfused[:, off + hc * D + oh * 512:
                                               off + hc * D + (oh + 1) * 512],
                c.R(wfused, off, off + half))

    emit_l0ffn(p, c, wfused, wso, reload_w1, w2stream, wgb, gbias)



def emit_ln_stats_only(p, c, eps=1e-5, sqbuf=None):
    xs, stats = c.xs, c.stats
    nt = TT
    sq = sqbuf if sqbuf is not None else p.sb("lns_sq", [128, D], F32)
    for t in range(nt):
        p.add("scalar", lambda e, t=t: e.activation(
            out=sq[:, :], in_=xs[:, t * D:(t + 1) * D], func=AF.Square,
            accum_out=stats[:, 2 * nt + t:2 * nt + t + 1]),
            reads=[c.R(xs, t * D, (t + 1) * D)],
            writes=[c.R(sq), c.R(stats, 2 * nt + t, 2 * nt + t + 1)], late=True)
        p.add("vector", lambda e, t=t: e.tensor_reduce(
            out=stats[:, 3 * nt + t:3 * nt + t + 1], in_=xs[:, t * D:(t + 1) * D],
            axis=AX.X, op=ALU.add),
            reads=[c.R(xs, t * D, (t + 1) * D)],
            writes=[c.R(stats, 3 * nt + t, 3 * nt + t + 1)], late=True)
    p.add("vector", lambda e: e.tensor_scalar(
        out=stats[:, 2 * nt:3 * nt], in0=stats[:, 2 * nt:3 * nt],
        scalar1=1.0 / D, scalar2=None, op0=ALU.mult),
        reads=[c.R(stats, 2 * nt, 3 * nt)], writes=[c.R(stats, 2 * nt, 3 * nt)], late=True)
    p.add("vector", lambda e: e.tensor_scalar(
        out=stats[:, 3 * nt:4 * nt], in0=stats[:, 3 * nt:4 * nt],
        scalar1=1.0 / D, scalar2=None, op0=ALU.mult),
        reads=[c.R(stats, 3 * nt, 4 * nt)], writes=[c.R(stats, 3 * nt, 4 * nt)], late=True)
    p.add("vector", lambda e: e.tensor_tensor(
        out=stats[:, nt:2 * nt], in0=stats[:, 3 * nt:4 * nt],
        in1=stats[:, 3 * nt:4 * nt], op=ALU.mult),
        reads=[c.R(stats, 3 * nt, 4 * nt)], writes=[c.R(stats, nt, 2 * nt)], late=True)
    p.add("vector", lambda e: e.tensor_tensor(
        out=stats[:, 2 * nt:3 * nt], in0=stats[:, 2 * nt:3 * nt],
        in1=stats[:, nt:2 * nt], op=ALU.subtract),
        reads=[c.R(stats, nt, 3 * nt)], writes=[c.R(stats, 2 * nt, 3 * nt)], late=True)
    p.add("vector", lambda e: e.tensor_scalar(
        out=stats[:, 2 * nt:3 * nt], in0=stats[:, 2 * nt:3 * nt],
        scalar1=eps, scalar2=None, op0=ALU.add),
        reads=[c.R(stats, 2 * nt, 3 * nt)], writes=[c.R(stats, 2 * nt, 3 * nt)], late=True)
    p.add("scalar", lambda e: e.activation(
        out=stats[:, 2 * nt:3 * nt], in_=stats[:, 2 * nt:3 * nt], func=AF.Ln),
        reads=[c.R(stats, 2 * nt, 3 * nt)], writes=[c.R(stats, 2 * nt, 3 * nt)], late=True)
    p.add("scalar", lambda e: e.activation(
        out=stats[:, 0:nt], in_=stats[:, 2 * nt:3 * nt], func=AF.Exp, scale=-0.5),
        reads=[c.R(stats, 2 * nt, 3 * nt)], writes=[c.R(stats, 0, nt)], late=True)
    p.add("vector", lambda e: e.tensor_tensor(
        out=stats[:, nt:2 * nt], in0=stats[:, 3 * nt:4 * nt],
        in1=stats[:, 0:nt], op=ALU.mult),
        reads=[c.R(stats, 3 * nt, 4 * nt), c.R(stats, 0, nt)],
        writes=[c.R(stats, nt, 2 * nt)], late=True)
    p.add("vector", lambda e: e.tensor_scalar(
        out=stats[:, nt:2 * nt], in0=stats[:, nt:2 * nt],
        scalar1=-1.0, scalar2=None, op0=ALU.mult),
        reads=[c.R(stats, nt, 2 * nt)], writes=[c.R(stats, nt, 2 * nt)], late=True)


def emit_l0attna(p, c, aps, uid=""):
    xs, stats, lnbuf, ident = c.xs, c.stats, c.lnbuf, c.ident
    NH, NG = 16, 4
    xr_fm = c.x_fm
    attn_fm = p.sb("attnfm" + uid, [128, DC * T], BF16)
    ropebuf = p.sb("ropebuf" + uid, [128, D], F32)
    ropetmp = p.sb("ropetmp" + uid, [128, 512], F32)
    cosb = p.sb("cosb" + uid, [128, 512], F32)
    sinb = p.sb("sinb" + uid, [128, 512], F32)
    qbuf = p.sb("qbuf" + uid, [64, 2 * T], BF16)
    kdup = p.sb("kdup" + uid, [64, 2 * T], BF16)
    vtm = p.sb("vtm" + uid, [128, TT * NG * 65], BF16)
    pst = p.sb("pst" + uid, [128, 4 * 512], BF16)
    zrow = p.sb("zrow" + uid, [1, 512], F32)
    zb = p.sb("zb" + uid, [64, 512], F32)
    onesc = p.sb("onesc" + uid, [1, 64], F32)
    maskT = p.sb("maskT" + uid, [128, 128], BF16)
    wnat = p.sb("wnat" + uid, [128, D], BF16)

    p.add("gpsimd", lambda e: e.memset(onesc[:, :], 1.0), writes=[c.R(onesc)])
    p.add("gpsimd", lambda e: e.memset(maskT[:, :], 1.0), writes=[c.R(maskT)])
    p.add("gpsimd", lambda e: e.affine_select(
        out=maskT[:, :], in_=maskT[:, :], compare_op=ALU.is_ge, fill=0.0,
        base=0, pattern=[[1, 128]], channel_multiplier=-1),
        reads=[c.R(maskT)], writes=[c.R(maskT)], late=True)
    p.add("gpsimd", lambda e: e.memset(
        vtm.ap().rearrange("p (t g o) -> p t g o", t=TT, g=NG)[:, :, :, 64:65], 1.0),
        writes=[c.R(vtm)])

    def wT(dram, name, Out):
        wb = p.sb(name, [128, DC * Out], BF16)
        for oc in range(Out // 128):
            nreg = c.R(wnat, 0, D)
            p.dma("gpsimd", lambda e, oc=oc: e.dma_start(
                out=wnat[:, 0:D], in_=dram[oc * 128:(oc + 1) * 128, :]),
                writes=[nreg], group=name + "_nat")
            for ic in range(DC):
                p.dma("sync", lambda e, oc=oc, ic=ic: e.dma_start_transpose(
                    wb[:, ic * Out + oc * 128: ic * Out + (oc + 1) * 128],
                    wnat[:, ic * 128:(ic + 1) * 128]),
                    reads=[rg(wnat, ic * 128, (ic + 1) * 128)],
                    writes=[c.R(wb, ic * Out + oc * 128, ic * Out + (oc + 1) * 128)],
                    group=name)
        return wb

    wqT = wT(aps["W_q"], "wqT" + uid, D)
    wkT = wT(aps["W_k"], "wkT" + uid, 256)
    wvT = wT(aps["W_v"], "wvT" + uid, 256)

    for t in range(TT):
        dst = lnbuf[:, 0:D]
        dreg = c.R(lnbuf, 0, D)
        p.add("scalar", lambda e, t=t, dst=dst: e.activation(
            out=dst, in_=xs[:, t * D:(t + 1) * D], func=AF.Identity,
            scale=stats[:, t:t + 1], bias=stats[:, TT + t:TT + t + 1]),
            reads=[c.R(xs, t * D, (t + 1) * D), c.R(stats, 0, 2 * TT)],
            writes=[dreg], late=True)
        cs = cosb[:, :]
        sn = sinb[:, :]
        csr = c.R(cosb, 0, 512)
        snr = c.R(sinb, 0, 512)
        p.dma2("sync", lambda e, t=t, cs=cs: e.dma_start(
            out=cs, in_=aps["rope_cos"][t * 128:(t + 1) * 128, :]),
            writes=[csr], group="ropec" + uid)
        p.dma2("sync", lambda e, t=t, sn=sn: e.dma_start(
            out=sn, in_=aps["rope_sin"][t * 128:(t + 1) * 128, :]),
            writes=[snr], group="ropes" + uid)
        lneo = dst.rearrange("p (k two) -> p two k", two=2)
        xe, xo = lneo[:, 0, :], lneo[:, 1, :]
        rb = ropebuf
        rreg = c.R(ropebuf, 0, D)
        p.add("vector", lambda e, xe=xe, cs=cs: e.tensor_tensor(
            out=ropebuf[:, 0:512], in0=xe, in1=cs, op=ALU.mult),
            reads=[dreg, csr], writes=[c.R(ropebuf, 0, 512)])
        p.add("vector", lambda e, xo=xo, sn=sn: e.tensor_tensor(
            out=ropetmp[:, 0:512], in0=xo, in1=sn, op=ALU.mult),
            reads=[dreg, snr], writes=[c.R(ropetmp, 0, 512)])
        p.add("vector", lambda e: e.tensor_tensor(
            out=ropebuf[:, 0:512], in0=ropebuf[:, 0:512], in1=ropetmp[:, 0:512],
            op=ALU.subtract),
            reads=[c.R(ropebuf, 0, 512), c.R(ropetmp, 0, 512)],
            writes=[c.R(ropebuf, 0, 512)])
        p.add("vector", lambda e, xo=xo, cs=cs: e.tensor_tensor(
            out=ropebuf[:, 512:1024], in0=xo, in1=cs, op=ALU.mult),
            reads=[dreg, csr], writes=[c.R(ropebuf, 512, 1024)])
        p.add("vector", lambda e, xe=xe, sn=sn: e.tensor_tensor(
            out=ropetmp[:, 0:512], in0=xe, in1=sn, op=ALU.mult),
            reads=[dreg, snr], writes=[c.R(ropetmp, 0, 512)])
        p.add("vector", lambda e: e.tensor_tensor(
            out=ropebuf[:, 512:1024], in0=ropebuf[:, 512:1024], in1=ropetmp[:, 0:512],
            op=ALU.add),
            reads=[c.R(ropebuf, 512, 1024), c.R(ropetmp, 0, 512)],
            writes=[c.R(ropebuf, 512, 1024)])
        emit_tm_tile_to_fm(p, c, ropebuf, rreg, xr_fm, t, pref="ra" + uid)

    xr_r = [c.R(xr_fm, ic * T, (ic + 1) * T) for ic in range(DC)]

    for t in range(TT):
        bank = t % 2

        def mmv(e, t=t, bank=bank):
            for ic in range(DC):
                ins = e.matmul(p.psum[bank][:, 0:256],
                               xr_fm[:, ic * T + t * 128: ic * T + (t + 1) * 128],
                               wvT[:, ic * 256:(ic + 1) * 256],
                               start=(ic == 0), stop=(ic == DC - 1))
            return ins
        p.add("tensor", mmv, reads=xr_r + [c.R(wvT)], writes=[p.pr(bank)], late=True)
        dst = vtm.ap().rearrange("p (t g o) -> p t g o", t=TT, g=NG)[:, t, :, 0:64]
        p.add("vector", lambda e, bank=bank, dst=dst: e.tensor_copy(
            dst, p.psum[bank].ap()[:, 0:256].rearrange("p (g o) -> p g o", g=NG)),
            reads=[p.pr(bank)],
            writes=[c.R(vtm, t * NG * 65, (t + 1) * NG * 65)], late=True)

    for h in range(NH):
        g = h // 4
        hp = h // 2
        if h % 8 == 0:
            oc = g // 2
            for tg in range(4):
                bank = tg % 2

                def mmk(e, oc=oc, tg=tg, bank=bank):
                    for ic in range(DC):
                        ins = e.matmul(p.psum[bank][:, :],
                                       wkT[:, ic * 256 + oc * 128: ic * 256 + (oc + 1) * 128],
                                       xr_fm[:, ic * T + tg * 512: ic * T + (tg + 1) * 512],
                                       start=(ic == 0), stop=(ic == DC - 1))
                    return ins
                p.add("tensor", mmk, reads=xr_r + [c.R(wkT)], writes=[p.pr(bank)], late=True)
                for half in range(2):
                    p.add("vector", lambda e, half=half, tg=tg, bank=bank: e.tensor_copy(
                        kdup.ap()[:, half * T + tg * 512: half * T + (tg + 1) * 512],
                        p.psum[bank].ap()[half * 64:(half + 1) * 64, :]),
                        reads=[p.pr(bank)],
                        writes=[c.R(kdup, half * T + tg * 512, half * T + (tg + 1) * 512)],
                        late=True)
        if h % 2 == 0:
            for tg in range(4):
                bank = tg % 2

                def mmq(e, hp=hp, tg=tg, bank=bank):
                    for ic in range(DC):
                        ins = e.matmul(p.psum[bank][:, :],
                                       wqT[:, ic * D + hp * 128: ic * D + (hp + 1) * 128],
                                       xr_fm[:, ic * T + tg * 512: ic * T + (tg + 1) * 512],
                                       start=(ic == 0), stop=(ic == DC - 1))
                    return ins
                p.add("tensor", mmq, reads=xr_r + [c.R(wqT)], writes=[p.pr(bank)],
                      late=True)
                for half in range(2):
                    p.add("vector", lambda e, half=half, tg=tg, bank=bank: e.tensor_copy(
                        qbuf.ap()[:, half * T + tg * 512: half * T + (tg + 1) * 512],
                        p.psum[bank].ap()[half * 64:(half + 1) * 64, :]),
                        reads=[p.pr(bank)],
                        writes=[c.R(qbuf, half * T + tg * 512, half * T + (tg + 1) * 512)],
                        late=True)
        qoff = (h % 2) * T
        qreg = c.R(qbuf, qoff, qoff + T)
        for kt in range(TT):
            ktc = kt // 4
            for tqc in range(ktc, 4):
                sbank = (kt + tqc) % 2
                pslot = (kt + tqc) % 4
                preg = c.R(pst, pslot * 512, (pslot + 1) * 512)

                def mms(e, kt=kt, tqc=tqc, sbank=sbank, g=g, qoff=qoff):
                    koff = (g % 2) * T
                    return e.matmul(
                        p.psum[sbank][:, :],
                        kdup.ap()[:, koff + kt * 128: koff + (kt + 1) * 128],
                        qbuf.ap()[:, qoff + tqc * 512: qoff + (tqc + 1) * 512],
                        start=True, stop=True)
                p.add("tensor", mms, reads=[c.R(kdup, (g % 2) * T, (g % 2 + 1) * T), qreg],
                      writes=[p.pr(sbank)], late=True)
                p.add("scalar", lambda e, sbank=sbank, pslot=pslot: e.activation(
                    out=pst[:, pslot * 512:(pslot + 1) * 512], in_=p.psum[sbank][:, :],
                    func=AF.Exp, scale=0.125),
                    reads=[p.pr(sbank)], writes=[preg], late=True)
                if tqc == ktc:
                    off = kt * 128 - tqc * 512
                    p.add("vector", lambda e, pslot=pslot, off=off: e.tensor_tensor(
                        out=pst[:, pslot * 512 + off: pslot * 512 + off + 128],
                        in0=pst[:, pslot * 512 + off: pslot * 512 + off + 128],
                        in1=maskT[:, :], op=ALU.mult),
                        reads=[preg, c.R(maskT)], writes=[preg], late=True)
                    if off > 0:
                        p.add("vector", lambda e, pslot=pslot, off=off: e.memset(
                            pst[:, pslot * 512: pslot * 512 + off], 0.0),
                            writes=[preg], late=True)

                def mmpv(e, kt=kt, tqc=tqc, pslot=pslot, g=g):
                    return e.matmul(
                        p.psum[2 + tqc].ap()[0:65, :],
                        vtm[:, kt * NG * 65 + g * 65: kt * NG * 65 + (g + 1) * 65],
                        pst[:, pslot * 512:(pslot + 1) * 512],
                        start=(kt == 0), stop=(kt == tqc * 4 + 3))
                p.add("tensor", mmpv, reads=[c.R(vtm), preg], writes=[p.pr(2 + tqc)],
                      late=True)
        zreg = c.R(zrow, 0, 512)
        half = h % 2
        for tqc in range(4):
            p.add("scalar", lambda e, tqc=tqc: e.activation(
                out=zrow[:, 0:512],
                in_=p.psum[2 + tqc].ap()[64:65, :], func=AF.Identity),
                reads=[p.pr(2 + tqc)], writes=[zreg], late=True)
            p.add("vector", lambda e: e.reciprocal(out=zrow[:, 0:512], in_=zrow[:, 0:512]),
                  reads=[zreg], writes=[zreg], late=True)
            def bc(e, tqc=tqc):
                return e.matmul(p.psum[tqc % 2].ap()[0:64, :], onesc[:, :],
                                zrow[:, 0:512],
                                start=True, stop=True)
            p.add("tensor", bc, reads=[c.R(onesc), zreg], writes=[p.pr(tqc % 2)],
                  late=True)
            p.add("scalar", lambda e, tqc=tqc: e.activation(
                out=zb[:, :], in_=p.psum[tqc % 2].ap()[0:64, :], func=AF.Identity),
                reads=[p.pr(tqc % 2)], writes=[c.R(zb)], late=True)
            adst = attn_fm.ap()[half * 64:(half + 1) * 64,
                                hp * T + tqc * 512: hp * T + (tqc + 1) * 512]
            p.add("vector", lambda e, tqc=tqc, adst=adst: e.tensor_tensor(
                out=adst, in0=p.psum[2 + tqc].ap()[0:64, :], in1=zb[:, :], op=ALU.mult),
                reads=[p.pr(2 + tqc), c.R(zb)],
                writes=[c.R(attn_fm, hp * T + tqc * 512, hp * T + (tqc + 1) * 512)],
                late=True)

    woT = wT(aps["W_o"], "wqT" + uid, D)
    afm_r = [c.R(attn_fm, ic * T, (ic + 1) * T) for ic in range(DC)]
    for t in range(TT):
        for oh in range(2):
            bank = (t * 2 + oh) % 2

            def mmo(e, t=t, oh=oh, bank=bank):
                for ic in range(DC):
                    ins = e.matmul(p.psum[bank][:, :],
                                   attn_fm[:, ic * T + t * 128: ic * T + (t + 1) * 128],
                                   woT[:, ic * D + oh * 512: ic * D + (oh + 1) * 512],
                                   start=(ic == 0), stop=(ic == DC - 1))
                return ins
            p.add("tensor", mmo, reads=afm_r + [c.R(woT)], writes=[p.pr(bank)],
                  late=True)
            dst = xs[:, t * D + oh * 512: t * D + (oh + 1) * 512]
            xreg = c.R(xs, t * D + oh * 512, t * D + (oh + 1) * 512)
            p.add("vector", lambda e, bank=bank, dst=dst: e.tensor_tensor(
                out=dst, in0=p.psum[bank][:, :], in1=dst, op=ALU.add),
                reads=[p.pr(bank), xreg], writes=[xreg], late=True)


def emit_l1attna(p, c, aps, uid=""):
    xs, stats, lnbuf, ident = c.xs, c.stats, c.lnbuf, c.ident
    NH = 16
    x_fm = c.x_fm
    attnfm = p.sb("l1a_attnfm" + uid, [128, DC * T], BF16)
    wpair = p.sb("l1a_wpair" + uid, [128, 5 * DC * 128], BF16)
    q1b = p.sb("l1a_q1" + uid, [64, 2 * T], BF16)
    k1b = p.sb("l1a_k1" + uid, [64, 2 * T], BF16)
    q2b = p.sb("l1a_q2" + uid, [64, 2 * T], BF16)
    k2b = p.sb("l1a_k2" + uid, [64, 2 * T], BF16)
    vtm = p.sb("l1a_vtm" + uid, [128, TT * 65], BF16)
    pst = p.sb("l1a_pst" + uid, [128, 4 * 512], BF16)
    zrow = p.sb("l1a_zrow" + uid, [1, 512], F32)
    zb1 = p.sb("l1a_zb1" + uid, [64, 512], F32)
    zb2 = p.sb("l1a_zb2" + uid, [64, 512], F32)
    ctmp = p.sb("l1a_ctmp" + uid, [64, 512], F32)
    cbuf = p.sb("l1a_cbuf" + uid, [64, 512], F32)
    onesc = p.sb("l1a_ones" + uid, [1, 64], F32)
    gamb = p.sb("l1a_gamb" + uid, [128, D], BF16)
    betb = p.sb("l1a_betb" + uid, [128, D], BF16)
    lamc = p.sb("l1a_lam" + uid, [128, 2], F32)
    atile = p.sb("l1a_atile" + uid, [128, D], BF16)
    astat = p.sb("l1a_astat" + uid, [128, 8], F32)
    aln = p.sb("l1a_aln" + uid, [128, D], F32)

    p.add("gpsimd", lambda e: e.memset(onesc[:, :], 1.0), writes=[c.R(onesc)])
    p.add("gpsimd", lambda e: e.memset(
        vtm.ap().rearrange("p (t o) -> p t o", t=TT)[:, :, 64:65], 1.0),
        writes=[c.R(vtm)])
    p.dma("gpsimd", lambda e: e.dma_start(
        out=gamb[:, :], in_=aps["ln_gamma"][None, :].broadcast_to((128, D))),
        writes=[c.R(gamb)], group="l1a_gb" + uid)
    p.dma("gpsimd", lambda e: e.dma_start(
        out=betb[:, :], in_=aps["ln_beta"][None, :].broadcast_to((128, D))),
        writes=[c.R(betb)], group="l1a_gb" + uid)
    p.dma("gpsimd", lambda e: e.dma_start(
        out=lamc[:, 0:1], in_=aps["lamb"][None, :].broadcast_to((128, 1))),
        writes=[c.R(lamc, 0, 1)], group="l1a_gb" + uid)
    p.add("vector", lambda e: e.tensor_scalar(
        out=lamc[:, 1:2], in0=lamc[:, 0:1], scalar1=-1.0, scalar2=1.0,
        op0=ALU.mult, op1=ALU.add),
        reads=[c.R(lamc, 0, 1)], writes=[c.R(lamc, 1, 2)], late=True)
    p.add("vector", lambda e: e.tensor_scalar(
        out=gamb[:, :], in0=gamb[:, :], scalar1=lamc[:, 1:2], scalar2=None, op0=ALU.mult),
        reads=[c.R(gamb), c.R(lamc, 1, 2)], writes=[c.R(gamb)], late=True)
    p.add("vector", lambda e: e.tensor_scalar(
        out=betb[:, :], in0=betb[:, :], scalar1=lamc[:, 1:2], scalar2=None, op0=ALU.mult),
        reads=[c.R(betb), c.R(lamc, 1, 2)], writes=[c.R(betb)], late=True)

    xr_r = [c.R(x_fm, ic * T, (ic + 1) * T) for ic in range(DC)]
    WNAMES = ["wq1", "wk1", "wq2", "wk2", "wv"]
    wreg = c.R(wpair, 0, 5 * DC * 128)

    def wsl(wi, ic):
        return wpair[:, (wi * DC + ic) * 128:(wi * DC + ic + 1) * 128]

    for h in range(NH):
        hp, half = h // 2, h % 2
        if half == 0:
            for wi, wn in enumerate(WNAMES):
                for hh in range(2):
                    p.dma("gpsimd", lambda e, wi=wi, wn=wn, hp=hp, hh=hh: e.dma_start(
                        out=wpair.ap()[:, wi * DC * 128:(wi + 1) * DC * 128].rearrange(
                            "p (cc hd) -> p cc hd", hd=128)[:, :, hh * 64:(hh + 1) * 64],
                        in_=aps[wn][2 * hp + hh].rearrange("(cc p) d -> p cc d", p=128)),
                        writes=[rg(wpair, wi * DC * 128, (wi + 1) * DC * 128)],
                        group="l1a_wp" + uid + str(hp % 2))
            for wi, buf in enumerate([q1b, k1b, q2b, k2b]):
                for tg in range(4):
                    bank = tg % 2

                    def mmp(e, wi=wi, tg=tg, bank=bank):
                        for ic in range(DC):
                            ins = e.matmul(p.psum[bank][:, :], wsl(wi, ic),
                                           x_fm[:, ic * T + tg * 512: ic * T + (tg + 1) * 512],
                                           start=(ic == 0), stop=(ic == DC - 1))
                        return ins
                    p.add("tensor", mmp, reads=xr_r + [wreg], writes=[p.pr(bank)],
                          late=True)
                    for hh in range(2):
                        p.add("vector", lambda e, buf=buf, hh=hh, tg=tg, bank=bank:
                              e.tensor_copy(
                                  buf.ap()[:, hh * T + tg * 512: hh * T + (tg + 1) * 512],
                                  p.psum[bank].ap()[hh * 64:(hh + 1) * 64, :]),
                              reads=[p.pr(bank)],
                              writes=[rg(buf, hh * T + tg * 512, hh * T + (tg + 1) * 512)],
                              late=True)
        for t in range(TT):
            bank = t % 2

            def mmv(e, t=t, bank=bank, half=half):
                for ic in range(DC):
                    ins = e.matmul(p.psum[bank][:, 0:64],
                                   x_fm[:, ic * T + t * 128: ic * T + (t + 1) * 128],
                                   wsl(4, ic)[:, half * 64:(half + 1) * 64],
                                   start=(ic == 0), stop=(ic == DC - 1))
                return ins
            p.add("tensor", mmv, reads=xr_r + [wreg], writes=[p.pr(bank)], late=True)
            p.add("vector", lambda e, t=t, bank=bank: e.tensor_copy(
                vtm[:, t * 65:t * 65 + 64], p.psum[bank].ap()[:, 0:64]),
                reads=[p.pr(bank)], writes=[c.R(vtm, t * 65, t * 65 + 64)], late=True)
        qoff = half * T
        for tqc in range(4):
            pv1, pv2 = 2 + tqc % 2, 4 + tqc % 2
            for kt in range(TT):
                for att, (qq, kk, pvb) in enumerate([(q1b, k1b, pv1), (q2b, k2b, pv2)]):
                    sbank = (kt + att) % 2
                    pslot = (2 * kt + att) % 4
                    preg = c.R(pst, pslot * 512, (pslot + 1) * 512)

                    def mms(e, kt=kt, tqc=tqc, sbank=sbank, qq=qq, kk=kk, qoff=qoff):
                        return e.matmul(
                            p.psum[sbank][:, :],
                            kk.ap()[:, qoff + kt * 128: qoff + (kt + 1) * 128],
                            qq.ap()[:, qoff + tqc * 512: qoff + (tqc + 1) * 512],
                            start=True, stop=True)
                    p.add("tensor", mms,
                          reads=[rg(qq, qoff, qoff + T), rg(kk, qoff, qoff + T)],
                          writes=[p.pr(sbank)], late=True)
                    p.add("scalar", lambda e, sbank=sbank, pslot=pslot: e.activation(
                        out=pst[:, pslot * 512:(pslot + 1) * 512],
                        in_=p.psum[sbank][:, :], func=AF.Exp, scale=0.125),
                        reads=[p.pr(sbank)], writes=[preg], late=True)

                    def mmpv(e, kt=kt, pvb=pvb, pslot=pslot):
                        return e.matmul(
                            p.psum[pvb].ap()[0:65, :],
                            vtm[:, kt * 65:(kt + 1) * 65],
                            pst[:, pslot * 512:(pslot + 1) * 512],
                            start=(kt == 0), stop=(kt == TT - 1))
                    p.add("tensor", mmpv, reads=[c.R(vtm), preg], writes=[p.pr(pvb)],
                          late=True)
            zreg = c.R(zrow, 0, 512)
            for att, (pvb, zbx) in enumerate([(pv1, zb1), (pv2, zb2)]):
                p.add("scalar", lambda e, pvb=pvb: e.activation(
                    out=zrow[:, :], in_=p.psum[pvb].ap()[64:65, :], func=AF.Identity),
                    reads=[p.pr(pvb)], writes=[zreg], late=True)
                p.add("vector", lambda e: e.reciprocal(out=zrow[:, :], in_=zrow[:, :]),
                      reads=[zreg], writes=[zreg], late=True)
                if att == 1:
                    p.add("vector", lambda e: e.tensor_scalar(
                        out=zrow[:, :], in0=zrow[:, :], scalar1=lamc.ap()[0:1, 0:1],
                        scalar2=None, op0=ALU.mult),
                        reads=[zreg, c.R(lamc, 0, 1)], writes=[zreg], late=True)

                def bcz(e, att=att):
                    return e.matmul(p.psum[att].ap()[0:64, :], onesc[:, :],
                                    zrow[:, :], start=True, stop=True)
                p.add("tensor", bcz, reads=[c.R(onesc), zreg], writes=[p.pr(att)],
                      late=True)
                p.add("scalar", lambda e, att=att, zbx=zbx: e.activation(
                    out=zbx[:, :], in_=p.psum[att].ap()[0:64, :], func=AF.Identity),
                    reads=[p.pr(att)], writes=[c.R(zbx)], late=True)
            p.add("vector", lambda e, pv2=pv2: e.tensor_tensor(
                out=ctmp[:, :], in0=p.psum[pv2].ap()[0:64, :], in1=zb2[:, :], op=ALU.mult),
                reads=[p.pr(pv2), c.R(zb2)], writes=[c.R(ctmp)], late=True)
            p.add("vector", lambda e, pv1=pv1: e.tensor_tensor(
                out=cbuf[:, :], in0=p.psum[pv1].ap()[0:64, :], in1=zb1[:, :], op=ALU.mult),
                reads=[p.pr(pv1), c.R(zb1)], writes=[c.R(cbuf)], late=True)
            p.add("vector", lambda e: e.tensor_tensor(
                out=cbuf[:, :], in0=cbuf[:, :], in1=ctmp[:, :], op=ALU.subtract),
                reads=[c.R(cbuf), c.R(ctmp)], writes=[c.R(cbuf)], late=True)
            adst = attnfm.ap()[half * 64:(half + 1) * 64,
                               hp * T + tqc * 512: hp * T + (tqc + 1) * 512]
            areg = c.R(attnfm, hp * T + tqc * 512, hp * T + (tqc + 1) * 512)
            p.add("vector", lambda e, adst=adst: e.tensor_copy(adst, cbuf[:, :]),
                  reads=[c.R(cbuf)], writes=[areg], late=True)

    afm_r = [c.R(attnfm, ic * T, (ic + 1) * T) for ic in range(DC)]
    for t in range(TT):
        bank = 6 + t % 2

        def tpa(e, t=t, bank=bank):
            for ic in range(DC):
                ins = e.transpose(p.psum[bank].ap()[:, ic * 128:(ic + 1) * 128],
                                  attnfm[:, ic * T + t * 128: ic * T + (t + 1) * 128],
                                  c.identb[:, :])
            return ins
        p.add("tensor", tpa, reads=afm_r + [c.R(c.identb)], writes=[p.pr(bank)], late=True)
        p.add("vector", lambda e, bank=bank: e.tensor_copy(
            atile[:, :], p.psum[bank][:, :].rearrange("p (a b) -> p (a b)", a=1) if False
            else p.psum[bank].ap()[:, 0:1024]),
            reads=[p.pr(bank)], writes=[c.R(atile)], late=True)
        p.add("scalar", lambda e: e.activation(
            out=aln[:, :], in_=atile[:, :], func=AF.Square, accum_out=astat[:, 2:3]),
            reads=[c.R(atile)], writes=[c.R(aln), c.R(astat, 2, 3)], late=True)
        p.add("vector", lambda e: e.tensor_reduce(
            out=astat[:, 3:4], in_=atile[:, :], axis=AX.X, op=ALU.add),
            reads=[c.R(atile)], writes=[c.R(astat, 3, 4)], late=True)
        p.add("vector", lambda e: e.tensor_scalar(
            out=astat[:, 0:1], in0=astat[:, 3:4], scalar1=1.0 / D, scalar2=None,
            op0=ALU.mult),
            reads=[c.R(astat, 3, 4)], writes=[c.R(astat, 0, 1)], late=True)
        p.add("vector", lambda e: e.tensor_tensor(
            out=astat[:, 4:5], in0=astat[:, 0:1], in1=astat[:, 0:1], op=ALU.mult),
            reads=[c.R(astat, 0, 1)], writes=[c.R(astat, 4, 5)], late=True)
        p.add("vector", lambda e: e.tensor_scalar(
            out=astat[:, 2:3], in0=astat[:, 2:3], scalar1=1.0 / D, scalar2=None,
            op0=ALU.mult),
            reads=[c.R(astat, 2, 3)], writes=[c.R(astat, 2, 3)], late=True)
        p.add("vector", lambda e: e.tensor_tensor(
            out=astat[:, 2:3], in0=astat[:, 2:3], in1=astat[:, 4:5], op=ALU.subtract),
            reads=[c.R(astat, 2, 5)], writes=[c.R(astat, 2, 3)], late=True)
        p.add("vector", lambda e: e.tensor_scalar(
            out=astat[:, 2:3], in0=astat[:, 2:3], scalar1=1e-5, scalar2=None, op0=ALU.add),
            reads=[c.R(astat, 2, 3)], writes=[c.R(astat, 2, 3)], late=True)
        p.add("scalar", lambda e: e.activation(
            out=astat[:, 2:3], in_=astat[:, 2:3], func=AF.Ln),
            reads=[c.R(astat, 2, 3)], writes=[c.R(astat, 2, 3)], late=True)
        p.add("scalar", lambda e: e.activation(
            out=astat[:, 5:6], in_=astat[:, 2:3], func=AF.Exp, scale=-0.5),
            reads=[c.R(astat, 2, 3)], writes=[c.R(astat, 5, 6)], late=True)
        p.add("vector", lambda e: e.tensor_tensor(
            out=astat[:, 6:7], in0=astat[:, 0:1], in1=astat[:, 5:6], op=ALU.mult),
            reads=[c.R(astat, 0, 1), c.R(astat, 5, 6)], writes=[c.R(astat, 6, 7)],
            late=True)
        p.add("vector", lambda e: e.tensor_scalar(
            out=astat[:, 6:7], in0=astat[:, 6:7], scalar1=-1.0, scalar2=None, op0=ALU.mult),
            reads=[c.R(astat, 6, 7)], writes=[c.R(astat, 6, 7)], late=True)
        p.add("scalar", lambda e: e.activation(
            out=aln[:, :], in_=atile[:, :], func=AF.Identity,
            scale=astat[:, 5:6], bias=astat[:, 6:7]),
            reads=[c.R(atile), c.R(astat, 5, 7)], writes=[c.R(aln)], late=True)
        p.add("vector", lambda e: e.tensor_tensor(
            out=aln[:, :], in0=aln[:, :], in1=gamb[:, :], op=ALU.mult),
            reads=[c.R(aln), c.R(gamb)], writes=[c.R(aln)])
        p.add("vector", lambda e: e.tensor_tensor(
            out=aln[:, :], in0=aln[:, :], in1=betb[:, :], op=ALU.add),
            reads=[c.R(aln), c.R(betb)], writes=[c.R(aln)])
        xreg = c.R(xs, t * D, (t + 1) * D)
        p.add("vector", lambda e, t=t: e.tensor_tensor(
            out=xs[:, t * D:(t + 1) * D], in0=xs[:, t * D:(t + 1) * D],
            in1=aln[:, :], op=ALU.add),
            reads=[xreg, c.R(aln)], writes=[xreg], late=True)


def build_L1_attn_a():
    nc = bass.Bass("TRN2", target_bir_lowering=False)
    x = nc.dram_tensor("x", [T, D], F32, kind="ExternalInput").ap()
    aps = {
        "wq1": nc.dram_tensor("wq1", [16, D, 64], F32, kind="ExternalInput").ap(),
        "wk1": nc.dram_tensor("wk1", [16, D, 64], F32, kind="ExternalInput").ap(),
        "wq2": nc.dram_tensor("wq2", [16, D, 64], F32, kind="ExternalInput").ap(),
        "wk2": nc.dram_tensor("wk2", [16, D, 64], F32, kind="ExternalInput").ap(),
        "wv": nc.dram_tensor("wv", [16, D, 64], F32, kind="ExternalInput").ap(),
        "ln_gamma": nc.dram_tensor("ln_gamma", [D], F32, kind="ExternalInput").ap(),
        "ln_beta": nc.dram_tensor("ln_beta", [D], F32, kind="ExternalInput").ap(),
        "lamb": nc.dram_tensor("lamb", [1], F32, kind="ExternalInput").ap(),
    }
    y = nc.dram_tensor("y", [T, D], F32, kind="ExternalOutput").ap()
    with ExitStack() as stack:
        p = Prog(nc, stack)
        c = Ctx(p)
        emit_identity(p, c)
        emit_load_x(p, c, x)
        emit_ln_to_fm(p, c)
        emit_l1attna(p, c, aps)
        emit_store_xs(p, c, y)
        with nc.Block() as block:
            p.emit(block)
    return nc



def emit_l1ffn(p, c, aps, uid=""):
    xs, stats, lnbuf, ident = c.xs, c.stats, c.lnbuf, c.ident
    H, G, HD, E = 16, 16, 64, 8
    h2fm = p.sb("l1f_h2fm" + uid, [128, DC * T], BF16)
    gateW = p.sb("l1f_gate" + uid, [128, TT * 16], F32)
    htile = p.sb("l1f_t2" + uid, [128, D], F32)
    hfmt = p.sb("l1f_hfmt" + uid, [128, D], BF16)
    qtile = p.sb("l1f_qtile" + uid, [128, D], BF16)
    kvtile = p.sb("l1f_kvt" + uid, [128, 2048], BF16)
    attb = p.sb("l1f_att" + uid, [128, 2 * 256], F32)
    otile = p.sb("l1f_qtile" + uid, [128, D], BF16)
    otileT = p.sb("l1f_hfmt" + uid, [128, D], BF16)
    xmid = p.sb("l1f_xmid" + uid, [128, D], F32)
    tst = p.sb("l1f_tst" + uid, [128, 16], F32)
    t2 = p.sb("l1f_t2" + uid, [128, D], F32)

    wqb = load_w_inout(p, c, aps["Wq"], "l1f_wqb" + uid, D, D)
    wdb = load_w_inout(p, c, aps["Wdown"], "l1f_wdb" + uid, D, 64)
    wob = load_w_inout(p, c, aps["Wo"], "l1f_wob" + uid, D, D)
    wupb = p.sb("l1f_wupb" + uid, [64, 2048], BF16)
    p.dma("gpsimd", lambda e: e.dma_start(out=wupb[:, :], in_=aps["Wup"][:, :]),
          writes=[c.R(wupb)], group="l1f_wup" + uid)
    wgb = load_w_inout(p, c, aps["Wg"], "l1f_wgb" + uid, D, 8)

    def bco(dram, name, n, dt=F32):
        buf = p.sb(name, [128, n], dt)
        p.dma("gpsimd", lambda e: e.dma_start(
            out=buf[:, :], in_=dram[None, :].broadcast_to((128, n))),
            writes=[c.R(buf)], group=name)
        return buf
    gam1b = bco(aps["gamma1"], "l1f_g1" + uid, D, BF16)
    gam2b = bco(aps["gamma2"], "l1f_g2" + uid, D, BF16)
    bqb = bco(aps["bq"], "l1f_bq" + uid, D, BF16)
    bupb = bco(aps["bup"], "l1f_bup" + uid, 2048, BF16)
    bob = bco(aps["bo"], "l1f_bo" + uid, D, BF16)
    bgb = bco(aps["bg"], "l1f_bg" + uid, 8)
    bdc = p.sb("l1f_bd" + uid, [64, 1], F32)
    p.dma("gpsimd", lambda e: e.dma_start(out=bdc[:, :], in_=aps["bdown"][:, None]),
          writes=[c.R(bdc)], group="l1f_bd" + uid)

    for t in range(TT):
        xreg = c.R(xs, t * D, (t + 1) * D)
        ureg = c.R(lnbuf, 0, D)
        p.add("scalar", lambda e, t=t: e.activation(
            out=lnbuf[:, :], in_=xs[:, t * D:(t + 1) * D], func=AF.Identity,
            scale=stats[:, t:t + 1], bias=stats[:, TT + t:TT + t + 1]),
            reads=[xreg, c.R(stats, 0, 2 * TT)], writes=[ureg], late=True)
        p.add("scalar", lambda e: e.activation(
            out=htile[:, :], in_=lnbuf[:, :], func=AF.Square, accum_out=tst[:, 0:1]),
            reads=[ureg], writes=[c.R(htile), c.R(tst, 0, 1)], late=True)
        p.add("vector", lambda e: e.tensor_scalar(
            out=tst[:, 0:1], in0=tst[:, 0:1], scalar1=1.0 / D, scalar2=1e-6,
            op0=ALU.mult, op1=ALU.add),
            reads=[c.R(tst, 0, 1)], writes=[c.R(tst, 0, 1)], late=True)
        p.add("scalar", lambda e: e.activation(
            out=tst[:, 0:1], in_=tst[:, 0:1], func=AF.Ln),
            reads=[c.R(tst, 0, 1)], writes=[c.R(tst, 0, 1)], late=True)
        p.add("scalar", lambda e: e.activation(
            out=tst[:, 1:2], in_=tst[:, 0:1], func=AF.Exp, scale=-0.5),
            reads=[c.R(tst, 0, 1)], writes=[c.R(tst, 1, 2)], late=True)
        p.add("scalar", lambda e: e.activation(
            out=htile[:, :], in_=lnbuf[:, :], func=AF.Identity, scale=tst[:, 1:2]),
            reads=[ureg, c.R(tst, 1, 2)], writes=[c.R(htile)], late=True)
        p.add("vector", lambda e: e.tensor_tensor(
            out=htile[:, :], in0=htile[:, :], in1=gam1b[:, :], op=ALU.mult),
            reads=[c.R(htile), c.R(gam1b)], writes=[c.R(htile)])
        for hh in range(0, DC, 4):
            bank = (hh // 4) % 2

            def tph(e, hh=hh, bank=bank):
                for j in range(4):
                    ins = e.transpose(p.psum[bank].ap()[:, j * 128:(j + 1) * 128],
                                      htile[:, (hh + j) * 128:(hh + j + 1) * 128],
                                      ident[:, :])
                return ins
            p.add("tensor", tph, reads=[c.R(htile), c.R(ident)], writes=[p.pr(bank)],
                  late=True)
            p.add("vector", lambda e, hh=hh, bank=bank: e.tensor_copy(
                hfmt[:, hh * 128:(hh + 4) * 128], p.psum[bank].ap()[:, 0:512]),
                reads=[p.pr(bank)], writes=[c.R(hfmt, hh * 128, (hh + 4) * 128)],
                late=True)
        hreg = c.R(hfmt, 0, D)
        for oh in range(2):
            def mmq(e, oh=oh):
                for ic in range(DC):
                    ins = e.matmul(p.psum[2 + oh][:, :],
                                   hfmt[:, ic * 128:(ic + 1) * 128],
                                   wqb[:, ic * D + oh * 512: ic * D + (oh + 1) * 512],
                                   start=(ic == 0), stop=(ic == DC - 1))
                return ins
            p.add("tensor", mmq, reads=[hreg, c.R(wqb)], writes=[p.pr(2 + oh)], late=True)
            p.add("vector", lambda e, oh=oh: e.tensor_tensor(
                out=qtile[:, oh * 512:(oh + 1) * 512], in0=p.psum[2 + oh][:, :],
                in1=bqb[:, oh * 512:(oh + 1) * 512], op=ALU.add),
                reads=[p.pr(2 + oh), c.R(bqb)],
                writes=[c.R(qtile, oh * 512, (oh + 1) * 512)], late=True)
        def mmd(e):
            for ic in range(DC):
                ins = e.matmul(p.psum[4].ap()[0:64, 0:128],
                               wdb[:, ic * 64:(ic + 1) * 64],
                               hfmt[:, ic * 128:(ic + 1) * 128],
                               start=(ic == 0), stop=(ic == DC - 1))
            return ins
        p.add("tensor", mmd, reads=[hreg, c.R(wdb)], writes=[p.pr(4)], late=True)
        d1reg = c.R(htile, 0, 128)
        p.add("scalar", lambda e: e.activation(
            out=otileT.ap()[0:64, 0:128], in_=p.psum[4].ap()[0:64, 0:128], func=AF.Identity,
            bias=bdc[:, 0:1]),
            reads=[p.pr(4), c.R(bdc)], writes=[c.R(otileT, 0, 128)], late=True)
        for oh in range(4):
            def mmkv(e, oh=oh):
                return e.matmul(p.psum[2 + oh % 2][:, :], otileT.ap()[0:64, 0:128],
                                wupb[:, oh * 512:(oh + 1) * 512], start=True, stop=True)
            p.add("tensor", mmkv, reads=[c.R(otileT, 0, 128), c.R(wupb)],
                  writes=[p.pr(2 + oh % 2)], late=True)
            p.add("vector", lambda e, oh=oh: e.tensor_tensor(
                out=kvtile[:, oh * 512:(oh + 1) * 512], in0=p.psum[2 + oh % 2][:, :],
                in1=bupb[:, oh * 512:(oh + 1) * 512], op=ALU.add),
                reads=[p.pr(2 + oh % 2), c.R(bupb)],
                writes=[c.R(kvtile, oh * 512, (oh + 1) * 512)], late=True)
        kreg = c.R(kvtile, 0, 2048)
        kview = kvtile[:, :].rearrange("p (g kv) -> p g kv", g=G)[:, :, 0:64]
        vview = kvtile[:, :].rearrange("p (g kv) -> p g kv", g=G)[:, :, 64:128]
        for h in range(H):
            qv = qtile[:, h * 64:(h + 1) * 64].rearrange(
                "p (a d) -> p a d", a=1).broadcast_to((128, G, 64))
            p.add("vector", lambda e, qv=qv: e.tensor_tensor(
                out=t2[:, :].rearrange("p (g d) -> p g d", d=64),
                in0=kview, in1=qv, op=ALU.mult),
                reads=[kreg, c.R(qtile, h * 64, (h + 1) * 64)],
                writes=[c.R(t2)], late=True)
            p.add("vector", lambda e, h=h: e.tensor_reduce(
                out=attb[:, h * 16:(h + 1) * 16],
                in_=t2[:, :].rearrange("p (g d) -> p g d", d=64),
                axis=AX.X, op=ALU.add),
                reads=[c.R(t2)], writes=[c.R(attb, h * 16, (h + 1) * 16)], late=True)
        areg = c.R(attb, 0, 256)
        p.add("scalar", lambda e: e.activation(
            out=attb[:, 0:256], in_=attb[:, 0:256], func=AF.Exp, scale=0.125),
            reads=[areg], writes=[areg], late=True)
        p.add("vector", lambda e: e.tensor_reduce(
            out=tst[:, 0:16], in_=attb[:, 0:256].rearrange("p (h g) -> p h g", g=16),
            axis=AX.X, op=ALU.add),
            reads=[areg], writes=[c.R(tst, 0, 16)], late=True)
        p.add("vector", lambda e: e.reciprocal(out=tst[:, 0:16], in_=tst[:, 0:16]),
              reads=[c.R(tst, 0, 16)], writes=[c.R(tst, 0, 16)], late=True)
        p.add("vector", lambda e: e.tensor_tensor(
            out=attb[:, 256:512].rearrange("p (h g) -> p h g", g=16),
            in0=attb[:, 0:256].rearrange("p (h g) -> p h g", g=16),
            in1=tst[:, 0:16].rearrange("p (h a) -> p h a", a=1).broadcast_to((128, 16, 16)),
            op=ALU.mult),
            reads=[areg, c.R(tst, 0, 16)], writes=[c.R(attb, 256, 512)], late=True)
        for h in range(H):
            av = attb[:, 256 + h * 16:256 + (h + 1) * 16].rearrange(
                "p (g a) -> p g a", a=1).broadcast_to((128, G, 64))
            p.add("vector", lambda e, av=av: e.tensor_tensor(
                out=t2[:, :].rearrange("p (d g) -> p g d", g=16),
                in0=vview, in1=av, op=ALU.mult),
                reads=[kreg, c.R(attb, 256 + h * 16, 256 + (h + 1) * 16)],
                writes=[c.R(t2)], late=True)
            p.add("vector", lambda e, h=h: e.tensor_reduce(
                out=xmid[:, h * 64:(h + 1) * 64],
                in_=t2[:, :].rearrange("p (d g) -> p d g", g=16),
                axis=AX.X, op=ALU.add),
                reads=[c.R(t2)], writes=[c.R(xmid, h * 64, (h + 1) * 64)], late=True)
        p.add("vector", lambda e: e.tensor_copy(otile[:, :], xmid[:, :]),
              reads=[c.R(xmid, 0, D)], writes=[c.R(otile, 0, D)], late=True)
        for hh in range(0, DC, 4):
            bank = 6 + (hh // 4) % 2

            def tpo(e, hh=hh, bank=bank):
                for j in range(4):
                    ins = e.transpose(p.psum[bank].ap()[:, (hh + j) % 4 * 128:((hh + j) % 4 + 1) * 128],
                                      otile[:, (hh + j) * 128:(hh + j + 1) * 128],
                                      c.identb[:, :])
                return ins
            p.add("tensor", tpo, reads=[c.R(otile), c.R(c.identb)], writes=[p.pr(bank)],
                  late=True)
            p.add("vector", lambda e, hh=hh, bank=bank: e.tensor_copy(
                otileT[:, hh * 128:(hh + 4) * 128], p.psum[bank].ap()[:, 0:512]),
                reads=[p.pr(bank)], writes=[c.R(otileT, hh * 128, (hh + 4) * 128)],
                late=True)
        for oh in range(2):
            def mmo(e, oh=oh):
                for ic in range(DC):
                    ins = e.matmul(p.psum[2 + oh][:, :],
                                   otileT[:, ic * 128:(ic + 1) * 128],
                                   wob[:, ic * D + oh * 512: ic * D + (oh + 1) * 512],
                                   start=(ic == 0), stop=(ic == DC - 1))
                return ins
            p.add("tensor", mmo, reads=[c.R(otileT, 0, D), c.R(wob)],
                  writes=[p.pr(2 + oh)], late=True)
            p.add("vector", lambda e, oh=oh: e.tensor_tensor(
                out=t2[:, oh * 512:(oh + 1) * 512], in0=p.psum[2 + oh][:, :],
                in1=bob[:, oh * 512:(oh + 1) * 512], op=ALU.add),
                reads=[p.pr(2 + oh), c.R(bob)], writes=[c.R(t2, oh * 512, (oh + 1) * 512)],
                late=True)
        p.add("vector", lambda e: e.tensor_tensor(
            out=xmid[:, :], in0=t2[:, :], in1=lnbuf[:, :], op=ALU.add),
            reads=[c.R(t2, 0, D), ureg], writes=[c.R(xmid)], late=True)
        p.add("gpsimd", lambda e, t=t: e.tensor_tensor(
            out=xs[:, t * D:(t + 1) * D], in0=xs[:, t * D:(t + 1) * D],
            in1=xmid[:, :], op=ALU.add),
            reads=[xreg, c.R(xmid)], writes=[xreg], late=True)
        p.add("scalar", lambda e: e.activation(
            out=t2[:, :], in_=xmid[:, :], func=AF.Square, accum_out=tst[:, 2:3]),
            reads=[c.R(xmid)], writes=[c.R(t2, 0, D), c.R(tst, 2, 3)], late=True)
        p.add("vector", lambda e: e.tensor_scalar(
            out=tst[:, 2:3], in0=tst[:, 2:3], scalar1=1.0 / D, scalar2=1e-6,
            op0=ALU.mult, op1=ALU.add),
            reads=[c.R(tst, 2, 3)], writes=[c.R(tst, 2, 3)], late=True)
        p.add("scalar", lambda e: e.activation(
            out=tst[:, 2:3], in_=tst[:, 2:3], func=AF.Ln),
            reads=[c.R(tst, 2, 3)], writes=[c.R(tst, 2, 3)], late=True)
        p.add("scalar", lambda e: e.activation(
            out=tst[:, 3:4], in_=tst[:, 2:3], func=AF.Exp, scale=-0.5),
            reads=[c.R(tst, 2, 3)], writes=[c.R(tst, 3, 4)], late=True)
        p.add("scalar", lambda e: e.activation(
            out=xmid[:, :], in_=xmid[:, :], func=AF.Identity, scale=tst[:, 3:4]),
            reads=[c.R(xmid), c.R(tst, 3, 4)], writes=[c.R(xmid)], late=True)
        p.add("vector", lambda e: e.tensor_tensor(
            out=xmid[:, :], in0=xmid[:, :], in1=gam2b[:, :], op=ALU.mult),
            reads=[c.R(xmid), c.R(gam2b)], writes=[c.R(xmid)])
        for hh in range(0, DC, 4):
            bank = (hh // 4) % 2

            def tph2(e, hh=hh, bank=bank):
                for j in range(4):
                    ins = e.transpose(p.psum[bank].ap()[:, j * 128:(j + 1) * 128],
                                      xmid[:, (hh + j) * 128:(hh + j + 1) * 128],
                                      ident[:, :])
                return ins
            p.add("tensor", tph2, reads=[c.R(xmid), c.R(ident)], writes=[p.pr(bank)],
                  late=True)
            dfm = h2fm.ap().rearrange("p (cc t2_) -> p cc t2_", t2_=T)[
                :, hh:hh + 4, t * 128:(t + 1) * 128]
            p.add("vector", lambda e, bank=bank, dfm=dfm, hh=hh, t=t: e.tensor_copy(
                dfm, p.psum[bank].ap()[:, 0:512].rearrange("p (cc q) -> p cc q", q=128)),
                reads=[p.pr(bank)],
                writes=[rg(h2fm, hh * T + t * 128, (hh + 4) * T)], late=True)
        def mmg(e, t=t):
            for ic in range(DC):
                ins = e.matmul(p.psum[5][:, 0:8],
                               h2fm[:, ic * T + t * 128: ic * T + (t + 1) * 128],
                               wgb[:, ic * 8:(ic + 1) * 8],
                               start=(ic == 0), stop=(ic == DC - 1))
            return ins
        p.add("tensor", mmg, reads=[c.R(h2fm), c.R(wgb)], writes=[p.pr(5)], late=True)
        greg = c.R(gateW, t * 16, (t + 1) * 16)
        p.add("vector", lambda e, t=t: e.tensor_tensor(
            out=gateW[:, t * 16:t * 16 + 8], in0=p.psum[5][:, 0:8], in1=bgb[:, :],
            op=ALU.add),
            reads=[p.pr(5), c.R(bgb)], writes=[greg], late=True)
        p.add("scalar", lambda e, t=t: e.activation(
            out=gateW[:, t * 16:t * 16 + 8], in_=gateW[:, t * 16:t * 16 + 8],
            func=AF.Exp, accum_out=gateW[:, t * 16 + 8:t * 16 + 9]),
            reads=[greg], writes=[greg], late=True)
        p.add("vector", lambda e, t=t: e.reciprocal(
            out=gateW[:, t * 16 + 9:t * 16 + 10], in_=gateW[:, t * 16 + 8:t * 16 + 9]),
            reads=[greg], writes=[greg], late=True)
        p.add("vector", lambda e, t=t: e.tensor_scalar(
            out=gateW[:, t * 16:t * 16 + 8], in0=gateW[:, t * 16:t * 16 + 8],
            scalar1=gateW[:, t * 16 + 9:t * 16 + 10], scalar2=None, op0=ALU.mult),
            reads=[greg], writes=[greg], late=True)

    web = [p.sb("l1f_wqb" + uid, [128, 8 * D], BF16), p.sb("l1f_wob" + uid, [128, 8 * D], BF16)]
    h2r = [c.R(h2fm, ic * T, (ic + 1) * T) for ic in range(DC)]
    for ei in range(E):
        p.dma("gpsimd", lambda e, ei=ei: e.dma_start(
            out=web[ei % 2].ap().rearrange("p (cc o) -> p cc o", o=D),
            in_=aps["We"][ei].rearrange("(cc p) o -> p cc o", p=128)),
            writes=[c.R(web[ei % 2])], group=f"l1f_we{ei % 2}" + uid)
        for t in range(TT):
            for oh in range(2):
                bank = 2 + ((t * 2 + oh) % 4)

                def mme(e, t=t, oh=oh, bank=bank, ei=ei):
                    for ic in range(DC):
                        ins = e.matmul(p.psum[bank][:, :],
                                       h2fm[:, ic * T + t * 128: ic * T + (t + 1) * 128],
                                       web[ei % 2][:, ic * D + oh * 512: ic * D + (oh + 1) * 512],
                                       start=(ic == 0), stop=(ic == DC - 1))
                    return ins
                p.add("tensor", mme, reads=h2r + [c.R(web[ei % 2])], writes=[p.pr(bank)],
                      late=True)
                dst = xs[:, t * D + oh * 512: t * D + (oh + 1) * 512]
                xreg = c.R(xs, t * D + oh * 512, t * D + (oh + 1) * 512)
                p.add("vector", lambda e, bank=bank, dst=dst, t=t, ei=ei:
                      e.scalar_tensor_tensor(
                          out=dst, in0=p.psum[bank][:, :],
                          scalar=gateW[:, t * 16 + ei:t * 16 + ei + 1], in1=dst,
                          op0=ALU.mult, op1=ALU.add),
                      reads=[p.pr(bank), xreg, c.R(gateW, t * 16 + ei, t * 16 + ei + 1)],
                      writes=[xreg], late=True)


def build_L1_ffn():
    nc = bass.Bass("TRN2", target_bir_lowering=False)
    x = nc.dram_tensor("x", [T, D], F32, kind="ExternalInput").ap()
    aps = {
        "gamma1": nc.dram_tensor("gamma1", [D], F32, kind="ExternalInput").ap(),
        "Wq": nc.dram_tensor("Wq", [D, D], F32, kind="ExternalInput").ap(),
        "bq": nc.dram_tensor("bq", [D], F32, kind="ExternalInput").ap(),
        "Wdown": nc.dram_tensor("Wdown", [D, 64], F32, kind="ExternalInput").ap(),
        "bdown": nc.dram_tensor("bdown", [64], F32, kind="ExternalInput").ap(),
        "Wup": nc.dram_tensor("Wup", [64, 2048], F32, kind="ExternalInput").ap(),
        "bup": nc.dram_tensor("bup", [2048], F32, kind="ExternalInput").ap(),
        "Wo": nc.dram_tensor("Wo", [D, D], F32, kind="ExternalInput").ap(),
        "bo": nc.dram_tensor("bo", [D], F32, kind="ExternalInput").ap(),
        "gamma2": nc.dram_tensor("gamma2", [D], F32, kind="ExternalInput").ap(),
        "Wg": nc.dram_tensor("Wg", [D, 8], F32, kind="ExternalInput").ap(),
        "bg": nc.dram_tensor("bg", [8], F32, kind="ExternalInput").ap(),
        "We": nc.dram_tensor("We", [8, D, D], F32, kind="ExternalInput").ap(),
    }
    y = nc.dram_tensor("y", [T, D], F32, kind="ExternalOutput").ap()
    with ExitStack() as stack:
        p = Prog(nc, stack)
        c = Ctx(p)
        emit_identity(p, c)
        emit_load_x(p, c, x)
        emit_ln_stats_only(p, c)
        emit_l1ffn(p, c, aps)
        emit_store_xs(p, c, y)
        with nc.Block() as block:
            p.emit(block)
    return nc



def emit_l1attnb(p, c, aps, uid=""):
    xs, stats, lnbuf, ident, identb = c.xs, c.stats, c.lnbuf, c.ident, c.identb
    nc = p.nc
    x_fm = c.x_fm
    qd = nc.dram_tensor("l1b_qd" + uid, [T, D], BF16, kind="Internal").ap()
    kd = nc.dram_tensor("l1b_kd" + uid, [T, D], BF16, kind="Internal").ap()
    vd = nc.dram_tensor("l1b_vd" + uid, [T, D], BF16, kind="Internal").ap()
    ld_ = nc.dram_tensor("l1b_ld" + uid, [T, D], F32, kind="Internal").ap()

    wbuf = p.sb("l1b_wbuf" + uid, [128, DC * D], BF16)
    scrA = p.sb("l1b_scrA" + uid, [128, 8192], BF16)
    ptile = scrA.ap()[:, 0:4096].bitcast(F32)
    btile = scrA.ap()[:, 4096:8192].bitcast(F32)
    _SA = sbname(scrA)
    _SO = getattr(scrA, "off", 0)

    def PTr(lo, hi):
        return (_SA, _SO + 4 * lo, _SO + 4 * hi)

    def BTr(lo, hi):
        return (_SA, _SO + 8192 + 4 * lo, _SO + 8192 + 4 * hi)
    otile = p.sb("l1b_otile" + uid, [128, D], BF16)
    biasb = p.sb("l1b_biasb" + uid, [128, D], F32)

    qt = p.sb("l1b_qt" + uid, [128, 2 * D], BF16)
    kt = p.sb("l1b_kt" + uid, [128, 2 * D], BF16)
    vt = p.sb("l1b_vt" + uid, [128, 2 * D], BF16)
    lt = p.sb("l1b_lt" + uid, [128, 2 * D], F32)
    bp = p.sb("l1b_bp" + uid, [128, D], F32)
    ehq = p.sb("l1b_ehq" + uid, [128, D], F32)
    qhat = p.sb("l1b_qhat" + uid, [128, D], BF16)
    khat = p.sb("l1b_khat" + uid, [128, D], BF16)
    khat1 = p.sb("l1b_khat1" + uid, [64, D], BF16)
    v1 = p.sb("l1b_v1" + uid, [64, D], BF16)
    qhf = p.sb("l1b_qhf" + uid, [128, D], BF16)
    khf = p.sb("l1b_khf" + uid, [128, D], BF16)
    qtl = p.sb("l1b_qtl" + uid, [128, D], BF16)
    att = p.sb("l1b_att" + uid, [128, 128], BF16)
    stot = p.sb("l1b_stot" + uid, [128, 8 * 4], F32)
    fcol = p.sb("l1b_fcol" + uid, [128, 8 * 4], F32)
    S = c.x_fm.ap().bitcast(F32)
    Sb = wbuf

    def SRr(lo, hi):
        return ("x_fm", 4 * lo, 4 * hi)
    stmp = p.sb("l1b_stmp" + uid, [128, 512], F32)
    u64 = p.sb("l1b_u64" + uid, [128, 128], F32)
    onesA = p.sb("l1b_onesA" + uid, [128, 2], F32)
    maskL = p.sb("l1b_maskL" + uid, [128, 128], F32)
    obuf = p.sb("l1b_obuf" + uid, [128, D], BF16)
    ofm = p.sb("l1b_ofm" + uid, [128, D], BF16)
    wob = scrA
    bob = p.sb("l1b_bob" + uid, [128, D], F32)
    p.dma("gpsimd", lambda e: e.dma_start(
        out=bob[:, :], in_=aps["bo"][None, :].broadcast_to((128, D))),
        writes=[c.R(bob)], group="l1b_bob" + uid)

    p.add("gpsimd", lambda e: e.memset(u64[:, :], 1.0), writes=[c.R(u64)])
    p.add("gpsimd", lambda e: e.affine_select(
        out=u64[:, :], in_=u64[:, :], compare_op=ALU.is_ge, fill=0.0,
        base=0, pattern=[[1, 128]], channel_multiplier=-1),
        reads=[c.R(u64)], writes=[c.R(u64)], late=True)
    p.add("gpsimd", lambda e: e.memset(u64.ap()[0:64, 64:128], 0.0),
          writes=[c.R(u64)], late=True)
    p.add("gpsimd", lambda e: e.memset(onesA[:, :], 0.0), writes=[c.R(onesA)])
    p.add("gpsimd", lambda e: e.memset(onesA.ap()[0:64, 0:1], 1.0), writes=[c.R(onesA)])
    p.add("gpsimd", lambda e: e.memset(onesA.ap()[64:128, 1:2], 1.0), writes=[c.R(onesA)])
    p.add("gpsimd", lambda e: e.memset(maskL[:, :], 1.0), writes=[c.R(maskL)])
    p.add("gpsimd", lambda e: e.affine_select(
        out=maskL[:, :], in_=maskL[:, :], compare_op=ALU.is_ge, fill=0.0,
        base=0, pattern=[[1, 128]], channel_multiplier=-1),
        reads=[c.R(maskL)], writes=[c.R(maskL)], late=True)

    xr_r = [c.R(x_fm, ic * T, (ic + 1) * T) for ic in range(DC)]

    PROJ = [("Wq", "bq", qd, "q"), ("Wk", "bk", kd, "k"),
            ("Wv", "bv", vd, "v"), ("Wg", "bg", ld_, "g")]
    for wn, bn, dest, kindp in PROJ:
        p.dma("gpsimd", lambda e, wn=wn: e.dma_start(
            out=wbuf.ap().rearrange("p (cc o) -> p cc o", o=D),
            in_=aps[wn].rearrange("(cc p) o -> p cc o", p=128)),
            writes=[c.R(wbuf)], group="l1b_w" + wn + uid)
        p.dma("gpsimd", lambda e, bn=bn: e.dma_start(
            out=biasb[:, :], in_=aps[bn][None, :].broadcast_to((128, D))),
            writes=[c.R(biasb)], group="l1b_w" + wn + uid)
        for t in range(TT):
            par = t % 2
            for oh in range(2):
                bank = oh

                def mmp(e, t=t, oh=oh, bank=bank):
                    for ic in range(DC):
                        ins = e.matmul(p.psum[bank][:, :],
                                       x_fm[:, ic * T + t * 128: ic * T + (t + 1) * 128],
                                       wbuf[:, ic * D + oh * 512: ic * D + (oh + 1) * 512],
                                       start=(ic == 0), stop=(ic == DC - 1))
                    return ins
                p.add("tensor", mmp, reads=xr_r + [c.R(wbuf)], writes=[p.pr(bank)],
                      late=True)
                seg = (par * D + oh * 512, par * D + (oh + 1) * 512)
                p.add("vector", lambda e, bank=bank, seg=seg: e.tensor_tensor(
                    out=ptile[:, seg[0]:seg[1]], in0=p.psum[bank][:, :],
                    in1=biasb[:, (seg[0] % D):(seg[0] % D) + 512], op=ALU.add),
                    reads=[p.pr(bank), c.R(biasb)],
                    writes=[PTr(seg[0], seg[1])], late=True)
            pseg = (par * D, (par + 1) * D)
            psrc = ptile[:, pseg[0]:pseg[1]]
            preg = PTr(pseg[0], pseg[1])
            bdst = btile[:, pseg[0]:pseg[1]]
            breg = BTr(pseg[0], pseg[1])
            if kindp == "q" or kindp == "v":
                p.add("scalar", lambda e, psrc=psrc, bdst=bdst: e.activation(
                    out=bdst, in_=psrc, func=AF.Identity),
                    reads=[preg], writes=[breg], late=True)
            elif kindp == "k":
                p.add("scalar", lambda e, psrc=psrc, bdst=bdst: e.activation(
                    out=bdst, in_=psrc, func=AF.Sigmoid),
                    reads=[preg], writes=[breg], late=True)
            else:
                p.add("scalar", lambda e, psrc=psrc, bdst=bdst: e.activation(
                    out=bdst, in_=psrc, func=AF.Abs),
                    reads=[preg], writes=[breg], late=True)
                p.add("scalar", lambda e, bdst=bdst: e.activation(
                    out=bdst, in_=bdst, func=AF.Exp, scale=-1.0),
                    reads=[breg], writes=[breg], late=True)
                p.add("scalar", lambda e, bdst=bdst: e.activation(
                    out=bdst, in_=bdst, func=AF.Ln, bias=1.0),
                    reads=[breg], writes=[breg], late=True)
                p.add("scalar", lambda e, psrc=psrc, bdst=bdst: e.activation(
                    out=ptile[:, 0:D] if False else psrc, in_=psrc, func=AF.Relu, scale=-1.0),
                    reads=[preg], writes=[preg], late=True)
                p.add("vector", lambda e, psrc=psrc, bdst=bdst: e.tensor_tensor(
                    out=bdst, in0=bdst, in1=psrc, op=ALU.add),
                    reads=[breg, preg], writes=[breg], late=True)
            if kindp == "g":
                p.dma("sync", lambda e, t=t, bdst=bdst: e.dma_start(
                    out=dest[t * 128:(t + 1) * 128, :], in_=bdst),
                    reads=[breg], writes=[("@dram" + kindp + uid, t, t + 1)],
                    group="l1b_sp" + kindp + uid)
            else:
                oreg = c.R(otile, 0, D)
                p.add("vector", lambda e, bdst=bdst: e.tensor_copy(otile[:, :], bdst),
                      reads=[breg], writes=[oreg], late=True)
                p.dma("sync", lambda e, t=t, dest=dest: e.dma_start(
                    out=dest[t * 128:(t + 1) * 128, :], in_=otile[:, :]),
                    reads=[oreg], writes=[("@dram" + kindp + uid, t, t + 1)],
                    group="l1b_sp" + kindp + uid)

    p.add("vector", lambda e: e.memset(S[:, :], 0.0), writes=[SRr(0, 8 * D)])
    p.add("vector", lambda e: e.memset(Sb[:, :], 0.0), writes=[c.R(wbuf)])
    p.dma("gpsimd", lambda e: e.dma_start(
        out=wob.ap().rearrange("p (cc o) -> p cc o", o=D),
        in_=aps["Wo"].rearrange("(cc p) o -> p cc o", p=128)),
        writes=[(_SA, _SO, _SO + 16384)], group="l1b_wo" + uid)

    dbg = aps.get("_dbg")

    for ch in range(TT):
        par = ch % 2
        for buf, dram, grp in [(qt, qd, "q"), (kt, kd, "k"), (vt, vd, "v"), (lt, ld_, "g")]:
            p.dma("sync", lambda e, buf=buf, dram=dram, ch=ch, par=par: e.dma_start(
                out=buf[:, par * D:(par + 1) * D], in_=dram[ch * 128:(ch + 1) * 128, :]),
                reads=[("@dram" + grp + uid, ch, ch + 1),
                       ("@dram" + grp + uid, TT - 1, TT)],
                writes=[rg(buf, par * D, (par + 1) * D)],
                group="l1b_ld" + grp + uid + str(par))
        qs = qt[:, par * D:(par + 1) * D]
        ks = kt[:, par * D:(par + 1) * D]
        vs = vt[:, par * D:(par + 1) * D]
        ls = lt[:, par * D:(par + 1) * D]
        qreg = rg(qt, par * D, (par + 1) * D)
        kreg = rg(kt, par * D, (par + 1) * D)
        vreg = rg(vt, par * D, (par + 1) * D)
        lreg = rg(lt, par * D, (par + 1) * D)
        for oh in range(2):
            def mmc(e, oh=oh, ls=ls):
                return e.matmul(p.psum[2 + oh][:, :], u64[:, :],
                                ls[:, oh * 512:(oh + 1) * 512], start=True, stop=True)
            p.add("tensor", mmc, reads=[c.R(u64), lreg], writes=[p.pr(2 + oh)], late=True)
            p.add("scalar", lambda e, oh=oh: e.activation(
                out=bp[:, oh * 512:(oh + 1) * 512], in_=p.psum[2 + oh][:, :],
                func=AF.Identity),
                reads=[p.pr(2 + oh)], writes=[c.R(bp, oh * 512, (oh + 1) * 512)],
                late=True)
        for nch in range(8):
            def mtot(e, nch=nch, ls=ls):
                e.matmul(p.psum[4][:, 2 * nch:2 * nch + 1], ls[:, nch * 128:(nch + 1) * 128],
                         onesA[:, 0:1], start=True, stop=True)
                return e.matmul(p.psum[4][:, 2 * nch + 1:2 * nch + 2],
                                ls[:, nch * 128:(nch + 1) * 128],
                                onesA[:, 1:2], start=False, stop=True)
            p.add("tensor", mtot, reads=[lreg, c.R(onesA)], writes=[p.pr(4)], late=True)
        p.add("scalar", lambda e: e.activation(
            out=stot[:, 0:16], in_=p.psum[4][:, 0:16], func=AF.Identity),
            reads=[p.pr(4)], writes=[c.R(stot, 0, 16)], late=True)
        p.add("vector", lambda e: e.tensor_tensor(
            out=stot[:, 16:24].rearrange("p (a b) -> p a b", b=1),
            in0=stot[:, 0:16].rearrange("p (a b) -> p a b", b=2)[:, :, 0:1],
            in1=stot[:, 0:16].rearrange("p (a b) -> p a b", b=2)[:, :, 1:2], op=ALU.add),
            reads=[c.R(stot, 0, 16)], writes=[c.R(stot, 16, 24)], late=True)
        p.add("scalar", lambda e: e.activation(
            out=fcol[:, 0:16], in_=stot[:, 0:16], func=AF.Exp, scale=-1.0),
            reads=[c.R(stot, 0, 16)], writes=[c.R(fcol, 0, 16)], late=True)
        p.add("scalar", lambda e: e.activation(
            out=fcol[:, 16:24], in_=stot[:, 16:24], func=AF.Exp, scale=-1.0),
            reads=[c.R(stot, 16, 24)], writes=[c.R(fcol, 16, 24)], late=True)
        p.add("scalar", lambda e: e.activation(
            out=ehq[:, :], in_=bp[:, :], func=AF.Exp, scale=-1.0),
            reads=[c.R(bp, 0, D)], writes=[c.R(ehq)], late=True)
        p.add("vector", lambda e, qs=qs: e.tensor_tensor(
            out=qhat[:, :], in0=qs, in1=ehq[:, :], op=ALU.mult),
            reads=[qreg, c.R(ehq)], writes=[c.R(qhat)], late=True)
        p.add("scalar", lambda e: e.activation(
            out=ehq[:, :], in_=bp[:, :], func=AF.Exp),
            reads=[c.R(bp, 0, D)], writes=[c.R(ehq)], late=True)
        p.add("vector", lambda e, ks=ks: e.tensor_tensor(
            out=khat[:, :], in0=ks, in1=ehq[:, :], op=ALU.mult),
            reads=[kreg, c.R(ehq)], writes=[c.R(khat)], late=True)
        p.add("vector", lambda e: e.tensor_copy(khat1[:, :], khat.ap()[64:128, :]),
              reads=[c.R(khat)], writes=[c.R(khat1)], late=True)
        p.add("vector", lambda e, vs=vs: e.tensor_copy(v1[:, :], vs.ap()[64:128, :] if False else vs[64:128, :]),
              reads=[vreg], writes=[c.R(v1)], late=True)
        for src, dst, dreg in [(qhat, qhf, c.R(qhf)), (khat, khf, c.R(khf))]:
            for hb in range(0, 8, 4):
                bank = 6 + (hb // 4)

                def tpq(e, src=src, hb=hb, bank=bank):
                    for j in range(4):
                        ins = e.transpose(p.psum[bank].ap()[:, j * 128:(j + 1) * 128],
                                          src[:, (hb + j) * 128:(hb + j + 1) * 128],
                                          identb[:, :])
                    return ins
                p.add("tensor", tpq, reads=[c.R(src), c.R(identb)], writes=[p.pr(bank)],
                      late=True)
                p.add("vector", lambda e, dst=dst, hb=hb, bank=bank: e.tensor_copy(
                    dst[:, hb * 128:(hb + 4) * 128], p.psum[bank].ap()[:, 0:512]),
                    reads=[p.pr(bank)], writes=[dreg], late=True)
        for ncb in range(8):
            p.add("vector", lambda e, ncb=ncb: e.tensor_copy(
                qtl[:, ncb * 128: ncb * 128 + 64], qhf[:, ncb * 128: ncb * 128 + 64]),
                reads=[c.R(qhf)], writes=[c.R(qtl)], late=True)
            p.add("vector", lambda e, ncb=ncb: e.tensor_scalar(
                out=qtl[:, ncb * 128 + 64: ncb * 128 + 128],
                in0=qhf[:, ncb * 128 + 64: ncb * 128 + 128],
                scalar1=fcol[:, 2 * ncb:2 * ncb + 1], scalar2=None, op0=ALU.mult),
                reads=[c.R(qhf), c.R(fcol, 0, 16)], writes=[c.R(qtl)], late=True)
        def matA(e):
            for ncb in range(8):
                e.matmul(p.psum[2].ap()[0:64, 0:64], khf[:, ncb * 128:ncb * 128 + 64],
                         qhf[:, ncb * 128:ncb * 128 + 64],
                         start=(ncb == 0), stop=(ncb == 7))
            for ncb in range(8):
                e.matmul(p.psum[2].ap()[0:64, 64:128], khf[:, ncb * 128:ncb * 128 + 64],
                         qtl[:, ncb * 128 + 64:ncb * 128 + 128],
                         start=(ncb == 0), stop=(ncb == 7))
            for ncb in range(8):
                e.matmul(p.psum[3].ap()[0:64, 0:64], khf[:, ncb * 128 + 64:ncb * 128 + 128],
                         qhf[:, ncb * 128:ncb * 128 + 64],
                         start=(ncb == 0), stop=(ncb == 7))
            for ncb in range(8):
                ins = e.matmul(p.psum[3].ap()[0:64, 64:128],
                               khf[:, ncb * 128 + 64:ncb * 128 + 128],
                               qhf[:, ncb * 128 + 64:ncb * 128 + 128],
                               start=(ncb == 0), stop=(ncb == 7))
            return ins
        p.add("tensor", matA, reads=[c.R(khf), c.R(qhf), c.R(qtl)],
              writes=[p.pr(2), p.pr(3)], late=True)
        p.add("vector", lambda e: e.tensor_tensor(
            out=att.ap()[0:64, :], in0=p.psum[2].ap()[0:64, 0:128],
            in1=maskL.ap()[0:64, :], op=ALU.mult),
            reads=[p.pr(2), c.R(maskL)], writes=[c.R(att)], late=True)
        p.add("vector", lambda e: e.tensor_tensor(
            out=att.ap()[64:128, :], in0=p.psum[3].ap()[0:64, 0:128],
            in1=maskL.ap()[64:128, :], op=ALU.mult),
            reads=[p.pr(3), c.R(maskL)], writes=[c.R(att)], late=True)
        for oh in range(2):
            def mo(e, oh=oh, vs=vs):
                for ncb in range(8):
                    e.matmul(p.psum[oh][:, :], qtl[:, ncb * 128:(ncb + 1) * 128],
                             Sb[:, ncb * D + oh * 512: ncb * D + (oh + 1) * 512],
                             start=(ncb == 0), stop=False)
                return e.matmul(p.psum[oh][:, :], att[:, :],
                                vs[:, oh * 512:(oh + 1) * 512], start=False, stop=True)
            p.add("tensor", mo, reads=[c.R(qtl), c.R(wbuf), c.R(att), vreg],
                  writes=[p.pr(oh)], late=True)
            p.add("vector", lambda e, oh=oh: e.tensor_copy(
                obuf[:, oh * 512:(oh + 1) * 512], p.psum[oh][:, :]),
                reads=[p.pr(oh)], writes=[c.R(obuf, oh * 512, (oh + 1) * 512)], late=True)
        for hb in range(0, 8, 4):
            bank = 6 + (hb // 4)

            def tpo2(e, hb=hb, bank=bank):
                for j in range(4):
                    ins = e.transpose(p.psum[bank].ap()[:, j * 128:(j + 1) * 128],
                                      obuf[:, (hb + j) * 128:(hb + j + 1) * 128],
                                      identb[:, :])
                return ins
            p.add("tensor", tpo2, reads=[c.R(obuf), c.R(identb)], writes=[p.pr(bank)],
                  late=True)
            p.add("vector", lambda e, hb=hb, bank=bank: e.tensor_copy(
                ofm[:, hb * 128:(hb + 4) * 128], p.psum[bank].ap()[:, 0:512]),
                reads=[p.pr(bank)], writes=[c.R(ofm)], late=True)
        for oh in range(2):
            def mwo(e, oh=oh):
                for ic in range(DC):
                    ins = e.matmul(p.psum[2 + oh][:, :], ofm[:, ic * 128:(ic + 1) * 128],
                                   wob[:, ic * D + oh * 512: ic * D + (oh + 1) * 512],
                                   start=(ic == 0), stop=(ic == DC - 1))
                return ins
            p.add("tensor", mwo, reads=[c.R(ofm), (_SA, _SO, _SO + 16384)], writes=[p.pr(2 + oh)],
                  late=True)
            xreg = c.R(xs, ch * D + oh * 512, ch * D + (oh + 1) * 512)
            p.add("vector", lambda e, oh=oh, ch=ch: e.tensor_tensor(
                out=stmp[:, :], in0=p.psum[2 + oh][:, :],
                in1=bob[:, oh * 512:(oh + 1) * 512], op=ALU.add),
                reads=[p.pr(2 + oh), c.R(bob)], writes=[c.R(stmp)], late=True)
            p.add("vector", lambda e, oh=oh, ch=ch: e.tensor_tensor(
                out=xs[:, ch * D + oh * 512: ch * D + (oh + 1) * 512],
                in0=xs[:, ch * D + oh * 512: ch * D + (oh + 1) * 512],
                in1=stmp[:, :], op=ALU.add),
                reads=[xreg, c.R(stmp)], writes=[xreg], late=True)
        if dbg is not None and ch == 0:
            for buf, nm in [(qt, "dq"), (kt, "dk"), (vt, "dv")]:
                p.dma("sync", lambda e, buf=buf, nm=nm: e.dma_start(
                    out=dbg[nm][0:128, :], in_=buf[:, 0:D]),
                    reads=[rg(buf, 0, D)], group="dbg" + nm)
            p.dma("sync", lambda e: e.dma_start(out=dbg["dl"][0:128, :], in_=lt[:, 0:D]),
                  reads=[rg(lt, 0, D)], group="dbgdl")
            p.dma("sync", lambda e: e.dma_start(out=dbg["att0"], in_=att[:, :]),
                  reads=[c.R(att)], group="dbga")
            p.dma("sync", lambda e: e.dma_start(out=dbg["o0"], in_=obuf[:, :]),
                  reads=[c.R(obuf, 0, D)], group="dbgo")
            p.dma("sync", lambda e: e.dma_start(out=dbg["bp0"], in_=bp[:, :]),
                  reads=[c.R(bp, 0, D)], group="dbgbp")
            p.dma("sync", lambda e: e.dma_start(out=dbg["qh0"], in_=qhf[:, :]),
                  reads=[c.R(qhf)], group="dbgqh")
        for ncb in range(8):
            for oh in range(2):
                def mu(e, ncb=ncb, oh=oh, vs=vs):
                    e.matmul(p.psum[4][:, :], khat.ap()[0:64, ncb * 128:(ncb + 1) * 128],
                             vs[0:64, oh * 512:(oh + 1) * 512], start=True, stop=True)
                    return e.matmul(p.psum[5][:, :], khat1[:, ncb * 128:(ncb + 1) * 128],
                                    v1[:, oh * 512:(oh + 1) * 512], start=True, stop=True)
                p.add("tensor", mu, reads=[c.R(khat), c.R(khat1), vreg, c.R(v1)],
                      writes=[p.pr(4), p.pr(5)], late=True)
                sseg = (ncb * D + oh * 512, ncb * D + (oh + 1) * 512)
                sreg = SRr(sseg[0], sseg[1])
                p.add("vector", lambda e, sseg=sseg, ncb=ncb: e.tensor_scalar(
                    out=S[:, sseg[0]:sseg[1]], in0=S[:, sseg[0]:sseg[1]],
                    scalar1=fcol[:, 16 + ncb:17 + ncb], scalar2=None, op0=ALU.mult),
                    reads=[sreg, c.R(fcol, 16, 24)], writes=[sreg], late=True)
                p.add("vector", lambda e, sseg=sseg, ncb=ncb: e.scalar_tensor_tensor(
                    out=S[:, sseg[0]:sseg[1]], in0=p.psum[4][:, :],
                    scalar=fcol[:, 16 + ncb:17 + ncb], in1=S[:, sseg[0]:sseg[1]],
                    op0=ALU.mult, op1=ALU.add),
                    reads=[p.pr(4), sreg, c.R(fcol, 16, 24)], writes=[sreg], late=True)
                p.add("vector", lambda e, sseg=sseg, ncb=ncb: e.scalar_tensor_tensor(
                    out=S[:, sseg[0]:sseg[1]], in0=p.psum[5][:, :],
                    scalar=fcol[:, 2 * ncb + 1:2 * ncb + 2], in1=S[:, sseg[0]:sseg[1]],
                    op0=ALU.mult, op1=ALU.add),
                    reads=[p.pr(5), sreg, c.R(fcol, 0, 16)], writes=[sreg], late=True)
                p.add("vector", lambda e, sseg=sseg: e.tensor_copy(
                    Sb[:, sseg[0]:sseg[1]], S[:, sseg[0]:sseg[1]]),
                    reads=[sreg], writes=[rg(wbuf, sseg[0], sseg[1])], late=True)


def build_L1_attn_b(dbg=False):
    nc = bass.Bass("TRN2", target_bir_lowering=False)
    x = nc.dram_tensor("x", [T, D], F32, kind="ExternalInput").ap()
    aps = {}
    for wn in ["Wq", "Wk", "Wv", "Wg", "Wo"]:
        aps[wn] = nc.dram_tensor(wn, [D, D], F32, kind="ExternalInput").ap()
    for bn in ["bq", "bk", "bv", "bg", "bo"]:
        aps[bn] = nc.dram_tensor(bn, [D], F32, kind="ExternalInput").ap()
    if dbg:
        aps["_dbg"] = {
            "dq": nc.dram_tensor("dq", [T, D], BF16, kind="ExternalOutput").ap(),
            "dk": nc.dram_tensor("dk", [T, D], BF16, kind="ExternalOutput").ap(),
            "dv": nc.dram_tensor("dv", [T, D], BF16, kind="ExternalOutput").ap(),
            "dl": nc.dram_tensor("dl", [T, D], F32, kind="ExternalOutput").ap(),
            "att0": nc.dram_tensor("att0", [128, 128], BF16, kind="ExternalOutput").ap(),
            "o0": nc.dram_tensor("o0", [128, D], BF16, kind="ExternalOutput").ap(),
            "bp0": nc.dram_tensor("bp0", [128, D], F32, kind="ExternalOutput").ap(),
            "qh0": nc.dram_tensor("qh0", [128, D], BF16, kind="ExternalOutput").ap(),
        }
    y = nc.dram_tensor("y", [T, D], F32, kind="ExternalOutput").ap()
    with ExitStack() as stack:
        p = Prog(nc, stack)
        c = Ctx(p)
        emit_identity(p, c)
        emit_load_x(p, c, x)
        emit_ln_to_fm(p, c)
        emit_l1attnb(p, c, aps)
        emit_store_xs(p, c, y)
        with nc.Block() as block:
            p.emit(block)
    return nc



def emit_l0attnb(p, c, aps, uid=""):
    xs, stats, lnbuf, ident, identb = c.xs, c.stats, c.lnbuf, c.ident, c.identb
    nc = p.nc
    NH, HP, NS = 32, 64, 128
    DI = 2048
    xn_fm = c.x_fm
    _XF = "x_fm"
    xcd = nc.dram_tensor("l0b_xcd" + uid, [T, DI], BF16, kind="Internal").ap()
    szd = nc.dram_tensor("l0b_szd" + uid, [T, DI], BF16, kind="Internal").ap()

    nat = p.sb("l0b_nat" + uid, [128, 2 * D], BF16)
    wslc = p.sb("l0b_wslc" + uid, [128, 2 * DC * 128], BF16)
    xcf = p.sb("l0b_xcf" + uid, [128, T], F32)
    xcb = p.sb("l0b_xcb" + uid, [128, T], BF16)
    B_fm = p.sb("l0b_Bfm" + uid, [128, T], BF16)
    C_fm = p.sb("l0b_Cfm" + uid, [128, T], BF16)
    dtb = p.sb("l0b_dtb" + uid, [128, TT * NH], F32)
    lgb = p.sb("l0b_lgb" + uid, [128, TT * NH], F32)
    btm = p.sb("l0b_btm" + uid, [128, NH], F32)
    btotb = p.sb("l0b_btot" + uid, [128, 2 * NH], F32)
    hcols = p.sb("l0b_hcols" + uid, [128, 4 * NH], F32)
    u128 = p.sb("l0b_u128" + uid, [128, 128], F32)
    msk01 = p.sb("l0b_msk" + uid, [128, 128], F32)
    onesc = p.sb("l0b_ones" + uid, [128, 2], F32)
    onesrow = p.sb("l0b_onesr" + uid, [1, 128], F32)
    btrow = p.sb("l0b_btr" + uid, [1, NH], F32)
    brep = p.sb("l0b_brep" + uid, [128, 2 * 128], F32)
    Mt = p.sb("l0b_Mt" + uid, [128, 2 * 128], BF16)
    dtmp = p.sb("l0b_dtmp" + uid, [128, 2 * 128], F32)
    CBm = p.sb("l0b_CBm" + uid, [128, 128], F32)
    B_tm = p.sb("l0b_Btm" + uid, [128, 128], BF16)
    Bhat = p.sb("l0b_Bhat" + uid, [128, 2 * 128], BF16)
    t64 = p.sb("l0b_t64" + uid, [128, 2 * 64], F32)
    xh_tm = p.sb("l0b_xht" + uid, [128, 2 * DI], BF16)
    ybuf = p.sb("l0b_ybuf" + uid, [128, DI], BF16)
    szt = p.sb("l0b_szt" + uid, [128, 2 * DI], BF16)
    gfm = p.sb("l0b_gfm" + uid, [128, DI], BF16)
    S = p.sb("l0b_S" + uid, [128, NH * HP], F32)
    Sbf = p.sb("l0b_Sbf" + uid, [128, NH * HP], BF16)
    nwb = p.sb("l0b_nwb" + uid, [128, D], F32)
    cwb = p.sb("l0b_cwb" + uid, [128, 16 * 4], F32)
    cbb = p.sb("l0b_cbb" + uid, [128, 16], F32)
    eA = p.sb("l0b_eA" + uid, [128, 2 * NH], F32)
    dtbias = p.sb("l0b_dtbias" + uid, [128, NH], F32)
    tst = p.sb("l0b_tst" + uid, [128, 4], F32)

    p.add("gpsimd", lambda e: e.memset(u128[:, :], 1.0), writes=[c.R(u128)])
    p.add("gpsimd", lambda e: e.affine_select(
        out=u128[:, :], in_=u128[:, :], compare_op=ALU.is_ge, fill=0.0,
        base=0, pattern=[[1, 128]], channel_multiplier=-1),
        reads=[c.R(u128)], writes=[c.R(u128)], late=True)
    p.add("gpsimd", lambda e: e.memset(msk01[:, :], 1.0), writes=[c.R(msk01)])
    p.add("gpsimd", lambda e: e.affine_select(
        out=msk01[:, :], in_=msk01[:, :], compare_op=ALU.is_ge, fill=0.0,
        base=0, pattern=[[1, 128]], channel_multiplier=-1),
        reads=[c.R(msk01)], writes=[c.R(msk01)], late=True)
    p.add("gpsimd", lambda e: e.memset(onesc[:, :], 1.0), writes=[c.R(onesc)])
    p.add("gpsimd", lambda e: e.memset(onesrow[:, :], 1.0), writes=[c.R(onesrow)])
    p.dma("gpsimd", lambda e: e.dma_start(
        out=nwb[:, :], in_=aps["norm_w"][None, :].broadcast_to((128, D))),
        writes=[c.R(nwb)], group="l0b_nw" + uid)
    p.dma("gpsimd", lambda e: e.dma_start(
        out=cwb.ap().rearrange("p (cc k) -> p cc k", k=4),
        in_=aps["conv_w"].rearrange("(cc p) k -> p cc k", p=128)),
        writes=[c.R(cwb)], group="l0b_cw" + uid)
    p.dma("gpsimd", lambda e: e.dma_start(
        out=cbb[:, :], in_=aps["conv_b"].rearrange("(cc p) -> p cc", p=128),
        allow_slow_non_contiguous=True),
        writes=[c.R(cbb)], group="l0b_cb" + uid)
    if aps.get("_dbg") is not None:
        p.dma("sync", lambda e: e.dma_start(out=aps["_dbg"]["cwbE"], in_=cwb[:, :]),
              reads=[c.R(cwb)], group="dbg_cwbE")
    p.dma("gpsimd", lambda e: e.dma_start(
        out=eA[:, 0:NH], in_=aps["A_log"][None, :].broadcast_to((128, NH))),
        writes=[c.R(eA, 0, NH)], group="l0b_eA" + uid)
    p.add("scalar", lambda e: e.activation(
        out=eA[:, 0:NH], in_=eA[:, 0:NH], func=AF.Exp),
        reads=[c.R(eA, 0, NH)], writes=[c.R(eA, 0, NH)], late=True)
    p.dma("gpsimd", lambda e: e.dma_start(
        out=eA[:, NH:2 * NH], in_=aps["D"][None, :].broadcast_to((128, NH))),
        writes=[c.R(eA, NH, 2 * NH)], group="l0b_D" + uid)
    p.dma("gpsimd", lambda e: e.dma_start(
        out=dtbias[:, :], in_=aps["dt_bias"][None, :].broadcast_to((128, NH))),
        writes=[c.R(dtbias)], group="l0b_dtb" + uid)
    p.add("vector", lambda e: e.memset(S[:, :], 0.0), writes=[c.R(S)])
    p.add("vector", lambda e: e.memset(Sbf[:, :], 0.0), writes=[c.R(Sbf)])

    for t in range(TT):
        xreg = c.R(xs, t * D, (t + 1) * D)
        ureg = c.R(lnbuf, 0, D)
        p.add("scalar", lambda e, t=t: e.activation(
            out=lnbuf[:, :], in_=xs[:, t * D:(t + 1) * D], func=AF.Identity,
            scale=stats[:, t:t + 1], bias=stats[:, TT + t:TT + t + 1]),
            reads=[xreg, c.R(stats, 0, 2 * TT)], writes=[ureg], late=True)
        p.add("vector", lambda e, t=t: e.tensor_tensor(
            out=xs[:, t * D:(t + 1) * D], in0=xs[:, t * D:(t + 1) * D],
            in1=lnbuf[:, :], op=ALU.add),
            reads=[xreg, ureg], writes=[xreg], late=True)
        p.add("scalar", lambda e: e.activation(
            out=xcf[:, 0:D], in_=lnbuf[:, :], func=AF.Square, accum_out=tst[:, 0:1]),
            reads=[ureg], writes=[c.R(xcf, 0, D), c.R(tst, 0, 1)], late=True)
        p.add("vector", lambda e: e.tensor_scalar(
            out=tst[:, 0:1], in0=tst[:, 0:1], scalar1=1.0 / D, scalar2=1e-6,
            op0=ALU.mult, op1=ALU.add),
            reads=[c.R(tst, 0, 1)], writes=[c.R(tst, 0, 1)], late=True)
        p.add("scalar", lambda e: e.activation(
            out=tst[:, 0:1], in_=tst[:, 0:1], func=AF.Ln),
            reads=[c.R(tst, 0, 1)], writes=[c.R(tst, 0, 1)], late=True)
        p.add("scalar", lambda e: e.activation(
            out=tst[:, 1:2], in_=tst[:, 0:1], func=AF.Exp, scale=-0.5),
            reads=[c.R(tst, 0, 1)], writes=[c.R(tst, 1, 2)], late=True)
        p.add("scalar", lambda e: e.activation(
            out=xcf[:, 0:D], in_=lnbuf[:, :], func=AF.Identity, scale=tst[:, 1:2]),
            reads=[ureg, c.R(tst, 1, 2)], writes=[c.R(xcf, 0, D)], late=True)
        p.add("vector", lambda e: e.tensor_tensor(
            out=xcf[:, 0:D], in0=xcf[:, 0:D], in1=nwb[:, :], op=ALU.mult),
            reads=[c.R(xcf, 0, D), c.R(nwb)], writes=[c.R(xcf, 0, D)])
        emit_tm_tile_to_fm(p, c, xcf, c.R(xcf, 0, D), xn_fm, t, pref="l0b" + uid)

    xn_r = [(_XF, 2 * ic * T, 2 * (ic + 1) * T) for ic in range(DC)]

    def wT_slab(oc_idx, r0, rows, slot):
        sreg = rg(wslc, slot * DC * 128, (slot + 1) * DC * 128)
        nreg = c.R(nat, (slot % 2) * D, (slot % 2 + 1) * D)
        p.dma("gpsimd", lambda e, r0=r0, rows=rows, slot=slot: e.dma_start(
            out=nat[:, (slot % 2) * D:(slot % 2) * D + D][0:rows, :],
            in_=aps["in_proj_w"][r0:r0 + rows, :]),
            writes=[nreg], group="l0b_nat" + uid + str(slot % 2))
        for ic in range(DC):
            p.dma("sync", lambda e, ic=ic, rows=rows, slot=slot: e.dma_start_transpose(
                wslc[:, slot * DC * 128 + ic * 128: slot * DC * 128 + ic * 128 + rows],
                nat[0:rows, (slot % 2) * D + ic * 128:(slot % 2) * D + (ic + 1) * 128]),
                reads=[nreg], writes=[sreg], group="l0b_wsl" + uid + str(slot % 2))
        return sreg

    for oc in range(16):
        slot = oc % 2
        sreg = wT_slab(oc, 2048 + oc * 128, 128, slot)
        for tg in range(4):
            bank = tg % 2

            def mmx(e, tg=tg, slot=slot, bank=bank):
                for ic in range(DC):
                    ins = e.matmul(p.psum[bank][:, :],
                                   wslc[:, slot * DC * 128 + ic * 128: slot * DC * 128 + (ic + 1) * 128],
                                   xn_fm[:, ic * T + tg * 512: ic * T + (tg + 1) * 512],
                                   start=(ic == 0), stop=(ic == DC - 1))
                return ins
            p.add("tensor", mmx, reads=xn_r + [sreg], writes=[p.pr(bank)], late=True)
            p.add("scalar", lambda e, tg=tg, bank=bank: e.activation(
                out=xcf[:, tg * 512:(tg + 1) * 512], in_=p.psum[bank][:, :],
                func=AF.Identity),
                reads=[p.pr(bank)], writes=[c.R(xcf, tg * 512, (tg + 1) * 512)],
                late=True)
        if aps.get("_dbg") is not None and oc == 0:
            p.dma("sync", lambda e: e.dma_start(out=aps["_dbg"]["xcf0"], in_=xcf[:, :]),
                  reads=[c.R(xcf, 0, T)], group="dbg_xcf0")
            p.dma("sync", lambda e: e.dma_start(out=aps["_dbg"]["cwbL"], in_=cwb[:, :]),
                  reads=[c.R(cwb)], group="dbg_cwbL")
        accf = p.sb("l0b_accf" + uid, [128, T], F32)
        p.add("vector", lambda e, oc=oc: e.tensor_scalar(
            out=accf[:, :], in0=xcf[:, :], scalar1=cwb[:, oc * 4 + 3:oc * 4 + 4],
            scalar2=None, op0=ALU.mult),
            reads=[c.R(xcf, 0, T), c.R(cwb)], writes=[c.R(accf)], late=True)
        for k in range(3):
            sh = 3 - k
            p.add("vector", lambda e, k=k, sh=sh, oc=oc: e.scalar_tensor_tensor(
                out=accf[:, sh:T], in0=xcf[:, 0:T - sh],
                scalar=cwb[:, oc * 4 + k:oc * 4 + k + 1], in1=accf[:, sh:T],
                op0=ALU.mult, op1=ALU.add),
                reads=[c.R(xcf, 0, T), c.R(cwb), c.R(accf)], writes=[c.R(accf)],
                late=True)
        p.add("scalar", lambda e, oc=oc: e.activation(
            out=xcb[:, :], in_=accf[:, :], func=AF.Silu, bias=cbb[:, oc:oc + 1]),
            reads=[c.R(accf), c.R(cbb)], writes=[c.R(xcb)], late=True)
        if aps.get("_dbg") is not None and oc == 0:
            p.dma("sync", lambda e: e.dma_start(out=aps["_dbg"]["acc0"], in_=accf[:, :]),
                  reads=[c.R(accf)], group="dbg_acc0")
            p.dma("sync", lambda e: e.dma_start(out=aps["_dbg"]["xcb0"], in_=xcb[:, :]),
                  reads=[c.R(xcb)], group="dbg_xcb0")
        for tch in range(0, TT, 4):
            bank = 6 + (tch // 4) % 2

            def tpx(e, tch=tch, bank=bank):
                for j in range(4):
                    ins = e.transpose(p.psum[bank].ap()[:, j * 128:(j + 1) * 128],
                                      xcb[:, (tch + j) * 128:(tch + j + 1) * 128],
                                      identb[:, :])
                return ins
            p.add("tensor", tpx, reads=[c.R(xcb), c.R(identb)], writes=[p.pr(bank)],
                  late=True)
            p.add("vector", lambda e, tch=tch, bank=bank: e.tensor_copy(
                gfm[:, (tch % 8) * 128:(tch % 8) * 128 + 512] if False else
                xh_tm[:, (tch % 8) * 128:(tch % 8) * 128 + 512],
                p.psum[bank].ap()[:, 0:512]),
                reads=[p.pr(bank)],
                writes=[rg(xh_tm, (tch % 8) * 128, (tch % 8) * 128 + 512)],
                late=True)
            for j in range(4):
                p.dma("sync", lambda e, tch=tch, j=j, oc=oc: e.dma_start(
                    out=xcd[(tch + j) * 128:(tch + j + 1) * 128, oc * 128:(oc + 1) * 128],
                    in_=xh_tm[:, (tch % 8 + j) * 128:(tch % 8 + j + 1) * 128]),
                    reads=[rg(xh_tm, (tch % 8 + j) * 128, (tch % 8 + j + 1) * 128)],
                    writes=[("@l0bxcd" + uid, oc * TT + tch + j, oc * TT + tch + j + 1)],
                    group="l0b_spx" + uid)

    for nm, dst, r0 in [("B", B_fm, 4096), ("C", C_fm, 4224)]:
        slot = 0 if nm == "B" else 1
        sreg = wT_slab(0, r0, 128, slot)
        for tg in range(4):
            bank = tg % 2

            def mmb(e, tg=tg, slot=slot, bank=bank):
                for ic in range(DC):
                    ins = e.matmul(p.psum[bank][:, :],
                                   wslc[:, slot * DC * 128 + ic * 128: slot * DC * 128 + (ic + 1) * 128],
                                   xn_fm[:, ic * T + tg * 512: ic * T + (tg + 1) * 512],
                                   start=(ic == 0), stop=(ic == DC - 1))
                return ins
            p.add("tensor", mmb, reads=xn_r + [sreg], writes=[p.pr(bank)], late=True)
            p.add("vector", lambda e, tg=tg, bank=bank, dst=dst: e.tensor_copy(
                dst[:, tg * 512:(tg + 1) * 512], p.psum[bank][:, :]),
                reads=[p.pr(bank)], writes=[c.R(dst, tg * 512, (tg + 1) * 512)],
                late=True)

    sreg = wT_slab(0, 4352, 32, 0)
    for t in range(TT):
        bank = t % 2

        def mmdt(e, t=t, bank=bank):
            for ic in range(DC):
                ins = e.matmul(p.psum[bank][:, 0:32],
                               xn_fm[:, ic * T + t * 128: ic * T + (t + 1) * 128],
                               wslc[:, ic * 128: ic * 128 + 32],
                               start=(ic == 0), stop=(ic == DC - 1))
            return ins
        p.add("tensor", mmdt, reads=xn_r + [sreg], writes=[p.pr(bank)], late=True)
        p.add("scalar", lambda e, t=t, bank=bank: e.activation(
            out=dtb[:, t * NH:(t + 1) * NH], in_=p.psum[bank][:, 0:32], func=AF.Identity),
            reads=[p.pr(bank)], writes=[c.R(dtb, t * NH, (t + 1) * NH)], late=True)
    allr = c.R(dtb, 0, TT * NH)
    allg = c.R(lgb, 0, TT * NH)
    p.add("vector", lambda e: e.tensor_tensor(
        out=dtb[:, :].rearrange("p (t h) -> p t h", h=NH), in0=dtb[:, :].rearrange("p (t h) -> p t h", h=NH),
        in1=dtbias[:, :].rearrange("p (a h) -> p a h", a=1).broadcast_to((128, TT, NH)),
        op=ALU.add),
        reads=[allr, c.R(dtbias)], writes=[allr], late=True)
    p.add("scalar", lambda e: e.activation(
        out=lgb[:, :], in_=dtb[:, :], func=AF.Abs),
        reads=[allr], writes=[allg], late=True)
    p.add("scalar", lambda e: e.activation(
        out=lgb[:, :], in_=lgb[:, :], func=AF.Exp, scale=-1.0),
        reads=[allg], writes=[allg], late=True)
    p.add("scalar", lambda e: e.activation(
        out=lgb[:, :], in_=lgb[:, :], func=AF.Ln, bias=1.0),
        reads=[allg], writes=[allg], late=True)
    p.add("scalar", lambda e: e.activation(
        out=dtb[:, :], in_=dtb[:, :], func=AF.Relu),
        reads=[allr], writes=[allr], late=True)
    p.add("vector", lambda e: e.tensor_tensor(
        out=dtb[:, :], in0=dtb[:, :], in1=lgb[:, :], op=ALU.add),
        reads=[allr, allg], writes=[allr], late=True)
    p.add("vector", lambda e: e.tensor_tensor(
        out=lgb[:, :].rearrange("p (t h) -> p t h", h=NH),
        in0=dtb[:, :].rearrange("p (t h) -> p t h", h=NH),
        in1=eA[:, 0:NH].rearrange("p (a h) -> p a h", a=1).broadcast_to((128, TT, NH)),
        op=ALU.mult),
        reads=[allr, c.R(eA, 0, NH)], writes=[allg], late=True)

    for oc in range(16):
        slot = oc % 2
        sreg = wT_slab(oc, 0 + oc * 128, 128, slot)
        for tg in range(4):
            bank = tg % 2

            def mmz(e, tg=tg, slot=slot, bank=bank):
                for ic in range(DC):
                    ins = e.matmul(p.psum[bank][:, :],
                                   wslc[:, slot * DC * 128 + ic * 128: slot * DC * 128 + (ic + 1) * 128],
                                   xn_fm[:, ic * T + tg * 512: ic * T + (tg + 1) * 512],
                                   start=(ic == 0), stop=(ic == DC - 1))
                return ins
            p.add("tensor", mmz, reads=xn_r + [sreg], writes=[p.pr(bank)], late=True)
            p.add("scalar", lambda e, tg=tg, bank=bank: e.activation(
                out=xcb[:, tg * 512:(tg + 1) * 512], in_=p.psum[bank][:, :], func=AF.Silu),
                reads=[p.pr(bank)], writes=[c.R(xcb, tg * 512, (tg + 1) * 512)],
                late=True)
        for tch in range(0, TT, 4):
            bank = 6 + (tch // 4) % 2

            def tpz(e, tch=tch, bank=bank):
                for j in range(4):
                    ins = e.transpose(p.psum[bank].ap()[:, j * 128:(j + 1) * 128],
                                      xcb[:, (tch + j) * 128:(tch + j + 1) * 128],
                                      identb[:, :])
                return ins
            p.add("tensor", tpz, reads=[c.R(xcb), c.R(identb)], writes=[p.pr(bank)],
                  late=True)
            p.add("vector", lambda e, tch=tch, bank=bank: e.tensor_copy(
                xh_tm[:, (tch % 8) * 128:(tch % 8) * 128 + 512],
                p.psum[bank].ap()[:, 0:512]),
                reads=[p.pr(bank)],
                writes=[rg(xh_tm, (tch % 8) * 128, (tch % 8) * 128 + 512)],
                late=True)
            for j in range(4):
                p.dma("sync", lambda e, tch=tch, j=j, oc=oc: e.dma_start(
                    out=szd[(tch + j) * 128:(tch + j + 1) * 128, oc * 128:(oc + 1) * 128],
                    in_=xh_tm[:, (tch % 8 + j) * 128:(tch % 8 + j + 1) * 128]),
                    reads=[rg(xh_tm, (tch % 8 + j) * 128, (tch % 8 + j + 1) * 128)],
                    writes=[("@l0bszd" + uid, oc * TT + tch + j, oc * TT + tch + j + 1)],
                    group="l0b_spz" + uid)

    wopT = c.x_fm
    for oc in range(DC):
        slot = oc % 2
        nreg = c.R(nat, slot * D, (slot + 1) * D)
        p.dma("gpsimd", lambda e, oc=oc, slot=slot: e.dma_start(
            out=nat.ap()[:, slot * D:(slot + 1) * D].rearrange("p (a i) -> p a i", a=1)[:, 0, :],
            in_=aps["out_proj_w"][oc * 128:(oc + 1) * 128, 0:1024]),
            writes=[nreg], group="l0b_wo1" + uid + str(slot))
        for ic in range(8):
            p.dma2("sync", lambda e, oc=oc, ic=ic, slot=slot: e.dma_start_transpose(
                wopT.ap()[:, ic * 1024 + oc * 128: ic * 1024 + (oc + 1) * 128],
                nat[:, slot * D + ic * 128: slot * D + (ic + 1) * 128]),
                reads=[nreg], writes=[(_XF, 2 * (ic * 1024 + oc * 128), 2 * (ic * 1024 + (oc + 1) * 128))],
                group="l0b_woT" + uid)
    nat2 = p.sb("l0b_nat2" + uid, [128, 2 * D], BF16)
    for oc in range(DC):
        slot = oc % 2
        nreg = c.R(nat2, slot * D, (slot + 1) * D)
        p.dma("gpsimd", lambda e, oc=oc, slot=slot: e.dma_start(
            out=nat2[:, slot * D:(slot + 1) * D],
            in_=aps["out_proj_w"][oc * 128:(oc + 1) * 128, 1024:2048]),
            writes=[nreg], group="l0b_wo2" + uid + str(slot))
        for ic in range(8):
            p.dma2("sync", lambda e, oc=oc, ic=ic, slot=slot: e.dma_start_transpose(
                wopT.ap()[:, (8 + ic) * 1024 + oc * 128: (8 + ic) * 1024 + (oc + 1) * 128],
                nat2[:, slot * D + ic * 128: slot * D + (ic + 1) * 128]),
                reads=[nreg],
                writes=[(_XF, 2 * ((8 + ic) * 1024 + oc * 128), 2 * ((8 + ic) * 1024 + (oc + 1) * 128))],
                group="l0b_woT" + uid)
    wop_r = (_XF, 0, 2 * 16 * 1024)

    dbg = aps.get("_dbg")
    if dbg is not None:
        p.dma("sync", lambda e: e.dma_start(out=dbg["dtb"], in_=dtb[:, :]),
              reads=[allr], group="dbg_dtb")
        p.dma("sync", lambda e: e.dma_start(out=dbg["lgb"], in_=lgb[:, :]),
              reads=[allg], group="dbg_lgb")
        p.dma("sync", lambda e: e.dma_start(out=dbg["Bfm"], in_=B_fm[:, :]),
              reads=[c.R(B_fm)], group="dbg_Bfm")
        p.dma("sync", lambda e: e.dma_start(out=dbg["Cfm"], in_=C_fm[:, :]),
              reads=[c.R(C_fm)], group="dbg_Cfm")

    for ch in range(TT):
        par = ch % 2
        xseg = (par * DI, (par + 1) * DI)
        xh_r = rg(xh_tm, xseg[0], xseg[1])
        p.dma2("sync", lambda e, ch=ch, par=par: e.dma_start(
            out=xh_tm[:, par * DI:(par + 1) * DI],
            in_=xcd[ch * 128:(ch + 1) * 128, :]),
            reads=[("@l0bxcd" + uid, 15 * TT + 15, 15 * TT + 16)],
            writes=[xh_r], group="l0b_ldx" + uid + str(par))
        p.dma2("sync", lambda e, ch=ch, par=par: e.dma_start(
            out=szt[:, par * DI:(par + 1) * DI],
            in_=szd[ch * 128:(ch + 1) * 128, :]),
            reads=[("@l0bszd" + uid, 15 * TT + 15, 15 * TT + 16)],
            writes=[rg(szt, xseg[0], xseg[1])], group="l0b_ldz" + uid + str(par))
        lgs = lgb[:, ch * NH:(ch + 1) * NH]
        lgr = c.R(lgb, ch * NH, (ch + 1) * NH)
        def mcs(e, lgs=lgs):
            return e.matmul(p.psum[2][:, 0:NH], u128[:, :], lgs, start=True, stop=True)
        p.add("tensor", mcs, reads=[c.R(u128), lgr], writes=[p.pr(2)], late=True)
        p.add("scalar", lambda e: e.activation(
            out=btm[:, :], in_=p.psum[2][:, 0:NH], func=AF.Identity),
            reads=[p.pr(2)], writes=[c.R(btm)], late=True)

        def mts(e, lgs=lgs):
            return e.matmul(p.psum[2].ap()[0:1, 64:64 + NH], onesc[:, 0:1], lgs,
                            start=True, stop=True)
        p.add("tensor", mts, reads=[lgr, c.R(onesc)], writes=[p.pr(2)], late=True)
        p.add("scalar", lambda e: e.activation(
            out=btrow[:, :], in_=p.psum[2].ap()[0:1, 64:64 + NH], func=AF.Identity),
            reads=[p.pr(2)], writes=[c.R(btrow)], late=True)

        def mbc2(e):
            return e.matmul(p.psum[3][:, 0:NH], onesrow[:, :], btrow[:, :],
                            start=True, stop=True)
        p.add("tensor", mbc2, reads=[c.R(onesrow), c.R(btrow)],
              writes=[p.pr(3)], late=True)
        p.add("scalar", lambda e: e.activation(
            out=btotb[:, 0:NH], in_=p.psum[3][:, 0:NH], func=AF.Identity),
            reads=[p.pr(3)], writes=[c.R(btotb, 0, NH)], late=True)
        p.add("vector", lambda e: e.tensor_scalar(
            out=btotb[:, NH:2 * NH], in0=btotb[:, 0:NH], scalar1=-1.0, scalar2=None,
            op0=ALU.mult),
            reads=[c.R(btotb, 0, NH)], writes=[c.R(btotb, NH, 2 * NH)], late=True)
        def mcb(e, ch=ch):
            return e.matmul(p.psum[4][:, 0:128], B_fm[:, ch * 128:(ch + 1) * 128],
                            C_fm[:, ch * 128:(ch + 1) * 128], start=True, stop=True)
        p.add("tensor", mcb, reads=[c.R(B_fm), c.R(C_fm)], writes=[p.pr(4)], late=True)
        p.add("vector", lambda e: e.tensor_tensor(
            out=CBm[:, :], in0=p.psum[4][:, 0:128], in1=msk01[:, :], op=ALU.mult),
            reads=[p.pr(4), c.R(msk01)], writes=[c.R(CBm)], late=True)
        p.add("scalar", lambda e: e.activation(
            out=hcols[:, 0:NH], in_=btm[:, :], func=AF.Exp, scale=-1.0),
            reads=[c.R(btm)], writes=[c.R(hcols, 0, NH)], late=True)
        p.add("vector", lambda e: e.tensor_tensor(
            out=hcols[:, NH:2 * NH], in0=btm[:, :], in1=btotb[:, 0:NH], op=ALU.subtract),
            reads=[c.R(btm), c.R(btotb, 0, NH)], writes=[c.R(hcols, NH, 2 * NH)],
            late=True)
        p.add("scalar", lambda e: e.activation(
            out=hcols[:, NH:2 * NH], in_=hcols[:, NH:2 * NH], func=AF.Exp),
            reads=[c.R(hcols, NH, 2 * NH)], writes=[c.R(hcols, NH, 2 * NH)], late=True)
        p.add("vector", lambda e, ch=ch: e.tensor_tensor(
            out=hcols[:, 2 * NH:3 * NH], in0=hcols[:, NH:2 * NH],
            in1=dtb[:, ch * NH:(ch + 1) * NH], op=ALU.mult),
            reads=[c.R(hcols, NH, 2 * NH), allr], writes=[c.R(hcols, 2 * NH, 3 * NH)],
            late=True)
        p.add("scalar", lambda e: e.activation(
            out=hcols[:, 3 * NH:4 * NH], in_=btotb[:, 0:NH], func=AF.Exp, scale=-1.0),
            reads=[c.R(btotb, 0, NH)], writes=[c.R(hcols, 3 * NH, 4 * NH)], late=True)
        def tpb(e, ch=ch):
            return e.transpose(p.psum[6].ap()[:, 0:128],
                               B_fm[:, ch * 128:(ch + 1) * 128], identb[:, :])
        p.add("tensor", tpb, reads=[c.R(B_fm), c.R(identb)], writes=[p.pr(6)], late=True)
        p.add("vector", lambda e: e.tensor_copy(B_tm[:, :], p.psum[6].ap()[:, 0:128]),
              reads=[p.pr(6)], writes=[c.R(B_tm)], late=True)
        for h in range(NH):
            w = h % 2
            W0, W1 = w * 128, (w + 1) * 128
            bcol = btm[:, h:h + 1]
            dtc = dtb[:, ch * NH + h: ch * NH + h + 1]
            brep_s = brep[:, W0:W1]
            brep_r = c.R(brep, W0, W1)
            dtmp_s = dtmp[:, W0:W1]
            dtmp_r = c.R(dtmp, W0, W1)
            Mt_s = Mt[:, W0:W1]
            Mt_r = c.R(Mt, W0, W1)
            Bh_s = Bhat[:, W0:W1]
            Bh_r = c.R(Bhat, W0, W1)
            t64_s = t64[:, w * 64:(w + 1) * 64]
            t64_r = c.R(t64, w * 64, (w + 1) * 64)
            bI, bA, bS = (0, 1, 5) if w == 0 else (2, 3, 5)
            p.add("vector", lambda e, bcol=bcol, brep_s=brep_s, brep_r=brep_r:
                  e.tensor_copy(brep_s, bcol.broadcast_to((128, 128))),
                  reads=[c.R(btm)], writes=[brep_r], late=True)

            def tpr(e, brep_s=brep_s, w=w):
                return e.transpose(p.psum[4].ap()[:, w * 128:(w + 1) * 128], brep_s,
                                   ident[:, :])
            p.add("tensor", tpr, reads=[brep_r, c.R(ident)], writes=[p.pr(4)],
                  late=True)
            p.add("vector", lambda e, bcol=bcol, dtmp_s=dtmp_s, w=w:
                  e.scalar_tensor_tensor(
                      out=dtmp_s, in0=p.psum[4].ap()[:, w * 128:(w + 1) * 128],
                      scalar=bcol, in1=msk01[:, :], op0=ALU.subtract, op1=ALU.mult),
                  reads=[p.pr(4), c.R(btm), c.R(msk01)], writes=[dtmp_r], late=True)
            p.add("scalar", lambda e, dtmp_s=dtmp_s: e.activation(
                out=dtmp_s, in_=dtmp_s, func=AF.Exp, scale=-1.0),
                reads=[dtmp_r], writes=[dtmp_r], late=True)
            p.add("vector", lambda e, dtc=dtc, dtmp_s=dtmp_s, Mt_s=Mt_s:
                  e.scalar_tensor_tensor(
                      out=Mt_s, in0=dtmp_s, scalar=dtc, in1=CBm[:, :],
                      op0=ALU.mult, op1=ALU.mult),
                  reads=[dtmp_r, allr, c.R(CBm)], writes=[Mt_r], late=True)

            def myi(e, ch=ch, h=h, bI=bI):
                return e.matmul(p.psum[bI][:, 0:HP], C_fm[:, ch * 128:(ch + 1) * 128],
                                Sbf[:, h * HP:(h + 1) * HP], start=True, stop=True)
            p.add("tensor", myi, reads=[c.R(C_fm), c.R(Sbf, h * HP, (h + 1) * HP)],
                  writes=[p.pr(bI)], late=True)

            def mya(e, h=h, par=par, Mt_s=Mt_s, bA=bA):
                return e.matmul(p.psum[bA][:, 0:HP], Mt_s,
                                xh_tm[:, par * DI + h * HP: par * DI + (h + 1) * HP],
                                start=True, stop=True)
            p.add("tensor", mya, reads=[Mt_r, xh_r], writes=[p.pr(bA)], late=True)
            p.add("scalar", lambda e, h=h, t64_s=t64_s, bI=bI: e.activation(
                out=t64_s, in_=p.psum[bI][:, 0:HP], func=AF.Identity,
                scale=hcols[:, h:h + 1]),
                reads=[p.pr(bI), c.R(hcols, 0, NH)], writes=[t64_r], late=True)
            p.add("vector", lambda e, t64_s=t64_s, bA=bA: e.tensor_tensor(
                out=t64_s, in0=t64_s, in1=p.psum[bA][:, 0:HP], op=ALU.add),
                reads=[t64_r, p.pr(bA)], writes=[t64_r], late=True)
            p.add("vector", lambda e, h=h, par=par, t64_s=t64_s: e.scalar_tensor_tensor(
                out=ybuf[:, h * HP:(h + 1) * HP],
                in0=xh_tm[:, par * DI + h * HP: par * DI + (h + 1) * HP],
                scalar=eA[:, NH + h:NH + h + 1], in1=t64_s,
                op0=ALU.mult, op1=ALU.add),
                reads=[xh_r, c.R(eA, NH, 2 * NH), t64_r],
                writes=[c.R(ybuf, h * HP, (h + 1) * HP)], late=True)
            p.add("vector", lambda e, h=h, Bh_s=Bh_s: e.tensor_scalar(
                out=Bh_s, in0=B_tm[:, :], scalar1=hcols[:, 2 * NH + h:2 * NH + h + 1],
                scalar2=None, op0=ALU.mult),
                reads=[c.R(B_tm), c.R(hcols, 2 * NH, 3 * NH)], writes=[Bh_r], late=True)

            def mst(e, h=h, par=par, Bh_s=Bh_s, w=w):
                return e.matmul(p.psum[5].ap()[:, w * 256:w * 256 + HP], Bh_s,
                                xh_tm[:, par * DI + h * HP: par * DI + (h + 1) * HP],
                                start=True, stop=True)
            p.add("tensor", mst, reads=[Bh_r, xh_r], writes=[p.pr(5)], late=True)
            sreg2 = c.R(S, h * HP, (h + 1) * HP)
            p.add("vector", lambda e, h=h: e.tensor_scalar(
                out=S[:, h * HP:(h + 1) * HP], in0=S[:, h * HP:(h + 1) * HP],
                scalar1=hcols[:, 3 * NH + h:3 * NH + h + 1], scalar2=None, op0=ALU.mult),
                reads=[sreg2, c.R(hcols, 3 * NH, 4 * NH)], writes=[sreg2], late=True)
            p.add("vector", lambda e, h=h, w=w: e.tensor_tensor(
                out=S[:, h * HP:(h + 1) * HP], in0=S[:, h * HP:(h + 1) * HP],
                in1=p.psum[5].ap()[:, w * 256:w * 256 + HP], op=ALU.add),
                reads=[sreg2, p.pr(5)], writes=[sreg2], late=True)
            p.add("vector", lambda e, h=h: e.tensor_copy(
                Sbf[:, h * HP:(h + 1) * HP], S[:, h * HP:(h + 1) * HP]),
                reads=[sreg2], writes=[c.R(Sbf, h * HP, (h + 1) * HP)], late=True)
        if dbg is not None and ch == 0:
            p.dma("sync", lambda e: e.dma_start(out=dbg["xh0"], in_=xh_tm[:, 0:DI]),
                  reads=[xh_r], group="dbg_xh0")
            p.dma("sync", lambda e: e.dma_start(out=dbg["sz0"], in_=szt[:, 0:DI]),
                  reads=[rg(szt, 0, DI)], group="dbg_sz0")
            p.dma("sync", lambda e: e.dma_start(out=dbg["y0"], in_=ybuf[:, :]),
                  reads=[c.R(ybuf, 0, DI)], group="dbg_y0")
            p.dma("sync", lambda e: e.dma_start(out=dbg["btm0"], in_=btm[:, :]),
                  reads=[c.R(btm)], group="dbg_btm0")
            p.dma("sync", lambda e: e.dma_start(out=dbg["CBm0"], in_=CBm[:, :]),
                  reads=[c.R(CBm)], group="dbg_CBm0")
            p.dma("sync", lambda e: e.dma_start(out=dbg["S0"], in_=S[:, :]),
                  reads=[c.R(S, 0, NH * HP)], group="dbg_S0")
        p.add("vector", lambda e, par=par: e.tensor_tensor(
            out=ybuf[:, :], in0=ybuf[:, :], in1=szt[:, par * DI:(par + 1) * DI],
            op=ALU.mult),
            reads=[c.R(ybuf, 0, DI), rg(szt, xseg[0], xseg[1])],
            writes=[c.R(ybuf, 0, DI)], late=True)
        for hb in range(0, 16, 4):
            bank = 6 + (hb // 4) % 2

            def tpg(e, hb=hb, bank=bank):
                for j in range(4):
                    ins = e.transpose(p.psum[bank].ap()[:, j * 128:(j + 1) * 128],
                                      ybuf[:, (hb + j) * 128:(hb + j + 1) * 128],
                                      identb[:, :])
                return ins
            p.add("tensor", tpg, reads=[c.R(ybuf, 0, DI), c.R(identb)],
                  writes=[p.pr(bank)], late=True)
            p.add("vector", lambda e, hb=hb, bank=bank: e.tensor_copy(
                gfm[:, hb * 128:(hb + 4) * 128], p.psum[bank].ap()[:, 0:512]),
                reads=[p.pr(bank)], writes=[c.R(gfm, hb * 128, (hb + 4) * 128)],
                late=True)
        for oh in range(2):
            def mwo(e, oh=oh):
                for ic in range(16):
                    ins = e.matmul(p.psum[2 + oh][:, :], gfm[:, ic * 128:(ic + 1) * 128],
                                   wopT.ap()[:, ic * 1024 + oh * 512: ic * 1024 + oh * 512 + 512],
                                   start=(ic == 0), stop=(ic == 15))
                return ins
            p.add("tensor", mwo, reads=[c.R(gfm, 0, DI), wop_r], writes=[p.pr(2 + oh)],
                  late=True)
            xreg = c.R(xs, ch * D + oh * 512, ch * D + (oh + 1) * 512)
            p.add("vector", lambda e, oh=oh, ch=ch: e.tensor_tensor(
                out=xs[:, ch * D + oh * 512: ch * D + (oh + 1) * 512],
                in0=xs[:, ch * D + oh * 512: ch * D + (oh + 1) * 512],
                in1=p.psum[2 + oh][:, :], op=ALU.add),
                reads=[xreg, p.pr(2 + oh)], writes=[xreg], late=True)


def build_L0_attn_b(dbg=False):
    nc = bass.Bass("TRN2", target_bir_lowering=False)
    x = nc.dram_tensor("x", [T, D], F32, kind="ExternalInput").ap()
    aps = {
        "norm_w": nc.dram_tensor("norm_w", [D], F32, kind="ExternalInput").ap(),
        "in_proj_w": nc.dram_tensor("in_proj_w", [4384, D], F32, kind="ExternalInput").ap(),
        "conv_w": nc.dram_tensor("conv_w", [2048, 4], F32, kind="ExternalInput").ap(),
        "conv_b": nc.dram_tensor("conv_b", [2048], F32, kind="ExternalInput").ap(),
        "A_log": nc.dram_tensor("A_log", [32], F32, kind="ExternalInput").ap(),
        "D": nc.dram_tensor("D", [32], F32, kind="ExternalInput").ap(),
        "dt_bias": nc.dram_tensor("dt_bias", [32], F32, kind="ExternalInput").ap(),
        "out_proj_w": nc.dram_tensor("out_proj_w", [D, 2048], F32, kind="ExternalInput").ap(),
    }
    if dbg:
        aps["_dbg"] = {
            "dtb": nc.dram_tensor("ddtb", [128, 512], F32, kind="ExternalOutput").ap(),
            "lgb": nc.dram_tensor("dlgb", [128, 512], F32, kind="ExternalOutput").ap(),
            "Bfm": nc.dram_tensor("dBfm", [128, 2048], BF16, kind="ExternalOutput").ap(),
            "Cfm": nc.dram_tensor("dCfm", [128, 2048], BF16, kind="ExternalOutput").ap(),
            "xh0": nc.dram_tensor("dxh0", [128, 2048], BF16, kind="ExternalOutput").ap(),
            "sz0": nc.dram_tensor("dsz0", [128, 2048], BF16, kind="ExternalOutput").ap(),
            "y0": nc.dram_tensor("dy0", [128, 2048], BF16, kind="ExternalOutput").ap(),
            "btm0": nc.dram_tensor("dbtm0", [128, 32], F32, kind="ExternalOutput").ap(),
            "CBm0": nc.dram_tensor("dCBm0", [128, 128], F32, kind="ExternalOutput").ap(),
            "S0": nc.dram_tensor("dS0", [128, 2048], F32, kind="ExternalOutput").ap(),
            "xcf0": nc.dram_tensor("dxcf0", [128, 2048], F32, kind="ExternalOutput").ap(),
            "acc0": nc.dram_tensor("dacc0", [128, 2048], F32, kind="ExternalOutput").ap(),
            "xcb0": nc.dram_tensor("dxcb0", [128, 2048], BF16, kind="ExternalOutput").ap(),
            "cwbE": nc.dram_tensor("dcwbE", [128, 64], F32, kind="ExternalOutput").ap(),
            "cwbL": nc.dram_tensor("dcwbL", [128, 64], F32, kind="ExternalOutput").ap(),
        }
    y = nc.dram_tensor("y", [T, D], F32, kind="ExternalOutput").ap()
    with ExitStack() as stack:
        p = Prog(nc, stack)
        c = Ctx(p)
        emit_identity(p, c)
        emit_load_x(p, c, x)
        emit_ln_stats_only(p, c)
        emit_l0attnb(p, c, aps)
        emit_store_xs(p, c, y)
        with nc.Block() as block:
            p.emit(block)
    return nc


def rope_tables():
    s, half = T, D // 2
    theta = 1.0 / 10000.0 ** (np.arange(half, dtype=np.float64) / half)
    ang = np.arange(s, dtype=np.float64)[:, None] * theta[None, :]
    return (np.cos(ang).astype(np.float32), np.sin(ang).astype(np.float32))


def build_L0_attn_a():
    nc = bass.Bass("TRN2", target_bir_lowering=False)
    x = nc.dram_tensor("x", [T, D], F32, kind="ExternalInput").ap()
    aps = {
        "W_q": nc.dram_tensor("W_q", [D, D], F32, kind="ExternalInput").ap(),
        "W_k": nc.dram_tensor("W_k", [256, D], F32, kind="ExternalInput").ap(),
        "W_v": nc.dram_tensor("W_v", [256, D], F32, kind="ExternalInput").ap(),
        "W_o": nc.dram_tensor("W_o", [D, D], F32, kind="ExternalInput").ap(),
        "rope_cos": nc.dram_tensor("rope_cos", [T, 512], F32, kind="ExternalInput").ap(),
        "rope_sin": nc.dram_tensor("rope_sin", [T, 512], F32, kind="ExternalInput").ap(),
    }
    y = nc.dram_tensor("y", [T, D], F32, kind="ExternalOutput").ap()
    with ExitStack() as stack:
        p = Prog(nc, stack)
        c = Ctx(p)
        emit_identity(p, c)
        emit_load_x(p, c, x)
        emit_ln_stats_only(p, c)
        emit_l0attna(p, c, aps)
        emit_store_xs(p, c, y)
        with nc.Block() as block:
            p.emit(block)
    return nc


def build_L0_ffn():
    nc = bass.Bass("TRN2", target_bir_lowering=False)
    x = nc.dram_tensor("x", [T, D], F32, kind="ExternalInput").ap()
    aps = {
        "w_shared_in": nc.dram_tensor("w_shared_in", [D, 2048], F32, kind="ExternalInput").ap(),
        "w_shared_out": nc.dram_tensor("w_shared_out", [1024, D], F32, kind="ExternalInput").ap(),
        "w1_shared": nc.dram_tensor("w1_shared", [D, 2048], F32, kind="ExternalInput").ap(),
        "w2": nc.dram_tensor("w2", [8, 1024, D], F32, kind="ExternalInput").ap(),
        "w_gate": nc.dram_tensor("w_gate", [D, 8], F32, kind="ExternalInput").ap(),
        "gate_bias": nc.dram_tensor("gate_bias", [8], F32, kind="ExternalInput").ap(),
    }
    y = nc.dram_tensor("y", [T, D], F32, kind="ExternalOutput").ap()

    with ExitStack() as stack:
        p = Prog(nc, stack)
        c = Ctx(p)
        emit_identity(p, c)
        emit_load_x(p, c, x)
        emit_ln_to_fm(p, c)
        emit_l0ffn_full(p, c, aps)
        emit_store_xs(p, c, y)
        with nc.Block() as block:
            p.emit(block)
    return nc


def build_fused():
    nc = bass.Bass("TRN2", target_bir_lowering=False)
    x = nc.dram_tensor("x", [T, D], F32, kind="ExternalInput").ap()
    y = nc.dram_tensor("y", [T, D], F32, kind="ExternalOutput").ap()

    def dr(pref, name, shape):
        return nc.dram_tensor(pref + name, shape, F32, kind="ExternalInput").ap()

    a_l0a = {n: dr("l0a_", n, s) for n, s in [
        ("W_q", [D, D]), ("W_k", [256, D]), ("W_v", [256, D]), ("W_o", [D, D]),
        ("rope_cos", [T, 512]), ("rope_sin", [T, 512])]}
    a_l0f = {n: dr("l0f_", n, s) for n, s in [
        ("w_shared_in", [D, 2048]), ("w_shared_out", [1024, D]),
        ("w1_shared", [D, 2048]), ("w2", [8, 1024, D]), ("w_gate", [D, 8]),
        ("gate_bias", [8])]}
    a_l0b = {n: dr("l0b_", n, s) for n, s in [
        ("norm_w", [D]), ("in_proj_w", [4384, D]), ("conv_w", [2048, 4]),
        ("conv_b", [2048]), ("A_log", [32]), ("D", [32]), ("dt_bias", [32]),
        ("out_proj_w", [D, 2048])]}
    a_l1a = {n: dr("l1a_", n, s) for n, s in [
        ("wq1", [16, D, 64]), ("wk1", [16, D, 64]), ("wq2", [16, D, 64]),
        ("wk2", [16, D, 64]), ("wv", [16, D, 64]), ("ln_gamma", [D]),
        ("ln_beta", [D]), ("lamb", [1])]}
    a_l1f = {n: dr("l1f_", n, s) for n, s in [
        ("gamma1", [D]), ("Wq", [D, D]), ("bq", [D]), ("Wdown", [D, 64]),
        ("bdown", [64]), ("Wup", [64, 2048]), ("bup", [2048]), ("Wo", [D, D]),
        ("bo", [D]), ("gamma2", [D]), ("Wg", [D, 8]), ("bg", [8]),
        ("We", [8, D, D])]}
    a_l1b = {n: dr("l1b_", n, s) for n, s in [
        ("Wq", [D, D]), ("bq", [D]), ("Wk", [D, D]), ("bk", [D]),
        ("Wv", [D, D]), ("bv", [D]), ("Wg", [D, D]), ("bg", [D]),
        ("Wo", [D, D]), ("bo", [D])]}

    with ExitStack() as stack:
        p = Prog(nc, stack)
        p.pooling = True
        p.arena_bytes = 105216
        c = Ctx(p)
        emit_identity(p, c)
        emit_load_x(p, c, x)
        p.begin_scope()
        emit_ln_stats_only(p, c)
        emit_l0attna(p, c, a_l0a)
        p.begin_scope()
        emit_ln_to_fm(p, c)
        emit_l0ffn_full(p, c, a_l0f)
        p.begin_scope()
        emit_ln_stats_only(p, c)
        emit_l0attnb(p, c, a_l0b)
        p.begin_scope()
        emit_ln_to_fm(p, c)
        emit_l0ffn_full(p, c, a_l0f)
        p.begin_scope()
        emit_ln_to_fm(p, c)
        emit_l1attna(p, c, a_l1a)
        p.begin_scope()
        emit_ln_stats_only(p, c)
        emit_l1ffn(p, c, a_l1f)
        p.begin_scope()
        emit_ln_to_fm(p, c)
        emit_l1attnb(p, c, a_l1b)
        p.begin_scope()
        emit_ln_stats_only(p, c)
        emit_l1ffn(p, c, a_l1f)
        emit_store_xs(p, c, y)
        with nc.Block() as block:
            p.emit(block)
    return nc


_CACHE = {}


def _fused_prog():
    if "f" not in _CACHE:
        _CACHE["f"] = build_fused()
        _CACHE["rope"] = rope_tables()
    return _CACHE["f"]


_IN_MAP = [
    ("l0a_", [("W_q", "L0_attn_a_W_q"), ("W_k", "L0_attn_a_W_k"),
              ("W_v", "L0_attn_a_W_v"), ("W_o", "L0_attn_a_W_o")]),
    ("l0f_", [("w_shared_in", "L0_ffn_w_shared_in"), ("w_shared_out", "L0_ffn_w_shared_out"),
              ("w1_shared", "L0_ffn_w1_shared"), ("w2", "L0_ffn_w2"),
              ("w_gate", "L0_ffn_w_gate"), ("gate_bias", "L0_ffn_gate_bias")]),
    ("l0b_", [("norm_w", "L0_attn_b_norm_w"), ("in_proj_w", "L0_attn_b_in_proj_w"),
              ("conv_w", "L0_attn_b_conv_w"), ("conv_b", "L0_attn_b_conv_b"),
              ("A_log", "L0_attn_b_A_log"), ("D", "L0_attn_b_D"),
              ("dt_bias", "L0_attn_b_dt_bias"), ("out_proj_w", "L0_attn_b_out_proj_w")]),
    ("l1a_", [("wq1", "L1_attn_a_wq1"), ("wk1", "L1_attn_a_wk1"),
              ("wq2", "L1_attn_a_wq2"), ("wk2", "L1_attn_a_wk2"), ("wv", "L1_attn_a_wv"),
              ("ln_gamma", "L1_attn_a_ln_gamma"), ("ln_beta", "L1_attn_a_ln_beta")]),
    ("l1f_", [("gamma1", "L1_ffn_gamma1"), ("Wq", "L1_ffn_Wq"), ("bq", "L1_ffn_bq"),
              ("Wdown", "L1_ffn_Wdown"), ("bdown", "L1_ffn_bdown"),
              ("Wup", "L1_ffn_Wup"), ("bup", "L1_ffn_bup"), ("Wo", "L1_ffn_Wo"),
              ("bo", "L1_ffn_bo"), ("gamma2", "L1_ffn_gamma2"), ("Wg", "L1_ffn_Wg"),
              ("bg", "L1_ffn_bg"), ("We", "L1_ffn_We")]),
    ("l1b_", [("Wq", "L1_attn_b_Wq"), ("bq", "L1_attn_b_bq"), ("Wk", "L1_attn_b_Wk"),
              ("bk", "L1_attn_b_bk"), ("Wv", "L1_attn_b_Wv"), ("bv", "L1_attn_b_bv"),
              ("Wg", "L1_attn_b_Wg"), ("bg", "L1_attn_b_bg"), ("Wo", "L1_attn_b_Wo"),
              ("bo", "L1_attn_b_bo")]),
]


def kernel(**z):
    nc = _fused_prog()
    cos, sin = _CACHE["rope"]
    x = np.ascontiguousarray(np.asarray(z["L0_attn_a_X"], np.float32))
    common = {}
    for pref, keys in _IN_MAP:
        for a, b in keys:
            common[pref + a] = np.ascontiguousarray(np.asarray(z[b], np.float32))
    common["l0a_rope_cos"] = cos
    common["l0a_rope_sin"] = sin
    common["l1a_lamb"] = np.asarray(z["L1_attn_a_lamb"], np.float32).reshape(1)
    in_maps = [dict(common, x=np.ascontiguousarray(x[i])) for i in range(NCORES)]
    res = run_bass_kernel_spmd(nc, in_maps, core_ids=list(range(NCORES)))
    return np.stack([np.asarray(res.results[i]["y"]) for i in range(NCORES)]).astype(np.float32)



def _progs():
    if "p" not in _CACHE:
        _CACHE["p"] = {
            "l0a": build_L0_attn_a(),
            "l0f": build_L0_ffn(),
            "l0b": build_L0_attn_b(),
            "l1a": build_L1_attn_a(),
            "l1f": build_L1_ffn(),
            "l1b": build_L1_attn_b(),
        }
        _CACHE["rope"] = rope_tables()
    return _CACHE["p"]


# version 07
# speedup vs baseline: 1.1465x; 1.1465x over previous
import math
from contextlib import ExitStack

import numpy as np
import ml_dtypes

import concourse.bass as bass
import concourse.mybir as mybir
from concourse.bass_utils import run_bass_kernel_spmd

F32 = mybir.dt.float32
BF16 = mybir.dt.bfloat16
AF = mybir.ActivationFunctionType
ALU = mybir.AluOpType
AX = mybir.AxisListType

T = 2048
D = 1024
TT = T // 128
DC = D // 128
NCORES = 8
BF = ml_dtypes.bfloat16

ENGINES = ("tensor", "vector", "scalar", "gpsimd", "sync")


class _Op:
    __slots__ = ("eng", "fn", "reads", "writes", "late", "kind", "dsem",
                 "dtarget", "waits", "tick", "idx")

    def __init__(self, eng, fn, reads, writes, late, kind, dsem):
        self.eng = eng
        self.fn = fn
        self.reads = reads
        self.writes = writes
        self.late = late
        self.kind = kind
        self.dsem = dsem
        self.dtarget = None
        self.waits = []
        self.tick = False
        self.idx = -1


def _overlap(a, b):
    return a[0] == b[0] and a[1] < b[2] and b[1] < a[2]


class TBuf:

    def __init__(self, ap, name, bpe, off=0):
        self._ap = ap
        self.name = name
        self.bpe = bpe
        self.off = off

    def __getitem__(self, idx):
        return self._ap[idx]

    def ap(self):
        return self._ap

    @property
    def shape(self):
        return self._ap.shape


def _bpe(t):
    if hasattr(t, "bpe"):
        return t.bpe
    d = getattr(t, "dtype", None)
    return 2 if d == BF16 else 4


class Prog:

    def __init__(self, nc, stack):
        self.nc = nc
        self.stack = stack
        self.ops = []
        self.last_w = {}
        self.readers = {}
        self.dgroups = {}
        self.psum = [stack.enter_context(nc.psum_tensor(f"pb{i}", [128, 512], F32))
                     for i in range(6)]
        self.psum += [stack.enter_context(nc.psum_tensor(f"pb{i}", [128, 1024], BF16))
                      for i in (6, 7)]
        self._sb = {}
        self._esem = {}

    GLOBALS = ("xs", "x_fm", "stats", "lnbuf", "ident", "identb", "lns_sq")

    def begin_scope(self):
        self._scope_cache = {}
        self._cursor = 0

    def sb(self, name, shape, dtype):
        if not getattr(self, "pooling", False) or name in self.GLOBALS:
            if name not in self._sb:
                self._sb[name] = self.stack.enter_context(
                    self.nc.sbuf_tensor(name, shape, dtype))
            return self._sb[name]
        if not hasattr(self, "_scope_cache"):
            self._scope_cache = {}
            self._cursor = 0
        if not hasattr(self, "_arena_hwm"):
            self._arena_hwm = 0
        if name in self._scope_cache:
            return self._scope_cache[name]
        bpe = 2 if dtype == BF16 else 4
        nbytes = -(-shape[1] * bpe // 512) * 512
        if not hasattr(self, "_arena"):
            self._arena = self.stack.enter_context(
                self.nc.sbuf_tensor("arena", [128, self.arena_bytes // 2], BF16))
        off = self._cursor
        self._cursor += nbytes
        self._arena_hwm = max(self._arena_hwm, self._cursor)
        assert self._cursor <= self.arena_bytes, (
            f"arena overflow at {name}: {self._cursor} > {self.arena_bytes}")
        base = self._arena.ap()
        if dtype == F32:
            base = base.bitcast(F32)
            view = base[0:shape[0], off // 4: off // 4 + shape[1]]
        else:
            view = base[0:shape[0], off // 2: off // 2 + shape[1]]
        tb = TBuf(view, "arena", bpe, off)
        self._scope_cache[name] = tb
        return tb

    def pr(self, i):
        return (f"@psum{i}", 0, 512)

    def dgroup(self, name):
        if name not in self.dgroups:
            h = self.stack.enter_context(self.nc.semaphore("dg_" + name))
            self.dgroups[name] = {"sem": h, "count": 0}
        return self.dgroups[name]

    def dma2(self, queue, fn, reads=(), writes=(), group="dflt"):
        g0 = self.dgroup(group + "#0")
        g1 = self.dgroup(group + "#1")
        n = g0["count"] + g1["count"]
        return self.dma(queue, fn, reads, writes, group=group + "#" + str(n % 2))

    def add(self, eng, fn, reads=(), writes=(), late=False):
        op = _Op(eng, fn, list(reads), list(writes), late, "c", None)
        self._resolve(op)
        self.ops.append(op)
        return op

    def dma(self, queue, fn, reads=(), writes=(), group="dflt"):
        g = self.dgroup(group)
        op = _Op(queue, fn, list(reads), list(writes), True, "d", group)
        self._resolve(op)
        g["count"] += 1
        op.dtarget = 16 * g["count"]
        self.ops.append(op)
        return op

    def _dep(self, op, prod):
        if prod is op:
            return
        if prod.kind == "d":
            op.waits.append(("D" + prod.dsem, prod.dtarget))
        else:
            if prod.eng == op.eng and op.kind == "c" and (
                    not prod.late or prod.eng == "tensor"):
                return
            prod.tick = True
            op.waits.append(("E" + prod.eng, prod))

    def _resolve(self, op):
        for r in op.reads:
            for reg, w in self.last_w.get(r[0], ()):
                if _overlap(r, reg):
                    self._dep(op, w)
        for wreg in op.writes:
            for reg, rd in self.readers.get(wreg[0], ()):
                if rd is op:
                    continue
                if _overlap(wreg, reg):
                    if rd.kind == "d":
                        op.waits.append(("D" + rd.dsem, rd.dtarget))
                    elif (rd.eng != op.eng or rd.late or op.kind == "d") and not (
                            rd.eng == op.eng == "tensor" and op.kind == "c"):
                        rd.tick = True
                        op.waits.append(("E" + rd.eng, rd))
            for reg, w in self.last_w.get(wreg[0], ()):
                if w is not op and _overlap(wreg, reg):
                    self._dep(op, w)
        for r in op.reads:
            self.readers.setdefault(r[0], []).append((r, op))
        for wreg in op.writes:
            self.last_w.setdefault(wreg[0], []).append((wreg, op))
            lst = self.last_w[wreg[0]]
            if len(lst) > 8:
                same = [e for e in lst[:-1] if e[0] == wreg]
                if same:
                    self.last_w[wreg[0]] = [e for e in lst if e[0] != wreg] + [(wreg, op)]
            self.readers[wreg[0]] = [e for e in self.readers.get(wreg[0], ())
                                     if not (_overlap(wreg, e[0]) and
                                             e[0][1] >= wreg[1] and e[0][2] <= wreg[2])
                                     or e[1] is op]

    def emit(self, block):
        counters = {e: 0 for e in ENGINES}
        tickval = {}
        for op in self.ops:
            if op.kind == "c" and op.tick:
                counters[op.eng] += 1
                tickval[id(op)] = counters[op.eng]
        esems = {}
        for e in ENGINES:
            if counters[e] > 0:
                esems[e] = self.stack.enter_context(self.nc.semaphore("es_" + e))

        byeng = {e: [] for e in ENGINES}
        for op in self.ops:
            byeng[op.eng].append(op)

        for engname in ENGINES:
            ops = byeng[engname]
            if not ops:
                continue

            def mk(engname, ops):
                def body(e):
                    cur = {}
                    for op in ops:
                        for key, val in op.waits:
                            if key.startswith("E"):
                                v = tickval[id(val)]
                                h = esems[key[1:]]
                            else:
                                v = val
                                h = self.dgroups[key[1:]]["sem"]
                            if cur.get(key, -1) < v:
                                e.wait_ge(h, v)
                                cur[key] = v
                        ins = op.fn(e)
                        if op.kind == "d":
                            ins.then_inc(self.dgroups[op.dsem]["sem"], 16)
                        elif op.tick:
                            ins.then_inc(esems[op.eng], 1)
                return body
            getattr(block, engname)(mk(engname, ops))


def rg(t, lo, hi):
    o = getattr(t, "off", 0)
    return (sbname(t), o + lo * _bpe(t), o + hi * _bpe(t))



def sbname(t):
    return t.name if hasattr(t, "name") else str(t)


class Ctx:

    def __init__(self, p):
        self.p = p
        self.ident = p.sb("ident", [128, 128], F32)
        self.identb = p.sb("identb", [128, 128], BF16)
        self.xs = p.sb("xs", [128, TT * D], F32)
        self.stats = p.sb("stats", [128, 4 * TT], F32)
        self.lnbuf = p.sb("lnbuf", [128, D], F32)

    @property
    def x_fm(self):
        return self.p.sb("x_fm", [128, DC * T], BF16)

    def R(self, t, lo=None, hi=None):
        n = sbname(t)
        o = getattr(t, "off", 0)
        if lo is None:
            if o:
                sh = t.shape
                return (n, o, o + sh[0] * 0 + sh[1] * _bpe(t)) if False else (n, o, o + t._ap.shape[-1] * _bpe(t))
            return (n, 0, 10 ** 9)
        b = _bpe(t)
        return (n, o + lo * b, o + hi * b)


def emit_identity(p, c):
    ident = c.ident
    p.add("gpsimd", lambda e: e.memset(ident[:, :], 0.0),
          writes=[c.R(ident)])
    p.add("gpsimd", lambda e: e.affine_select(
        out=ident[:, :], in_=ident[:, :], compare_op=ALU.not_equal, fill=1.0,
        base=0, pattern=[[-1, 128]], channel_multiplier=1),
        reads=[c.R(ident)], writes=[c.R(ident)], late=True)
    identb = c.identb
    p.add("gpsimd", lambda e: e.memset(identb[:, :], 0.0), writes=[c.R(identb)])
    p.add("gpsimd", lambda e: e.affine_select(
        out=identb[:, :], in_=identb[:, :], compare_op=ALU.not_equal, fill=1.0,
        base=0, pattern=[[-1, 128]], channel_multiplier=1),
        reads=[c.R(identb)], writes=[c.R(identb)], late=True)


def emit_load_x(p, c, x_hbm):
    xs = c.xs
    p.dma("sync", lambda e: e.dma_start(
        out=xs.ap().rearrange("p (t d) -> p t d", t=TT),
        in_=x_hbm.rearrange("(t p) d -> p t d", p=128)),
        writes=[c.R(xs)], group="ldx")


def emit_ln_to_fm(p, c, pref="ln", eps=1e-5, also_tm_f32=None):
    xs, stats, lnbuf, x_fm = c.xs, c.stats, c.lnbuf, c.x_fm
    nt = TT
    emit_ln_stats_only(p, c, eps)

    for t in range(nt):
        dst = lnbuf[:, 0:D]
        dreg = c.R(lnbuf, 0, D)
        p.add("scalar", lambda e, t=t, dst=dst: e.activation(
            out=dst, in_=xs[:, t * D:(t + 1) * D], func=AF.Identity,
            scale=stats[:, t:t + 1], bias=stats[:, nt + t:nt + t + 1]),
            reads=[c.R(xs, t * D, (t + 1) * D), c.R(stats, 0, 2 * nt)],
            writes=[dreg], late=True)
        if also_tm_f32 is not None:
            p.add("scalar", lambda e, t=t: e.activation(
                out=also_tm_f32[:, t * D:(t + 1) * D], in_=xs[:, t * D:(t + 1) * D],
                func=AF.Identity, scale=stats[:, t:t + 1], bias=stats[:, nt + t:nt + t + 1]),
                reads=[c.R(xs, t * D, (t + 1) * D), c.R(stats, 0, 2 * nt)],
                writes=[c.R(also_tm_f32, t * D, (t + 1) * D)])
        emit_tm_tile_to_fm(p, c, dst, dreg, x_fm, t, pref=pref)


def emit_tm_tile_to_fm(p, c, src, sreg, dst_fm, t, pref="tr", ncols=D,
                       fm_T=T, dst_late=False):
    nchunk = ncols // 128
    for h in range(0, nchunk, 4):
        n4 = min(4, nchunk - h)
        bank = p.psum[(h // 4) % 2]

        def tp(e, h=h, n4=n4, bank=bank):
            for j in range(n4):
                ins = e.transpose(bank[:, j * 128:(j + 1) * 128],
                                  src[:, (h + j) * 128:(h + j + 1) * 128],
                                  c.ident[:, :])
            return ins
        p.add("tensor", tp, reads=[sreg, c.R(c.ident)],
              writes=[p.pr((h // 4) % 2)], late=True)
        dfm = dst_fm.ap().rearrange("p (cc t) -> p cc t", t=fm_T)[
            :, h:h + n4, t * 128:(t + 1) * 128]
        p.add("vector", lambda e, bank=bank, dfm=dfm, n4=n4: e.tensor_copy(
            dfm, bank[:, 0:n4 * 128].rearrange("p (cc t) -> p cc t", t=128)),
            reads=[p.pr((h // 4) % 2)],
            writes=[rg(dst_fm, h * fm_T, (h + n4) * fm_T)], late=dst_late)


def load_w_inout(p, c, w_hbm, name, In, Out, queue="gpsimd", group=None):
    wb = p.sb(name, [128, (In // 128) * Out], BF16)
    p.dma(queue, lambda e: e.dma_start(
        out=wb.ap().rearrange("p (cc o) -> p cc o", o=Out),
        in_=w_hbm.rearrange("(cc p) o -> p cc o", p=128)),
        writes=[c.R(wb)], group=group or name)
    return wb


def load_w_outin_T(p, c, w_hbm, name, Out, In, group=None):
    nat = p.sb(name + "_nat", [128, (Out // 128) * In], BF16)
    wb = p.sb(name, [128, (In // 128) * Out], BF16)
    g = group or name
    p.dma("gpsimd", lambda e: e.dma_start(
        out=nat.ap().rearrange("p (cc i) -> p cc i", i=In),
        in_=w_hbm.rearrange("(cc p) i -> p cc i", p=128)),
        writes=[c.R(nat)], group=g + "_nat")
    for oc in range(Out // 128):
        for ic in range(In // 128):
            p.dma("sync", lambda e, oc=oc, ic=ic: e.dma_start_transpose(
                wb[:, ic * Out + oc * 128: ic * Out + (oc + 1) * 128],
                nat[:, oc * In + ic * 128: oc * In + (ic + 1) * 128]),
                reads=[c.R(nat, oc * In + ic * 128, oc * In + (ic + 1) * 128)],
                writes=[c.R(wb, ic * Out + oc * 128, ic * Out + (oc + 1) * 128)],
                group=g)
    return wb


def emit_store_xs(p, c, y_hbm):
    xs = c.xs
    p.dma("sync", lambda e: e.dma_start(
        out=y_hbm.rearrange("(t p) d -> p t d", p=128),
        in_=xs.ap().rearrange("p (t d) -> p t d", t=TT)),
        reads=[c.R(xs)], group="sty")
    g = p.dgroup("sty")
    p.add("sync", lambda e: e.wait_ge(g["sem"], 16 * g["count"]))


def mm_accum(p, c, bank, lhsT_list, rhs_list, reads, n=512, start_clear=True):
    K = len(lhsT_list)

    def fn(e):
        for i in range(K):
            ins = e.matmul(p.psum[bank][:, 0:n], lhsT_list[i], rhs_list[i],
                           start=(start_clear and i == 0), stop=(i == K - 1))
        return ins
    p.add("tensor", fn, reads=list(reads), writes=[p.pr(bank)], late=True)



def emit_l0ffn(p, c, win, wso, w1b, w2stream, wgb, gbias):
    xs, x_fm = c.xs, c.x_fm
    hid_fm = p.sb("hid_fm", [128, 8 * T], BF16)
    hs_fm = p.sb("hs_fm", [128, 8 * 512], BF16)
    sa = p.sb("sa_silu", [128, 2 * 512], F32)
    gate = p.sb("gate", [128, TT * 16], F32)
    wdense = p.sb("wdense", [128, TT * 8], F32)

    xfm_r = [c.R(x_fm, ic * T, (ic + 1) * T) for ic in range(DC)]

    for t in range(TT):
        lhs = [x_fm[:, ic * T + t * 128: ic * T + (t + 1) * 128] for ic in range(DC)]
        rhs = [wgb[:, ic * 8:(ic + 1) * 8] for ic in range(DC)]
        bank = 5
        p.add("tensor", lambda e, lhs=lhs, rhs=rhs: [
            e.matmul(p.psum[5][:, 0:8], lhs[i], rhs[i], start=(i == 0), stop=(i == 7))
            for i in range(8)][-1],
            reads=xfm_r + [c.R(wgb)], writes=[p.pr(5)], late=True)
        g0 = gate[:, t * 16:t * 16 + 8]
        g1 = gate[:, t * 16 + 8:t * 16 + 16]
        wd = wdense[:, t * 8:(t + 1) * 8]
        g0r = c.R(gate, t * 16, t * 16 + 8)
        g1r = c.R(gate, t * 16 + 8, t * 16 + 16)
        wdr = c.R(wdense, t * 8, (t + 1) * 8)
        p.add("vector", lambda e, g0=g0: e.tensor_tensor(
            out=g0, in0=p.psum[5][:, 0:8], in1=gbias[:, :], op=ALU.add),
            reads=[p.pr(5), c.R(gbias)], writes=[g0r], late=True)
        p.add("scalar", lambda e, g0=g0, g1=g1: e.activation(
            out=g0, in_=g0, func=AF.Exp, accum_out=g1[:, 0:1]),
            reads=[g0r], writes=[g0r, g1r], late=True)
        p.add("vector", lambda e, g1=g1: e.reciprocal(out=g1[:, 1:2], in_=g1[:, 0:1]),
            reads=[g1r], writes=[g1r], late=True)
        p.add("vector", lambda e, g0=g0, g1=g1: e.tensor_scalar(
            out=g0, in0=g0, scalar1=g1[:, 1:2], scalar2=None, op0=ALU.mult),
            reads=[g0r, g1r], writes=[g0r], late=True)
        p.add("vector", lambda e, g0=g0, g1=g1: e.tensor_reduce(
            out=g1[:, 2:3], in_=g0, axis=AX.X, op=ALU.max),
            reads=[g0r], writes=[g1r], late=True)
        p.add("vector", lambda e, g0=g0, g1=g1, wd=wd: e.tensor_scalar(
            out=wd, in0=g0, scalar1=g1[:, 2:3], scalar2=None, op0=ALU.is_lt),
            reads=[g0r, g1r], writes=[wdr], late=True)
        p.add("vector", lambda e, g0=g0, wd=wd: e.tensor_tensor(
            out=wd, in0=wd, in1=g0, op=ALU.mult),
            reads=[g0r, wdr], writes=[wdr], late=True)
        p.add("vector", lambda e, g1=g1, wd=wd: e.tensor_reduce(
            out=g1[:, 3:4], in_=wd, axis=AX.X, op=ALU.max),
            reads=[wdr], writes=[g1r], late=True)
        p.add("vector", lambda e, g0=g0, g1=g1, wd=wd: e.tensor_scalar(
            out=wd, in0=g0, scalar1=g1[:, 3:4], scalar2=None, op0=ALU.is_ge),
            reads=[g0r, g1r], writes=[wdr], late=True)
        p.add("vector", lambda e, g0=g0, wd=wd: e.tensor_tensor(
            out=wd, in0=wd, in1=g0, op=ALU.mult),
            reads=[g0r, wdr], writes=[wdr], late=True)

    def swiglu_path(wsrc, out_fm, out_name, fmT):
        for tg in range(T // 512):
            for oc in range(8):
                ba, bb = (tg * 8 + oc) % 2, 2 + (tg * 8 + oc) % 2
                lr = c.R(wsrc, 0, 10 ** 9)

                def mma(e, tg=tg, oc=oc, ba=ba, wsrc=wsrc):
                    for ic in range(DC):
                        ins = e.matmul(p.psum[ba][:, :],
                                       wsrc[:, ic * 2048 + oc * 128: ic * 2048 + (oc + 1) * 128],
                                       x_fm[:, ic * T + tg * 512: ic * T + (tg + 1) * 512],
                                       start=(ic == 0), stop=(ic == DC - 1))
                    return ins

                def mmb(e, tg=tg, oc=oc, bb=bb, wsrc=wsrc):
                    for ic in range(DC):
                        ins = e.matmul(p.psum[bb][:, :],
                                       wsrc[:, ic * 2048 + (oc + 8) * 128: ic * 2048 + (oc + 9) * 128],
                                       x_fm[:, ic * T + tg * 512: ic * T + (tg + 1) * 512],
                                       start=(ic == 0), stop=(ic == DC - 1))
                    return ins
                p.add("tensor", mma, reads=xfm_r + [lr], writes=[p.pr(ba)], late=True)
                p.add("tensor", mmb, reads=xfm_r + [lr], writes=[p.pr(bb)], late=True)
                sslice = sa[:, (oc % 2) * 512:(oc % 2 + 1) * 512]
                ssreg = c.R(sa, (oc % 2) * 512, (oc % 2 + 1) * 512)
                p.add("scalar", lambda e, ba=ba, sslice=sslice: e.activation(
                    out=sslice, in_=p.psum[ba][:, :], func=AF.Silu),
                    reads=[p.pr(ba)], writes=[ssreg], late=True)
                dst = out_fm.ap().rearrange("p (cc t) -> p cc t", t=fmT)[
                    :, oc, tg * 512:(tg + 1) * 512]
                p.add("vector", lambda e, bb=bb, sslice=sslice, dst=dst: e.tensor_tensor(
                    out=dst, in0=p.psum[bb][:, :], in1=sslice, op=ALU.mult),
                    reads=[p.pr(bb), ssreg],
                    writes=[(out_name, oc * fmT + tg * 512, oc * fmT + (tg + 1) * 512)],
                    late=True)

    for tg in range(T // 512):
        for oc in range(8):
            ba, bb = oc % 2, 2 + oc % 2

            def mma(e, tg=tg, oc=oc, ba=ba):
                for ic in range(DC):
                    ins = e.matmul(p.psum[ba][:, :],
                                   win[:, ic * 2048 + oc * 128: ic * 2048 + (oc + 1) * 128],
                                   x_fm[:, ic * T + tg * 512: ic * T + (tg + 1) * 512],
                                   start=(ic == 0), stop=(ic == DC - 1))
                return ins

            def mmb(e, tg=tg, oc=oc, bb=bb):
                for ic in range(DC):
                    ins = e.matmul(p.psum[bb][:, :],
                                   win[:, ic * 2048 + (oc + 8) * 128: ic * 2048 + (oc + 9) * 128],
                                   x_fm[:, ic * T + tg * 512: ic * T + (tg + 1) * 512],
                                   start=(ic == 0), stop=(ic == DC - 1))
                return ins
            p.add("tensor", mma, reads=xfm_r + [c.R(win)], writes=[p.pr(ba)], late=True)
            p.add("tensor", mmb, reads=xfm_r + [c.R(win)], writes=[p.pr(bb)], late=True)
            sslice = sa[:, (oc % 2) * 512:(oc % 2 + 1) * 512]
            ssreg = c.R(sa, (oc % 2) * 512, (oc % 2 + 1) * 512)
            p.add("scalar", lambda e, ba=ba, sslice=sslice: e.activation(
                out=sslice, in_=p.psum[ba][:, :], func=AF.Silu),
                reads=[p.pr(ba)], writes=[ssreg], late=True)
            dst = hs_fm[:, oc * 512:(oc + 1) * 512]
            p.add("vector", lambda e, bb=bb, sslice=sslice, dst=dst: e.tensor_tensor(
                out=dst, in0=p.psum[bb][:, :], in1=sslice, op=ALU.mult),
                reads=[p.pr(bb), ssreg], writes=[c.R(hs_fm, oc * 512, (oc + 1) * 512)],
                late=True)
        for tt4 in range(4):
            t128 = tg * 4 + tt4
            for oh in range(2):
                bank = 4 + oh

                def mmo(e, tt4=tt4, oh=oh, bank=bank):
                    for hc in range(8):
                        ins = e.matmul(p.psum[bank][:, :],
                                       hs_fm[:, hc * 512 + tt4 * 128: hc * 512 + (tt4 + 1) * 128],
                                       wso[:, hc * D + oh * 512: hc * D + (oh + 1) * 512],
                                       start=(hc == 0), stop=(hc == 7))
                    return ins
                p.add("tensor", mmo, reads=[c.R(hs_fm), c.R(wso)],
                      writes=[p.pr(bank)], late=True)
                dst = xs[:, t128 * D + oh * 512: t128 * D + (oh + 1) * 512]
                xreg = c.R(xs, t128 * D + oh * 512, t128 * D + (oh + 1) * 512)
                p.add("vector", lambda e, bank=bank, dst=dst: e.tensor_tensor(
                    out=dst, in0=p.psum[bank][:, :], in1=dst, op=ALU.add),
                    reads=[p.pr(bank), xreg], writes=[xreg], late=True)

    w1b = w1b()
    swiglu_path(w1b, hid_fm, sbname(hid_fm), T)

    for ei in range(8):
        wslice, wreg = w2stream(ei)
        for t in range(TT):
            for oh in range(2):
                bank = 2 + ((t * 2 + oh) % 4)

                def mme(e, t=t, oh=oh, bank=bank, wslice=wslice):
                    for hc in range(8):
                        ins = e.matmul(p.psum[bank][:, :],
                                       hid_fm[:, hc * T + t * 128: hc * T + (t + 1) * 128],
                                       wslice(hc, oh),
                                       start=(hc == 0), stop=(hc == 7))
                    return ins
                p.add("tensor", mme, reads=[c.R(hid_fm), wreg], writes=[p.pr(bank)],
                      late=True)
                dst = xs[:, t * D + oh * 512: t * D + (oh + 1) * 512]
                xreg = c.R(xs, t * D + oh * 512, t * D + (oh + 1) * 512)
                wdcol = wdense[:, t * 8 + ei:t * 8 + ei + 1]
                p.add("vector", lambda e, bank=bank, dst=dst, wdcol=wdcol:
                      e.scalar_tensor_tensor(out=dst, in0=p.psum[bank][:, :],
                                             scalar=wdcol, in1=dst,
                                             op0=ALU.mult, op1=ALU.add),
                      reads=[p.pr(bank), xreg, c.R(wdense, t * 8 + ei, t * 8 + ei + 1)],
                      writes=[xreg], late=True)


def emit_l0ffn_full(p, c, aps, uid=""):
    wgb = load_w_inout(p, c, aps["w_gate"], "wgb" + uid, D, 8)
    gbias = p.sb("gbias" + uid, [128, 8], F32)
    p.dma("gpsimd", lambda e: e.dma_start(
        out=gbias[:, :], in_=aps["gate_bias"][None, :].broadcast_to((128, 8))),
        writes=[c.R(gbias)], group="gbias" + uid)
    wfused = load_w_inout(p, c, aps["w_shared_in"], "wfused" + uid, D, 2048,
                          group="wfused_a" + uid)
    wso = load_w_inout(p, c, aps["w_shared_out"], "wso" + uid, 1024, D)

    def reload_w1():
        p.dma("gpsimd", lambda e: e.dma_start(
            out=wfused.ap().rearrange("p (cc o) -> p cc o", o=2048),
            in_=aps["w1_shared"].rearrange("(cc p) o -> p cc o", p=128)),
            writes=[c.R(wfused)], group="wfused_b" + uid)
        return wfused

    half = 8 * D

    def w2stream(ei):
        off = (ei % 2) * half
        p.dma("gpsimd", lambda e, ei=ei, off=off: e.dma_start(
            out=wfused.ap()[:, off:off + half].rearrange("p (cc o) -> p cc o", o=D),
            in_=aps["w2"][ei].rearrange("(cc p) o -> p cc o", p=128)),
            writes=[c.R(wfused, off, off + half)], group=f"w2_{ei % 2}" + uid)
        return (lambda hc, oh, off=off: wfused[:, off + hc * D + oh * 512:
                                               off + hc * D + (oh + 1) * 512],
                c.R(wfused, off, off + half))

    emit_l0ffn(p, c, wfused, wso, reload_w1, w2stream, wgb, gbias)



def emit_ln_stats_only(p, c, eps=1e-5, sqbuf=None):
    xs, stats = c.xs, c.stats
    nt = TT
    sq = sqbuf if sqbuf is not None else p.sb("lns_sq", [128, D], F32)
    for t in range(nt):
        p.add("scalar", lambda e, t=t: e.activation(
            out=sq[:, :], in_=xs[:, t * D:(t + 1) * D], func=AF.Square,
            accum_out=stats[:, 2 * nt + t:2 * nt + t + 1]),
            reads=[c.R(xs, t * D, (t + 1) * D)],
            writes=[c.R(sq), c.R(stats, 2 * nt + t, 2 * nt + t + 1)], late=True)
        p.add("vector", lambda e, t=t: e.tensor_reduce(
            out=stats[:, 3 * nt + t:3 * nt + t + 1], in_=xs[:, t * D:(t + 1) * D],
            axis=AX.X, op=ALU.add),
            reads=[c.R(xs, t * D, (t + 1) * D)],
            writes=[c.R(stats, 3 * nt + t, 3 * nt + t + 1)], late=True)
    p.add("vector", lambda e: e.tensor_scalar(
        out=stats[:, 2 * nt:3 * nt], in0=stats[:, 2 * nt:3 * nt],
        scalar1=1.0 / D, scalar2=None, op0=ALU.mult),
        reads=[c.R(stats, 2 * nt, 3 * nt)], writes=[c.R(stats, 2 * nt, 3 * nt)], late=True)
    p.add("vector", lambda e: e.tensor_scalar(
        out=stats[:, 3 * nt:4 * nt], in0=stats[:, 3 * nt:4 * nt],
        scalar1=1.0 / D, scalar2=None, op0=ALU.mult),
        reads=[c.R(stats, 3 * nt, 4 * nt)], writes=[c.R(stats, 3 * nt, 4 * nt)], late=True)
    p.add("vector", lambda e: e.tensor_tensor(
        out=stats[:, nt:2 * nt], in0=stats[:, 3 * nt:4 * nt],
        in1=stats[:, 3 * nt:4 * nt], op=ALU.mult),
        reads=[c.R(stats, 3 * nt, 4 * nt)], writes=[c.R(stats, nt, 2 * nt)], late=True)
    p.add("vector", lambda e: e.tensor_tensor(
        out=stats[:, 2 * nt:3 * nt], in0=stats[:, 2 * nt:3 * nt],
        in1=stats[:, nt:2 * nt], op=ALU.subtract),
        reads=[c.R(stats, nt, 3 * nt)], writes=[c.R(stats, 2 * nt, 3 * nt)], late=True)
    p.add("vector", lambda e: e.tensor_scalar(
        out=stats[:, 2 * nt:3 * nt], in0=stats[:, 2 * nt:3 * nt],
        scalar1=eps, scalar2=None, op0=ALU.add),
        reads=[c.R(stats, 2 * nt, 3 * nt)], writes=[c.R(stats, 2 * nt, 3 * nt)], late=True)
    p.add("scalar", lambda e: e.activation(
        out=stats[:, 2 * nt:3 * nt], in_=stats[:, 2 * nt:3 * nt], func=AF.Ln),
        reads=[c.R(stats, 2 * nt, 3 * nt)], writes=[c.R(stats, 2 * nt, 3 * nt)], late=True)
    p.add("scalar", lambda e: e.activation(
        out=stats[:, 0:nt], in_=stats[:, 2 * nt:3 * nt], func=AF.Exp, scale=-0.5),
        reads=[c.R(stats, 2 * nt, 3 * nt)], writes=[c.R(stats, 0, nt)], late=True)
    p.add("vector", lambda e: e.tensor_tensor(
        out=stats[:, nt:2 * nt], in0=stats[:, 3 * nt:4 * nt],
        in1=stats[:, 0:nt], op=ALU.mult),
        reads=[c.R(stats, 3 * nt, 4 * nt), c.R(stats, 0, nt)],
        writes=[c.R(stats, nt, 2 * nt)], late=True)
    p.add("vector", lambda e: e.tensor_scalar(
        out=stats[:, nt:2 * nt], in0=stats[:, nt:2 * nt],
        scalar1=-1.0, scalar2=None, op0=ALU.mult),
        reads=[c.R(stats, nt, 2 * nt)], writes=[c.R(stats, nt, 2 * nt)], late=True)


def emit_l0attna(p, c, aps, uid=""):
    xs, stats, lnbuf, ident = c.xs, c.stats, c.lnbuf, c.ident
    NH, NG = 16, 4
    xr_fm = c.x_fm
    attn_fm = p.sb("attnfm" + uid, [128, DC * T], BF16)
    ropebuf = p.sb("ropebuf" + uid, [128, D], F32)
    ropetmp = p.sb("ropetmp" + uid, [128, 512], F32)
    cosb = p.sb("cosb" + uid, [128, 512], F32)
    sinb = p.sb("sinb" + uid, [128, 512], F32)
    qbuf = p.sb("qbuf" + uid, [64, 2 * T], BF16)
    kdup = p.sb("kdup" + uid, [64, 2 * T], BF16)
    vtm = p.sb("vtm" + uid, [128, TT * NG * 65], BF16)
    pst = p.sb("pst" + uid, [128, 4 * 512], BF16)
    zrow = p.sb("zrow" + uid, [1, 512], F32)
    zb = p.sb("zb" + uid, [64, 512], F32)
    onesc = p.sb("onesc" + uid, [1, 64], F32)
    maskT = p.sb("maskT" + uid, [128, 128], BF16)
    wnat = p.sb("wnat" + uid, [128, D], BF16)

    p.add("gpsimd", lambda e: e.memset(onesc[:, :], 1.0), writes=[c.R(onesc)])
    p.add("gpsimd", lambda e: e.memset(maskT[:, :], 1.0), writes=[c.R(maskT)])
    p.add("gpsimd", lambda e: e.affine_select(
        out=maskT[:, :], in_=maskT[:, :], compare_op=ALU.is_ge, fill=0.0,
        base=0, pattern=[[1, 128]], channel_multiplier=-1),
        reads=[c.R(maskT)], writes=[c.R(maskT)], late=True)
    p.add("gpsimd", lambda e: e.memset(
        vtm.ap().rearrange("p (t g o) -> p t g o", t=TT, g=NG)[:, :, :, 64:65], 1.0),
        writes=[c.R(vtm)])

    def wT(dram, name, Out):
        wb = p.sb(name, [128, DC * Out], BF16)
        for oc in range(Out // 128):
            nreg = c.R(wnat, 0, D)
            p.dma("gpsimd", lambda e, oc=oc: e.dma_start(
                out=wnat[:, 0:D], in_=dram[oc * 128:(oc + 1) * 128, :]),
                writes=[nreg], group=name + "_nat")
            for ic in range(DC):
                p.dma("sync", lambda e, oc=oc, ic=ic: e.dma_start_transpose(
                    wb[:, ic * Out + oc * 128: ic * Out + (oc + 1) * 128],
                    wnat[:, ic * 128:(ic + 1) * 128]),
                    reads=[rg(wnat, ic * 128, (ic + 1) * 128)],
                    writes=[c.R(wb, ic * Out + oc * 128, ic * Out + (oc + 1) * 128)],
                    group=name)
        return wb

    wqT = wT(aps["W_q"], "wqT" + uid, D)
    wkT = wT(aps["W_k"], "wkT" + uid, 256)
    wvT = wT(aps["W_v"], "wvT" + uid, 256)

    for t in range(TT):
        dst = lnbuf[:, 0:D]
        dreg = c.R(lnbuf, 0, D)
        p.add("scalar", lambda e, t=t, dst=dst: e.activation(
            out=dst, in_=xs[:, t * D:(t + 1) * D], func=AF.Identity,
            scale=stats[:, t:t + 1], bias=stats[:, TT + t:TT + t + 1]),
            reads=[c.R(xs, t * D, (t + 1) * D), c.R(stats, 0, 2 * TT)],
            writes=[dreg], late=True)
        cs = cosb[:, :]
        sn = sinb[:, :]
        csr = c.R(cosb, 0, 512)
        snr = c.R(sinb, 0, 512)
        p.dma2("sync", lambda e, t=t, cs=cs: e.dma_start(
            out=cs, in_=aps["rope_cos"][t * 128:(t + 1) * 128, :]),
            writes=[csr], group="ropec" + uid)
        p.dma2("sync", lambda e, t=t, sn=sn: e.dma_start(
            out=sn, in_=aps["rope_sin"][t * 128:(t + 1) * 128, :]),
            writes=[snr], group="ropes" + uid)
        lneo = dst.rearrange("p (k two) -> p two k", two=2)
        xe, xo = lneo[:, 0, :], lneo[:, 1, :]
        rb = ropebuf
        rreg = c.R(ropebuf, 0, D)
        p.add("vector", lambda e, xe=xe, cs=cs: e.tensor_tensor(
            out=ropebuf[:, 0:512], in0=xe, in1=cs, op=ALU.mult),
            reads=[dreg, csr], writes=[c.R(ropebuf, 0, 512)])
        p.add("vector", lambda e, xo=xo, sn=sn: e.tensor_tensor(
            out=ropetmp[:, 0:512], in0=xo, in1=sn, op=ALU.mult),
            reads=[dreg, snr], writes=[c.R(ropetmp, 0, 512)])
        p.add("vector", lambda e: e.tensor_tensor(
            out=ropebuf[:, 0:512], in0=ropebuf[:, 0:512], in1=ropetmp[:, 0:512],
            op=ALU.subtract),
            reads=[c.R(ropebuf, 0, 512), c.R(ropetmp, 0, 512)],
            writes=[c.R(ropebuf, 0, 512)])
        p.add("vector", lambda e, xo=xo, cs=cs: e.tensor_tensor(
            out=ropebuf[:, 512:1024], in0=xo, in1=cs, op=ALU.mult),
            reads=[dreg, csr], writes=[c.R(ropebuf, 512, 1024)])
        p.add("vector", lambda e, xe=xe, sn=sn: e.tensor_tensor(
            out=ropetmp[:, 0:512], in0=xe, in1=sn, op=ALU.mult),
            reads=[dreg, snr], writes=[c.R(ropetmp, 0, 512)])
        p.add("vector", lambda e: e.tensor_tensor(
            out=ropebuf[:, 512:1024], in0=ropebuf[:, 512:1024], in1=ropetmp[:, 0:512],
            op=ALU.add),
            reads=[c.R(ropebuf, 512, 1024), c.R(ropetmp, 0, 512)],
            writes=[c.R(ropebuf, 512, 1024)])
        emit_tm_tile_to_fm(p, c, ropebuf, rreg, xr_fm, t, pref="ra" + uid)

    xr_r = [c.R(xr_fm, ic * T, (ic + 1) * T) for ic in range(DC)]

    for t in range(TT):
        bank = t % 2

        def mmv(e, t=t, bank=bank):
            for ic in range(DC):
                ins = e.matmul(p.psum[bank][:, 0:256],
                               xr_fm[:, ic * T + t * 128: ic * T + (t + 1) * 128],
                               wvT[:, ic * 256:(ic + 1) * 256],
                               start=(ic == 0), stop=(ic == DC - 1))
            return ins
        p.add("tensor", mmv, reads=xr_r + [c.R(wvT)], writes=[p.pr(bank)], late=True)
        dst = vtm.ap().rearrange("p (t g o) -> p t g o", t=TT, g=NG)[:, t, :, 0:64]
        p.add("vector", lambda e, bank=bank, dst=dst: e.tensor_copy(
            dst, p.psum[bank].ap()[:, 0:256].rearrange("p (g o) -> p g o", g=NG)),
            reads=[p.pr(bank)],
            writes=[c.R(vtm, t * NG * 65, (t + 1) * NG * 65)], late=True)

    for h in range(NH):
        g = h // 4
        hp = h // 2
        if h % 8 == 0:
            oc = g // 2
            for tg in range(4):
                bank = tg % 2

                def mmk(e, oc=oc, tg=tg, bank=bank):
                    for ic in range(DC):
                        ins = e.matmul(p.psum[bank][:, :],
                                       wkT[:, ic * 256 + oc * 128: ic * 256 + (oc + 1) * 128],
                                       xr_fm[:, ic * T + tg * 512: ic * T + (tg + 1) * 512],
                                       start=(ic == 0), stop=(ic == DC - 1))
                    return ins
                p.add("tensor", mmk, reads=xr_r + [c.R(wkT)], writes=[p.pr(bank)], late=True)
                for half in range(2):
                    p.add("vector", lambda e, half=half, tg=tg, bank=bank: e.tensor_copy(
                        kdup.ap()[:, half * T + tg * 512: half * T + (tg + 1) * 512],
                        p.psum[bank].ap()[half * 64:(half + 1) * 64, :]),
                        reads=[p.pr(bank)],
                        writes=[c.R(kdup, half * T + tg * 512, half * T + (tg + 1) * 512)],
                        late=True)
        if h % 2 == 0:
            for tg in range(4):
                bank = tg % 2

                def mmq(e, hp=hp, tg=tg, bank=bank):
                    for ic in range(DC):
                        ins = e.matmul(p.psum[bank][:, :],
                                       wqT[:, ic * D + hp * 128: ic * D + (hp + 1) * 128],
                                       xr_fm[:, ic * T + tg * 512: ic * T + (tg + 1) * 512],
                                       start=(ic == 0), stop=(ic == DC - 1))
                    return ins
                p.add("tensor", mmq, reads=xr_r + [c.R(wqT)], writes=[p.pr(bank)],
                      late=True)
                for half in range(2):
                    p.add("vector", lambda e, half=half, tg=tg, bank=bank: e.tensor_copy(
                        qbuf.ap()[:, half * T + tg * 512: half * T + (tg + 1) * 512],
                        p.psum[bank].ap()[half * 64:(half + 1) * 64, :]),
                        reads=[p.pr(bank)],
                        writes=[c.R(qbuf, half * T + tg * 512, half * T + (tg + 1) * 512)],
                        late=True)
        qoff = (h % 2) * T
        qreg = c.R(qbuf, qoff, qoff + T)
        pend = []
        iters = [(kt, tqc) for kt in range(TT) for tqc in range(kt // 4, 4)]
        for it_i, (kt, tqc) in enumerate(iters):
            ktc = kt // 4
            sbank = it_i % 2
            pslot = it_i % 4
            preg = c.R(pst, pslot * 512, (pslot + 1) * 512)

            def mms(e, kt=kt, tqc=tqc, sbank=sbank, g=g, qoff=qoff):
                koff = (g % 2) * T
                return e.matmul(
                    p.psum[sbank][:, :],
                    kdup.ap()[:, koff + kt * 128: koff + (kt + 1) * 128],
                    qbuf.ap()[:, qoff + tqc * 512: qoff + (tqc + 1) * 512],
                    start=True, stop=True)
            p.add("tensor", mms, reads=[c.R(kdup, (g % 2) * T, (g % 2 + 1) * T), qreg],
                  writes=[p.pr(sbank)], late=True)
            p.add("scalar", lambda e, sbank=sbank, pslot=pslot: e.activation(
                out=pst[:, pslot * 512:(pslot + 1) * 512], in_=p.psum[sbank][:, :],
                func=AF.Exp, scale=0.125),
                reads=[p.pr(sbank)], writes=[preg], late=True)
            if tqc == ktc:
                off = kt * 128 - tqc * 512
                p.add("vector", lambda e, pslot=pslot, off=off: e.tensor_tensor(
                    out=pst[:, pslot * 512 + off: pslot * 512 + off + 128],
                    in0=pst[:, pslot * 512 + off: pslot * 512 + off + 128],
                    in1=maskT[:, :], op=ALU.mult),
                    reads=[preg, c.R(maskT)], writes=[preg], late=True)
                if off > 0:
                    p.add("vector", lambda e, pslot=pslot, off=off: e.memset(
                        pst[:, pslot * 512: pslot * 512 + off], 0.0),
                        writes=[preg], late=True)

            def mmpv(e, kt=kt, tqc=tqc, pslot=pslot, g=g):
                return e.matmul(
                    p.psum[2 + tqc].ap()[0:65, :],
                    vtm[:, kt * NG * 65 + g * 65: kt * NG * 65 + (g + 1) * 65],
                    pst[:, pslot * 512:(pslot + 1) * 512],
                    start=(kt == 0), stop=(kt == tqc * 4 + 3))
            pend.append((mmpv, preg, tqc))
            if len(pend) > 1:
                f, pr_, tq_ = pend.pop(0)
                p.add("tensor", f, reads=[c.R(vtm), pr_], writes=[p.pr(2 + tq_)],
                      late=True)
        for f, pr_, tq_ in pend:
            p.add("tensor", f, reads=[c.R(vtm), pr_], writes=[p.pr(2 + tq_)], late=True)
        zreg = c.R(zrow, 0, 512)
        half = h % 2
        for tqc in range(4):
            p.add("scalar", lambda e, tqc=tqc: e.activation(
                out=zrow[:, 0:512],
                in_=p.psum[2 + tqc].ap()[64:65, :], func=AF.Identity),
                reads=[p.pr(2 + tqc)], writes=[zreg], late=True)
            p.add("vector", lambda e: e.reciprocal(out=zrow[:, 0:512], in_=zrow[:, 0:512]),
                  reads=[zreg], writes=[zreg], late=True)
            def bc(e, tqc=tqc):
                return e.matmul(p.psum[tqc % 2].ap()[0:64, :], onesc[:, :],
                                zrow[:, 0:512],
                                start=True, stop=True)
            p.add("tensor", bc, reads=[c.R(onesc), zreg], writes=[p.pr(tqc % 2)],
                  late=True)
            p.add("scalar", lambda e, tqc=tqc: e.activation(
                out=zb[:, :], in_=p.psum[tqc % 2].ap()[0:64, :], func=AF.Identity),
                reads=[p.pr(tqc % 2)], writes=[c.R(zb)], late=True)
            adst = attn_fm.ap()[half * 64:(half + 1) * 64,
                                hp * T + tqc * 512: hp * T + (tqc + 1) * 512]
            p.add("vector", lambda e, tqc=tqc, adst=adst: e.tensor_tensor(
                out=adst, in0=p.psum[2 + tqc].ap()[0:64, :], in1=zb[:, :], op=ALU.mult),
                reads=[p.pr(2 + tqc), c.R(zb)],
                writes=[c.R(attn_fm, hp * T + tqc * 512, hp * T + (tqc + 1) * 512)],
                late=True)

    woT = wT(aps["W_o"], "wqT" + uid, D)
    afm_r = [c.R(attn_fm, ic * T, (ic + 1) * T) for ic in range(DC)]
    for t in range(TT):
        for oh in range(2):
            bank = (t * 2 + oh) % 2

            def mmo(e, t=t, oh=oh, bank=bank):
                for ic in range(DC):
                    ins = e.matmul(p.psum[bank][:, :],
                                   attn_fm[:, ic * T + t * 128: ic * T + (t + 1) * 128],
                                   woT[:, ic * D + oh * 512: ic * D + (oh + 1) * 512],
                                   start=(ic == 0), stop=(ic == DC - 1))
                return ins
            p.add("tensor", mmo, reads=afm_r + [c.R(woT)], writes=[p.pr(bank)],
                  late=True)
            dst = xs[:, t * D + oh * 512: t * D + (oh + 1) * 512]
            xreg = c.R(xs, t * D + oh * 512, t * D + (oh + 1) * 512)
            p.add("vector", lambda e, bank=bank, dst=dst: e.tensor_tensor(
                out=dst, in0=p.psum[bank][:, :], in1=dst, op=ALU.add),
                reads=[p.pr(bank), xreg], writes=[xreg], late=True)


def emit_l1attna(p, c, aps, uid=""):
    xs, stats, lnbuf, ident = c.xs, c.stats, c.lnbuf, c.ident
    NH = 16
    x_fm = c.x_fm
    attnfm = p.sb("l1a_attnfm" + uid, [128, DC * T], BF16)
    wpair = p.sb("l1a_wpair" + uid, [128, 5 * DC * 128], BF16)
    q1b = p.sb("l1a_q1" + uid, [64, 2 * T], BF16)
    k1b = p.sb("l1a_k1" + uid, [64, 2 * T], BF16)
    q2b = p.sb("l1a_q2" + uid, [64, 2 * T], BF16)
    k2b = p.sb("l1a_k2" + uid, [64, 2 * T], BF16)
    vtm = p.sb("l1a_vtm" + uid, [128, TT * 65], BF16)
    pst = p.sb("l1a_pst" + uid, [128, 4 * 512], BF16)
    zrow = p.sb("l1a_zrow" + uid, [1, 512], F32)
    zb1 = p.sb("l1a_zb1" + uid, [64, 512], F32)
    zb2 = p.sb("l1a_zb2" + uid, [64, 512], F32)
    ctmp = p.sb("l1a_ctmp" + uid, [64, 512], F32)
    cbuf = p.sb("l1a_cbuf" + uid, [64, 512], F32)
    onesc = p.sb("l1a_ones" + uid, [1, 64], F32)
    gamb = p.sb("l1a_gamb" + uid, [128, D], BF16)
    betb = p.sb("l1a_betb" + uid, [128, D], BF16)
    lamc = p.sb("l1a_lam" + uid, [128, 2], F32)
    atile = p.sb("l1a_atile" + uid, [128, D], BF16)
    astat = p.sb("l1a_astat" + uid, [128, 8], F32)
    aln = p.sb("l1a_aln" + uid, [128, D], F32)

    p.add("gpsimd", lambda e: e.memset(onesc[:, :], 1.0), writes=[c.R(onesc)])
    p.add("gpsimd", lambda e: e.memset(
        vtm.ap().rearrange("p (t o) -> p t o", t=TT)[:, :, 64:65], 1.0),
        writes=[c.R(vtm)])
    p.dma("gpsimd", lambda e: e.dma_start(
        out=gamb[:, :], in_=aps["ln_gamma"][None, :].broadcast_to((128, D))),
        writes=[c.R(gamb)], group="l1a_gb" + uid)
    p.dma("gpsimd", lambda e: e.dma_start(
        out=betb[:, :], in_=aps["ln_beta"][None, :].broadcast_to((128, D))),
        writes=[c.R(betb)], group="l1a_gb" + uid)
    p.dma("gpsimd", lambda e: e.dma_start(
        out=lamc[:, 0:1], in_=aps["lamb"][None, :].broadcast_to((128, 1))),
        writes=[c.R(lamc, 0, 1)], group="l1a_gb" + uid)
    p.add("vector", lambda e: e.tensor_scalar(
        out=lamc[:, 1:2], in0=lamc[:, 0:1], scalar1=-1.0, scalar2=1.0,
        op0=ALU.mult, op1=ALU.add),
        reads=[c.R(lamc, 0, 1)], writes=[c.R(lamc, 1, 2)], late=True)
    p.add("vector", lambda e: e.tensor_scalar(
        out=gamb[:, :], in0=gamb[:, :], scalar1=lamc[:, 1:2], scalar2=None, op0=ALU.mult),
        reads=[c.R(gamb), c.R(lamc, 1, 2)], writes=[c.R(gamb)], late=True)
    p.add("vector", lambda e: e.tensor_scalar(
        out=betb[:, :], in0=betb[:, :], scalar1=lamc[:, 1:2], scalar2=None, op0=ALU.mult),
        reads=[c.R(betb), c.R(lamc, 1, 2)], writes=[c.R(betb)], late=True)

    xr_r = [c.R(x_fm, ic * T, (ic + 1) * T) for ic in range(DC)]
    WNAMES = ["wq1", "wk1", "wq2", "wk2", "wv"]
    wreg = c.R(wpair, 0, 5 * DC * 128)

    def wsl(wi, ic):
        return wpair[:, (wi * DC + ic) * 128:(wi * DC + ic + 1) * 128]

    for h in range(NH):
        hp, half = h // 2, h % 2
        if half == 0:
            for wi, wn in enumerate(WNAMES):
                for hh in range(2):
                    p.dma("gpsimd", lambda e, wi=wi, wn=wn, hp=hp, hh=hh: e.dma_start(
                        out=wpair.ap()[:, wi * DC * 128:(wi + 1) * DC * 128].rearrange(
                            "p (cc hd) -> p cc hd", hd=128)[:, :, hh * 64:(hh + 1) * 64],
                        in_=aps[wn][2 * hp + hh].rearrange("(cc p) d -> p cc d", p=128)),
                        writes=[rg(wpair, wi * DC * 128, (wi + 1) * DC * 128)],
                        group="l1a_wp" + uid + str(hp % 2))
            for wi, buf in enumerate([q1b, k1b, q2b, k2b]):
                for tg in range(4):
                    bank = tg % 2

                    def mmp(e, wi=wi, tg=tg, bank=bank):
                        for ic in range(DC):
                            ins = e.matmul(p.psum[bank][:, :], wsl(wi, ic),
                                           x_fm[:, ic * T + tg * 512: ic * T + (tg + 1) * 512],
                                           start=(ic == 0), stop=(ic == DC - 1))
                        return ins
                    p.add("tensor", mmp, reads=xr_r + [wreg], writes=[p.pr(bank)],
                          late=True)
                    for hh in range(2):
                        p.add("vector", lambda e, buf=buf, hh=hh, tg=tg, bank=bank:
                              e.tensor_copy(
                                  buf.ap()[:, hh * T + tg * 512: hh * T + (tg + 1) * 512],
                                  p.psum[bank].ap()[hh * 64:(hh + 1) * 64, :]),
                              reads=[p.pr(bank)],
                              writes=[rg(buf, hh * T + tg * 512, hh * T + (tg + 1) * 512)],
                              late=True)
        for t in range(TT):
            bank = t % 2

            def mmv(e, t=t, bank=bank, half=half):
                for ic in range(DC):
                    ins = e.matmul(p.psum[bank][:, 0:64],
                                   x_fm[:, ic * T + t * 128: ic * T + (t + 1) * 128],
                                   wsl(4, ic)[:, half * 64:(half + 1) * 64],
                                   start=(ic == 0), stop=(ic == DC - 1))
                return ins
            p.add("tensor", mmv, reads=xr_r + [wreg], writes=[p.pr(bank)], late=True)
            p.add("vector", lambda e, t=t, bank=bank: e.tensor_copy(
                vtm[:, t * 65:t * 65 + 64], p.psum[bank].ap()[:, 0:64]),
                reads=[p.pr(bank)], writes=[c.R(vtm, t * 65, t * 65 + 64)], late=True)
        qoff = half * T
        for tqc in range(4):
            pv1, pv2 = 2 + tqc % 2, 4 + tqc % 2
            pend = []
            for kt in range(TT):
                for att, (qq, kk, pvb) in enumerate([(q1b, k1b, pv1), (q2b, k2b, pv2)]):
                    sbank = (2 * kt + att) % 2
                    pslot = (2 * kt + att) % 4
                    preg = c.R(pst, pslot * 512, (pslot + 1) * 512)

                    def mms(e, kt=kt, tqc=tqc, sbank=sbank, qq=qq, kk=kk, qoff=qoff):
                        return e.matmul(
                            p.psum[sbank][:, :],
                            kk.ap()[:, qoff + kt * 128: qoff + (kt + 1) * 128],
                            qq.ap()[:, qoff + tqc * 512: qoff + (tqc + 1) * 512],
                            start=True, stop=True)
                    p.add("tensor", mms,
                          reads=[rg(qq, qoff, qoff + T), rg(kk, qoff, qoff + T)],
                          writes=[p.pr(sbank)], late=True)
                    p.add("scalar", lambda e, sbank=sbank, pslot=pslot: e.activation(
                        out=pst[:, pslot * 512:(pslot + 1) * 512],
                        in_=p.psum[sbank][:, :], func=AF.Exp, scale=0.125),
                        reads=[p.pr(sbank)], writes=[preg], late=True)

                    def mmpv(e, kt=kt, pvb=pvb, pslot=pslot):
                        return e.matmul(
                            p.psum[pvb].ap()[0:65, :],
                            vtm[:, kt * 65:(kt + 1) * 65],
                            pst[:, pslot * 512:(pslot + 1) * 512],
                            start=(kt == 0), stop=(kt == TT - 1))
                    pend.append((mmpv, preg, pvb))
                    if len(pend) > 2:
                        f, pr_, pv_ = pend.pop(0)
                        p.add("tensor", f, reads=[c.R(vtm), pr_], writes=[p.pr(pv_)],
                              late=True)
            for f, pr_, pv_ in pend:
                p.add("tensor", f, reads=[c.R(vtm), pr_], writes=[p.pr(pv_)], late=True)
            zreg = c.R(zrow, 0, 512)
            for att, (pvb, zbx) in enumerate([(pv1, zb1), (pv2, zb2)]):
                p.add("scalar", lambda e, pvb=pvb: e.activation(
                    out=zrow[:, :], in_=p.psum[pvb].ap()[64:65, :], func=AF.Identity),
                    reads=[p.pr(pvb)], writes=[zreg], late=True)
                p.add("vector", lambda e: e.reciprocal(out=zrow[:, :], in_=zrow[:, :]),
                      reads=[zreg], writes=[zreg], late=True)
                if att == 1:
                    p.add("vector", lambda e: e.tensor_scalar(
                        out=zrow[:, :], in0=zrow[:, :], scalar1=lamc.ap()[0:1, 0:1],
                        scalar2=None, op0=ALU.mult),
                        reads=[zreg, c.R(lamc, 0, 1)], writes=[zreg], late=True)

                def bcz(e, att=att):
                    return e.matmul(p.psum[att].ap()[0:64, :], onesc[:, :],
                                    zrow[:, :], start=True, stop=True)
                p.add("tensor", bcz, reads=[c.R(onesc), zreg], writes=[p.pr(att)],
                      late=True)
                p.add("scalar", lambda e, att=att, zbx=zbx: e.activation(
                    out=zbx[:, :], in_=p.psum[att].ap()[0:64, :], func=AF.Identity),
                    reads=[p.pr(att)], writes=[c.R(zbx)], late=True)
            p.add("vector", lambda e, pv2=pv2: e.tensor_tensor(
                out=ctmp[:, :], in0=p.psum[pv2].ap()[0:64, :], in1=zb2[:, :], op=ALU.mult),
                reads=[p.pr(pv2), c.R(zb2)], writes=[c.R(ctmp)], late=True)
            p.add("vector", lambda e, pv1=pv1: e.tensor_tensor(
                out=cbuf[:, :], in0=p.psum[pv1].ap()[0:64, :], in1=zb1[:, :], op=ALU.mult),
                reads=[p.pr(pv1), c.R(zb1)], writes=[c.R(cbuf)], late=True)
            p.add("vector", lambda e: e.tensor_tensor(
                out=cbuf[:, :], in0=cbuf[:, :], in1=ctmp[:, :], op=ALU.subtract),
                reads=[c.R(cbuf), c.R(ctmp)], writes=[c.R(cbuf)], late=True)
            adst = attnfm.ap()[half * 64:(half + 1) * 64,
                               hp * T + tqc * 512: hp * T + (tqc + 1) * 512]
            areg = c.R(attnfm, hp * T + tqc * 512, hp * T + (tqc + 1) * 512)
            p.add("vector", lambda e, adst=adst: e.tensor_copy(adst, cbuf[:, :]),
                  reads=[c.R(cbuf)], writes=[areg], late=True)

    afm_r = [c.R(attnfm, ic * T, (ic + 1) * T) for ic in range(DC)]
    for t in range(TT):
        bank = 6 + t % 2

        def tpa(e, t=t, bank=bank):
            for ic in range(DC):
                ins = e.transpose(p.psum[bank].ap()[:, ic * 128:(ic + 1) * 128],
                                  attnfm[:, ic * T + t * 128: ic * T + (t + 1) * 128],
                                  c.identb[:, :])
            return ins
        p.add("tensor", tpa, reads=afm_r + [c.R(c.identb)], writes=[p.pr(bank)], late=True)
        p.add("vector", lambda e, bank=bank: e.tensor_copy(
            atile[:, :], p.psum[bank][:, :].rearrange("p (a b) -> p (a b)", a=1) if False
            else p.psum[bank].ap()[:, 0:1024]),
            reads=[p.pr(bank)], writes=[c.R(atile)], late=True)
        p.add("scalar", lambda e: e.activation(
            out=aln[:, :], in_=atile[:, :], func=AF.Square, accum_out=astat[:, 2:3]),
            reads=[c.R(atile)], writes=[c.R(aln), c.R(astat, 2, 3)], late=True)
        p.add("vector", lambda e: e.tensor_reduce(
            out=astat[:, 3:4], in_=atile[:, :], axis=AX.X, op=ALU.add),
            reads=[c.R(atile)], writes=[c.R(astat, 3, 4)], late=True)
        p.add("vector", lambda e: e.tensor_scalar(
            out=astat[:, 0:1], in0=astat[:, 3:4], scalar1=1.0 / D, scalar2=None,
            op0=ALU.mult),
            reads=[c.R(astat, 3, 4)], writes=[c.R(astat, 0, 1)], late=True)
        p.add("vector", lambda e: e.tensor_tensor(
            out=astat[:, 4:5], in0=astat[:, 0:1], in1=astat[:, 0:1], op=ALU.mult),
            reads=[c.R(astat, 0, 1)], writes=[c.R(astat, 4, 5)], late=True)
        p.add("vector", lambda e: e.tensor_scalar(
            out=astat[:, 2:3], in0=astat[:, 2:3], scalar1=1.0 / D, scalar2=None,
            op0=ALU.mult),
            reads=[c.R(astat, 2, 3)], writes=[c.R(astat, 2, 3)], late=True)
        p.add("vector", lambda e: e.tensor_tensor(
            out=astat[:, 2:3], in0=astat[:, 2:3], in1=astat[:, 4:5], op=ALU.subtract),
            reads=[c.R(astat, 2, 5)], writes=[c.R(astat, 2, 3)], late=True)
        p.add("vector", lambda e: e.tensor_scalar(
            out=astat[:, 2:3], in0=astat[:, 2:3], scalar1=1e-5, scalar2=None, op0=ALU.add),
            reads=[c.R(astat, 2, 3)], writes=[c.R(astat, 2, 3)], late=True)
        p.add("scalar", lambda e: e.activation(
            out=astat[:, 2:3], in_=astat[:, 2:3], func=AF.Ln),
            reads=[c.R(astat, 2, 3)], writes=[c.R(astat, 2, 3)], late=True)
        p.add("scalar", lambda e: e.activation(
            out=astat[:, 5:6], in_=astat[:, 2:3], func=AF.Exp, scale=-0.5),
            reads=[c.R(astat, 2, 3)], writes=[c.R(astat, 5, 6)], late=True)
        p.add("vector", lambda e: e.tensor_tensor(
            out=astat[:, 6:7], in0=astat[:, 0:1], in1=astat[:, 5:6], op=ALU.mult),
            reads=[c.R(astat, 0, 1), c.R(astat, 5, 6)], writes=[c.R(astat, 6, 7)],
            late=True)
        p.add("vector", lambda e: e.tensor_scalar(
            out=astat[:, 6:7], in0=astat[:, 6:7], scalar1=-1.0, scalar2=None, op0=ALU.mult),
            reads=[c.R(astat, 6, 7)], writes=[c.R(astat, 6, 7)], late=True)
        p.add("scalar", lambda e: e.activation(
            out=aln[:, :], in_=atile[:, :], func=AF.Identity,
            scale=astat[:, 5:6], bias=astat[:, 6:7]),
            reads=[c.R(atile), c.R(astat, 5, 7)], writes=[c.R(aln)], late=True)
        p.add("vector", lambda e: e.tensor_tensor(
            out=aln[:, :], in0=aln[:, :], in1=gamb[:, :], op=ALU.mult),
            reads=[c.R(aln), c.R(gamb)], writes=[c.R(aln)])
        p.add("vector", lambda e: e.tensor_tensor(
            out=aln[:, :], in0=aln[:, :], in1=betb[:, :], op=ALU.add),
            reads=[c.R(aln), c.R(betb)], writes=[c.R(aln)])
        xreg = c.R(xs, t * D, (t + 1) * D)
        p.add("vector", lambda e, t=t: e.tensor_tensor(
            out=xs[:, t * D:(t + 1) * D], in0=xs[:, t * D:(t + 1) * D],
            in1=aln[:, :], op=ALU.add),
            reads=[xreg, c.R(aln)], writes=[xreg], late=True)


def build_L1_attn_a():
    nc = bass.Bass("TRN2", target_bir_lowering=False)
    x = nc.dram_tensor("x", [T, D], F32, kind="ExternalInput").ap()
    aps = {
        "wq1": nc.dram_tensor("wq1", [16, D, 64], F32, kind="ExternalInput").ap(),
        "wk1": nc.dram_tensor("wk1", [16, D, 64], F32, kind="ExternalInput").ap(),
        "wq2": nc.dram_tensor("wq2", [16, D, 64], F32, kind="ExternalInput").ap(),
        "wk2": nc.dram_tensor("wk2", [16, D, 64], F32, kind="ExternalInput").ap(),
        "wv": nc.dram_tensor("wv", [16, D, 64], F32, kind="ExternalInput").ap(),
        "ln_gamma": nc.dram_tensor("ln_gamma", [D], F32, kind="ExternalInput").ap(),
        "ln_beta": nc.dram_tensor("ln_beta", [D], F32, kind="ExternalInput").ap(),
        "lamb": nc.dram_tensor("lamb", [1], F32, kind="ExternalInput").ap(),
    }
    y = nc.dram_tensor("y", [T, D], F32, kind="ExternalOutput").ap()
    with ExitStack() as stack:
        p = Prog(nc, stack)
        c = Ctx(p)
        emit_identity(p, c)
        emit_load_x(p, c, x)
        emit_ln_to_fm(p, c)
        emit_l1attna(p, c, aps)
        emit_store_xs(p, c, y)
        with nc.Block() as block:
            p.emit(block)
    return nc



def emit_l1ffn(p, c, aps, uid=""):
    xs, stats, lnbuf, ident = c.xs, c.stats, c.lnbuf, c.ident
    H, G, HD, E = 16, 16, 64, 8
    h2fm = p.sb("l1f_h2fm" + uid, [128, DC * T], BF16)
    gateW = p.sb("l1f_gate" + uid, [128, TT * 16], F32)
    htile = p.sb("l1f_t2" + uid, [128, D], F32)
    hfmt = p.sb("l1f_hfmt" + uid, [128, D], BF16)
    qtile = p.sb("l1f_qtile" + uid, [128, D], BF16)
    kvtile = p.sb("l1f_kvt" + uid, [128, 2048], BF16)
    attb = p.sb("l1f_att" + uid, [128, 2 * 256], F32)
    otile = p.sb("l1f_qtile" + uid, [128, D], BF16)
    otileT = p.sb("l1f_hfmt" + uid, [128, D], BF16)
    xmid = p.sb("l1f_xmid" + uid, [128, D], F32)
    tst = p.sb("l1f_tst" + uid, [128, 16], F32)
    t2 = p.sb("l1f_t2" + uid, [128, D], F32)

    wqb = load_w_inout(p, c, aps["Wq"], "l1f_wqb" + uid, D, D)
    wdb = load_w_inout(p, c, aps["Wdown"], "l1f_wdb" + uid, D, 64)
    wob = load_w_inout(p, c, aps["Wo"], "l1f_wob" + uid, D, D)
    wupb = p.sb("l1f_wupb" + uid, [64, 2048], BF16)
    p.dma("gpsimd", lambda e: e.dma_start(out=wupb[:, :], in_=aps["Wup"][:, :]),
          writes=[c.R(wupb)], group="l1f_wup" + uid)
    wgb = load_w_inout(p, c, aps["Wg"], "l1f_wgb" + uid, D, 8)

    def bco(dram, name, n, dt=F32):
        buf = p.sb(name, [128, n], dt)
        p.dma("gpsimd", lambda e: e.dma_start(
            out=buf[:, :], in_=dram[None, :].broadcast_to((128, n))),
            writes=[c.R(buf)], group=name)
        return buf
    gam1b = bco(aps["gamma1"], "l1f_g1" + uid, D, BF16)
    gam2b = bco(aps["gamma2"], "l1f_g2" + uid, D, BF16)
    bqb = bco(aps["bq"], "l1f_bq" + uid, D, BF16)
    bupb = bco(aps["bup"], "l1f_bup" + uid, 2048, BF16)
    bob = bco(aps["bo"], "l1f_bo" + uid, D, BF16)
    bgb = bco(aps["bg"], "l1f_bg" + uid, 8)
    bdc = p.sb("l1f_bd" + uid, [64, 1], F32)
    p.dma("gpsimd", lambda e: e.dma_start(out=bdc[:, :], in_=aps["bdown"][:, None]),
          writes=[c.R(bdc)], group="l1f_bd" + uid)

    for t in range(TT):
        xreg = c.R(xs, t * D, (t + 1) * D)
        ureg = c.R(lnbuf, 0, D)
        p.add("scalar", lambda e, t=t: e.activation(
            out=lnbuf[:, :], in_=xs[:, t * D:(t + 1) * D], func=AF.Identity,
            scale=stats[:, t:t + 1], bias=stats[:, TT + t:TT + t + 1]),
            reads=[xreg, c.R(stats, 0, 2 * TT)], writes=[ureg], late=True)
        p.add("scalar", lambda e: e.activation(
            out=htile[:, :], in_=lnbuf[:, :], func=AF.Square, accum_out=tst[:, 0:1]),
            reads=[ureg], writes=[c.R(htile), c.R(tst, 0, 1)], late=True)
        p.add("vector", lambda e: e.tensor_scalar(
            out=tst[:, 0:1], in0=tst[:, 0:1], scalar1=1.0 / D, scalar2=1e-6,
            op0=ALU.mult, op1=ALU.add),
            reads=[c.R(tst, 0, 1)], writes=[c.R(tst, 0, 1)], late=True)
        p.add("scalar", lambda e: e.activation(
            out=tst[:, 0:1], in_=tst[:, 0:1], func=AF.Ln),
            reads=[c.R(tst, 0, 1)], writes=[c.R(tst, 0, 1)], late=True)
        p.add("scalar", lambda e: e.activation(
            out=tst[:, 1:2], in_=tst[:, 0:1], func=AF.Exp, scale=-0.5),
            reads=[c.R(tst, 0, 1)], writes=[c.R(tst, 1, 2)], late=True)
        p.add("scalar", lambda e: e.activation(
            out=htile[:, :], in_=lnbuf[:, :], func=AF.Identity, scale=tst[:, 1:2]),
            reads=[ureg, c.R(tst, 1, 2)], writes=[c.R(htile)], late=True)
        p.add("vector", lambda e: e.tensor_tensor(
            out=htile[:, :], in0=htile[:, :], in1=gam1b[:, :], op=ALU.mult),
            reads=[c.R(htile), c.R(gam1b)], writes=[c.R(htile)])
        for hh in range(0, DC, 4):
            bank = (hh // 4) % 2

            def tph(e, hh=hh, bank=bank):
                for j in range(4):
                    ins = e.transpose(p.psum[bank].ap()[:, j * 128:(j + 1) * 128],
                                      htile[:, (hh + j) * 128:(hh + j + 1) * 128],
                                      ident[:, :])
                return ins
            p.add("tensor", tph, reads=[c.R(htile), c.R(ident)], writes=[p.pr(bank)],
                  late=True)
            p.add("vector", lambda e, hh=hh, bank=bank: e.tensor_copy(
                hfmt[:, hh * 128:(hh + 4) * 128], p.psum[bank].ap()[:, 0:512]),
                reads=[p.pr(bank)], writes=[c.R(hfmt, hh * 128, (hh + 4) * 128)],
                late=True)
        hreg = c.R(hfmt, 0, D)
        for oh in range(2):
            def mmq(e, oh=oh):
                for ic in range(DC):
                    ins = e.matmul(p.psum[2 + oh][:, :],
                                   hfmt[:, ic * 128:(ic + 1) * 128],
                                   wqb[:, ic * D + oh * 512: ic * D + (oh + 1) * 512],
                                   start=(ic == 0), stop=(ic == DC - 1))
                return ins
            p.add("tensor", mmq, reads=[hreg, c.R(wqb)], writes=[p.pr(2 + oh)], late=True)
            p.add("vector", lambda e, oh=oh: e.tensor_tensor(
                out=qtile[:, oh * 512:(oh + 1) * 512], in0=p.psum[2 + oh][:, :],
                in1=bqb[:, oh * 512:(oh + 1) * 512], op=ALU.add),
                reads=[p.pr(2 + oh), c.R(bqb)],
                writes=[c.R(qtile, oh * 512, (oh + 1) * 512)], late=True)
        def mmd(e):
            for ic in range(DC):
                ins = e.matmul(p.psum[4].ap()[0:64, 0:128],
                               wdb[:, ic * 64:(ic + 1) * 64],
                               hfmt[:, ic * 128:(ic + 1) * 128],
                               start=(ic == 0), stop=(ic == DC - 1))
            return ins
        p.add("tensor", mmd, reads=[hreg, c.R(wdb)], writes=[p.pr(4)], late=True)
        d1reg = c.R(htile, 0, 128)
        p.add("scalar", lambda e: e.activation(
            out=otileT.ap()[0:64, 0:128], in_=p.psum[4].ap()[0:64, 0:128], func=AF.Identity,
            bias=bdc[:, 0:1]),
            reads=[p.pr(4), c.R(bdc)], writes=[c.R(otileT, 0, 128)], late=True)
        for oh in range(4):
            def mmkv(e, oh=oh):
                return e.matmul(p.psum[2 + oh % 2][:, :], otileT.ap()[0:64, 0:128],
                                wupb[:, oh * 512:(oh + 1) * 512], start=True, stop=True)
            p.add("tensor", mmkv, reads=[c.R(otileT, 0, 128), c.R(wupb)],
                  writes=[p.pr(2 + oh % 2)], late=True)
            p.add("vector", lambda e, oh=oh: e.tensor_tensor(
                out=kvtile[:, oh * 512:(oh + 1) * 512], in0=p.psum[2 + oh % 2][:, :],
                in1=bupb[:, oh * 512:(oh + 1) * 512], op=ALU.add),
                reads=[p.pr(2 + oh % 2), c.R(bupb)],
                writes=[c.R(kvtile, oh * 512, (oh + 1) * 512)], late=True)
        kreg = c.R(kvtile, 0, 2048)
        kview = kvtile[:, :].rearrange("p (g kv) -> p g kv", g=G)[:, :, 0:64]
        vview = kvtile[:, :].rearrange("p (g kv) -> p g kv", g=G)[:, :, 64:128]
        for h in range(H):
            qv = qtile[:, h * 64:(h + 1) * 64].rearrange(
                "p (a d) -> p a d", a=1).broadcast_to((128, G, 64))
            p.add("vector", lambda e, qv=qv: e.tensor_tensor(
                out=t2[:, :].rearrange("p (g d) -> p g d", d=64),
                in0=kview, in1=qv, op=ALU.mult),
                reads=[kreg, c.R(qtile, h * 64, (h + 1) * 64)],
                writes=[c.R(t2)], late=True)
            p.add("vector", lambda e, h=h: e.tensor_reduce(
                out=attb[:, h * 16:(h + 1) * 16],
                in_=t2[:, :].rearrange("p (g d) -> p g d", d=64),
                axis=AX.X, op=ALU.add),
                reads=[c.R(t2)], writes=[c.R(attb, h * 16, (h + 1) * 16)], late=True)
        areg = c.R(attb, 0, 256)
        p.add("scalar", lambda e: e.activation(
            out=attb[:, 0:256], in_=attb[:, 0:256], func=AF.Exp, scale=0.125),
            reads=[areg], writes=[areg], late=True)
        p.add("vector", lambda e: e.tensor_reduce(
            out=tst[:, 0:16], in_=attb[:, 0:256].rearrange("p (h g) -> p h g", g=16),
            axis=AX.X, op=ALU.add),
            reads=[areg], writes=[c.R(tst, 0, 16)], late=True)
        p.add("vector", lambda e: e.reciprocal(out=tst[:, 0:16], in_=tst[:, 0:16]),
              reads=[c.R(tst, 0, 16)], writes=[c.R(tst, 0, 16)], late=True)
        p.add("vector", lambda e: e.tensor_tensor(
            out=attb[:, 256:512].rearrange("p (h g) -> p h g", g=16),
            in0=attb[:, 0:256].rearrange("p (h g) -> p h g", g=16),
            in1=tst[:, 0:16].rearrange("p (h a) -> p h a", a=1).broadcast_to((128, 16, 16)),
            op=ALU.mult),
            reads=[areg, c.R(tst, 0, 16)], writes=[c.R(attb, 256, 512)], late=True)
        for h in range(H):
            av = attb[:, 256 + h * 16:256 + (h + 1) * 16].rearrange(
                "p (g a) -> p g a", a=1).broadcast_to((128, G, 64))
            p.add("vector", lambda e, av=av: e.tensor_tensor(
                out=t2[:, :].rearrange("p (d g) -> p g d", g=16),
                in0=vview, in1=av, op=ALU.mult),
                reads=[kreg, c.R(attb, 256 + h * 16, 256 + (h + 1) * 16)],
                writes=[c.R(t2)], late=True)
            p.add("vector", lambda e, h=h: e.tensor_reduce(
                out=xmid[:, h * 64:(h + 1) * 64],
                in_=t2[:, :].rearrange("p (d g) -> p d g", g=16),
                axis=AX.X, op=ALU.add),
                reads=[c.R(t2)], writes=[c.R(xmid, h * 64, (h + 1) * 64)], late=True)
        p.add("vector", lambda e: e.tensor_copy(otile[:, :], xmid[:, :]),
              reads=[c.R(xmid, 0, D)], writes=[c.R(otile, 0, D)], late=True)
        for hh in range(0, DC, 4):
            bank = 6 + (hh // 4) % 2

            def tpo(e, hh=hh, bank=bank):
                for j in range(4):
                    ins = e.transpose(p.psum[bank].ap()[:, (hh + j) % 4 * 128:((hh + j) % 4 + 1) * 128],
                                      otile[:, (hh + j) * 128:(hh + j + 1) * 128],
                                      c.identb[:, :])
                return ins
            p.add("tensor", tpo, reads=[c.R(otile), c.R(c.identb)], writes=[p.pr(bank)],
                  late=True)
            p.add("vector", lambda e, hh=hh, bank=bank: e.tensor_copy(
                otileT[:, hh * 128:(hh + 4) * 128], p.psum[bank].ap()[:, 0:512]),
                reads=[p.pr(bank)], writes=[c.R(otileT, hh * 128, (hh + 4) * 128)],
                late=True)
        for oh in range(2):
            def mmo(e, oh=oh):
                for ic in range(DC):
                    ins = e.matmul(p.psum[2 + oh][:, :],
                                   otileT[:, ic * 128:(ic + 1) * 128],
                                   wob[:, ic * D + oh * 512: ic * D + (oh + 1) * 512],
                                   start=(ic == 0), stop=(ic == DC - 1))
                return ins
            p.add("tensor", mmo, reads=[c.R(otileT, 0, D), c.R(wob)],
                  writes=[p.pr(2 + oh)], late=True)
            p.add("vector", lambda e, oh=oh: e.tensor_tensor(
                out=t2[:, oh * 512:(oh + 1) * 512], in0=p.psum[2 + oh][:, :],
                in1=bob[:, oh * 512:(oh + 1) * 512], op=ALU.add),
                reads=[p.pr(2 + oh), c.R(bob)], writes=[c.R(t2, oh * 512, (oh + 1) * 512)],
                late=True)
        p.add("vector", lambda e: e.tensor_tensor(
            out=xmid[:, :], in0=t2[:, :], in1=lnbuf[:, :], op=ALU.add),
            reads=[c.R(t2, 0, D), ureg], writes=[c.R(xmid)], late=True)
        p.add("gpsimd", lambda e, t=t: e.tensor_tensor(
            out=xs[:, t * D:(t + 1) * D], in0=xs[:, t * D:(t + 1) * D],
            in1=xmid[:, :], op=ALU.add),
            reads=[xreg, c.R(xmid)], writes=[xreg], late=True)
        p.add("scalar", lambda e: e.activation(
            out=t2[:, :], in_=xmid[:, :], func=AF.Square, accum_out=tst[:, 2:3]),
            reads=[c.R(xmid)], writes=[c.R(t2, 0, D), c.R(tst, 2, 3)], late=True)
        p.add("vector", lambda e: e.tensor_scalar(
            out=tst[:, 2:3], in0=tst[:, 2:3], scalar1=1.0 / D, scalar2=1e-6,
            op0=ALU.mult, op1=ALU.add),
            reads=[c.R(tst, 2, 3)], writes=[c.R(tst, 2, 3)], late=True)
        p.add("scalar", lambda e: e.activation(
            out=tst[:, 2:3], in_=tst[:, 2:3], func=AF.Ln),
            reads=[c.R(tst, 2, 3)], writes=[c.R(tst, 2, 3)], late=True)
        p.add("scalar", lambda e: e.activation(
            out=tst[:, 3:4], in_=tst[:, 2:3], func=AF.Exp, scale=-0.5),
            reads=[c.R(tst, 2, 3)], writes=[c.R(tst, 3, 4)], late=True)
        p.add("scalar", lambda e: e.activation(
            out=xmid[:, :], in_=xmid[:, :], func=AF.Identity, scale=tst[:, 3:4]),
            reads=[c.R(xmid), c.R(tst, 3, 4)], writes=[c.R(xmid)], late=True)
        p.add("vector", lambda e: e.tensor_tensor(
            out=xmid[:, :], in0=xmid[:, :], in1=gam2b[:, :], op=ALU.mult),
            reads=[c.R(xmid), c.R(gam2b)], writes=[c.R(xmid)])
        for hh in range(0, DC, 4):
            bank = (hh // 4) % 2

            def tph2(e, hh=hh, bank=bank):
                for j in range(4):
                    ins = e.transpose(p.psum[bank].ap()[:, j * 128:(j + 1) * 128],
                                      xmid[:, (hh + j) * 128:(hh + j + 1) * 128],
                                      ident[:, :])
                return ins
            p.add("tensor", tph2, reads=[c.R(xmid), c.R(ident)], writes=[p.pr(bank)],
                  late=True)
            dfm = h2fm.ap().rearrange("p (cc t2_) -> p cc t2_", t2_=T)[
                :, hh:hh + 4, t * 128:(t + 1) * 128]
            p.add("vector", lambda e, bank=bank, dfm=dfm, hh=hh, t=t: e.tensor_copy(
                dfm, p.psum[bank].ap()[:, 0:512].rearrange("p (cc q) -> p cc q", q=128)),
                reads=[p.pr(bank)],
                writes=[rg(h2fm, hh * T + t * 128, (hh + 4) * T)], late=True)
        def mmg(e, t=t):
            for ic in range(DC):
                ins = e.matmul(p.psum[5][:, 0:8],
                               h2fm[:, ic * T + t * 128: ic * T + (t + 1) * 128],
                               wgb[:, ic * 8:(ic + 1) * 8],
                               start=(ic == 0), stop=(ic == DC - 1))
            return ins
        p.add("tensor", mmg, reads=[c.R(h2fm), c.R(wgb)], writes=[p.pr(5)], late=True)
        greg = c.R(gateW, t * 16, (t + 1) * 16)
        p.add("vector", lambda e, t=t: e.tensor_tensor(
            out=gateW[:, t * 16:t * 16 + 8], in0=p.psum[5][:, 0:8], in1=bgb[:, :],
            op=ALU.add),
            reads=[p.pr(5), c.R(bgb)], writes=[greg], late=True)
        p.add("scalar", lambda e, t=t: e.activation(
            out=gateW[:, t * 16:t * 16 + 8], in_=gateW[:, t * 16:t * 16 + 8],
            func=AF.Exp, accum_out=gateW[:, t * 16 + 8:t * 16 + 9]),
            reads=[greg], writes=[greg], late=True)
        p.add("vector", lambda e, t=t: e.reciprocal(
            out=gateW[:, t * 16 + 9:t * 16 + 10], in_=gateW[:, t * 16 + 8:t * 16 + 9]),
            reads=[greg], writes=[greg], late=True)
        p.add("vector", lambda e, t=t: e.tensor_scalar(
            out=gateW[:, t * 16:t * 16 + 8], in0=gateW[:, t * 16:t * 16 + 8],
            scalar1=gateW[:, t * 16 + 9:t * 16 + 10], scalar2=None, op0=ALU.mult),
            reads=[greg], writes=[greg], late=True)

    web = [p.sb("l1f_wqb" + uid, [128, 8 * D], BF16), p.sb("l1f_wob" + uid, [128, 8 * D], BF16)]
    h2r = [c.R(h2fm, ic * T, (ic + 1) * T) for ic in range(DC)]
    for ei in range(E):
        p.dma("gpsimd", lambda e, ei=ei: e.dma_start(
            out=web[ei % 2].ap().rearrange("p (cc o) -> p cc o", o=D),
            in_=aps["We"][ei].rearrange("(cc p) o -> p cc o", p=128)),
            writes=[c.R(web[ei % 2])], group=f"l1f_we{ei % 2}" + uid)
        for t in range(TT):
            for oh in range(2):
                bank = 2 + ((t * 2 + oh) % 4)

                def mme(e, t=t, oh=oh, bank=bank, ei=ei):
                    for ic in range(DC):
                        ins = e.matmul(p.psum[bank][:, :],
                                       h2fm[:, ic * T + t * 128: ic * T + (t + 1) * 128],
                                       web[ei % 2][:, ic * D + oh * 512: ic * D + (oh + 1) * 512],
                                       start=(ic == 0), stop=(ic == DC - 1))
                    return ins
                p.add("tensor", mme, reads=h2r + [c.R(web[ei % 2])], writes=[p.pr(bank)],
                      late=True)
                dst = xs[:, t * D + oh * 512: t * D + (oh + 1) * 512]
                xreg = c.R(xs, t * D + oh * 512, t * D + (oh + 1) * 512)
                p.add("vector", lambda e, bank=bank, dst=dst, t=t, ei=ei:
                      e.scalar_tensor_tensor(
                          out=dst, in0=p.psum[bank][:, :],
                          scalar=gateW[:, t * 16 + ei:t * 16 + ei + 1], in1=dst,
                          op0=ALU.mult, op1=ALU.add),
                      reads=[p.pr(bank), xreg, c.R(gateW, t * 16 + ei, t * 16 + ei + 1)],
                      writes=[xreg], late=True)


def build_L1_ffn():
    nc = bass.Bass("TRN2", target_bir_lowering=False)
    x = nc.dram_tensor("x", [T, D], F32, kind="ExternalInput").ap()
    aps = {
        "gamma1": nc.dram_tensor("gamma1", [D], F32, kind="ExternalInput").ap(),
        "Wq": nc.dram_tensor("Wq", [D, D], F32, kind="ExternalInput").ap(),
        "bq": nc.dram_tensor("bq", [D], F32, kind="ExternalInput").ap(),
        "Wdown": nc.dram_tensor("Wdown", [D, 64], F32, kind="ExternalInput").ap(),
        "bdown": nc.dram_tensor("bdown", [64], F32, kind="ExternalInput").ap(),
        "Wup": nc.dram_tensor("Wup", [64, 2048], F32, kind="ExternalInput").ap(),
        "bup": nc.dram_tensor("bup", [2048], F32, kind="ExternalInput").ap(),
        "Wo": nc.dram_tensor("Wo", [D, D], F32, kind="ExternalInput").ap(),
        "bo": nc.dram_tensor("bo", [D], F32, kind="ExternalInput").ap(),
        "gamma2": nc.dram_tensor("gamma2", [D], F32, kind="ExternalInput").ap(),
        "Wg": nc.dram_tensor("Wg", [D, 8], F32, kind="ExternalInput").ap(),
        "bg": nc.dram_tensor("bg", [8], F32, kind="ExternalInput").ap(),
        "We": nc.dram_tensor("We", [8, D, D], F32, kind="ExternalInput").ap(),
    }
    y = nc.dram_tensor("y", [T, D], F32, kind="ExternalOutput").ap()
    with ExitStack() as stack:
        p = Prog(nc, stack)
        c = Ctx(p)
        emit_identity(p, c)
        emit_load_x(p, c, x)
        emit_ln_stats_only(p, c)
        emit_l1ffn(p, c, aps)
        emit_store_xs(p, c, y)
        with nc.Block() as block:
            p.emit(block)
    return nc



def emit_l1attnb(p, c, aps, uid=""):
    xs, stats, lnbuf, ident, identb = c.xs, c.stats, c.lnbuf, c.ident, c.identb
    nc = p.nc
    x_fm = c.x_fm
    qd = nc.dram_tensor("l1b_qd" + uid, [T, D], BF16, kind="Internal").ap()
    kd = nc.dram_tensor("l1b_kd" + uid, [T, D], BF16, kind="Internal").ap()
    vd = nc.dram_tensor("l1b_vd" + uid, [T, D], BF16, kind="Internal").ap()
    ld_ = nc.dram_tensor("l1b_ld" + uid, [T, D], F32, kind="Internal").ap()

    wbuf = p.sb("l1b_wbuf" + uid, [128, DC * D], BF16)
    scrA = p.sb("l1b_scrA" + uid, [128, 8192], BF16)
    ptile = scrA.ap()[:, 0:4096].bitcast(F32)
    btile = scrA.ap()[:, 4096:8192].bitcast(F32)
    _SA = sbname(scrA)
    _SO = getattr(scrA, "off", 0)

    def PTr(lo, hi):
        return (_SA, _SO + 4 * lo, _SO + 4 * hi)

    def BTr(lo, hi):
        return (_SA, _SO + 8192 + 4 * lo, _SO + 8192 + 4 * hi)
    otile = p.sb("l1b_otile" + uid, [128, D], BF16)
    biasb = p.sb("l1b_biasb" + uid, [128, D], F32)

    qt = p.sb("l1b_qt" + uid, [128, 2 * D], BF16)
    kt = p.sb("l1b_kt" + uid, [128, 2 * D], BF16)
    vt = p.sb("l1b_vt" + uid, [128, 2 * D], BF16)
    lt = p.sb("l1b_lt" + uid, [128, 2 * D], F32)
    bp = p.sb("l1b_bp" + uid, [128, D], F32)
    ehq = p.sb("l1b_ehq" + uid, [128, D], F32)
    qhat = p.sb("l1b_qhat" + uid, [128, D], BF16)
    khat = p.sb("l1b_khat" + uid, [128, D], BF16)
    khat1 = p.sb("l1b_khat1" + uid, [64, D], BF16)
    v1 = p.sb("l1b_v1" + uid, [64, D], BF16)
    qhf = p.sb("l1b_qhf" + uid, [128, D], BF16)
    khf = p.sb("l1b_khf" + uid, [128, D], BF16)
    qtl = p.sb("l1b_qtl" + uid, [128, D], BF16)
    att = p.sb("l1b_att" + uid, [128, 128], BF16)
    stot = p.sb("l1b_stot" + uid, [128, 8 * 4], F32)
    fcol = p.sb("l1b_fcol" + uid, [128, 8 * 4], F32)
    S = c.x_fm.ap().bitcast(F32)
    Sb = wbuf

    def SRr(lo, hi):
        return ("x_fm", 4 * lo, 4 * hi)
    stmp = p.sb("l1b_stmp" + uid, [128, 512], F32)
    u64 = p.sb("l1b_u64" + uid, [128, 128], F32)
    onesA = p.sb("l1b_onesA" + uid, [128, 2], F32)
    maskL = p.sb("l1b_maskL" + uid, [128, 128], F32)
    obuf = p.sb("l1b_obuf" + uid, [128, D], BF16)
    ofm = p.sb("l1b_ofm" + uid, [128, D], BF16)
    wob = scrA
    bob = p.sb("l1b_bob" + uid, [128, D], F32)
    p.dma("gpsimd", lambda e: e.dma_start(
        out=bob[:, :], in_=aps["bo"][None, :].broadcast_to((128, D))),
        writes=[c.R(bob)], group="l1b_bob" + uid)

    p.add("gpsimd", lambda e: e.memset(u64[:, :], 1.0), writes=[c.R(u64)])
    p.add("gpsimd", lambda e: e.affine_select(
        out=u64[:, :], in_=u64[:, :], compare_op=ALU.is_ge, fill=0.0,
        base=0, pattern=[[1, 128]], channel_multiplier=-1),
        reads=[c.R(u64)], writes=[c.R(u64)], late=True)
    p.add("gpsimd", lambda e: e.memset(u64.ap()[0:64, 64:128], 0.0),
          writes=[c.R(u64)], late=True)
    p.add("gpsimd", lambda e: e.memset(onesA[:, :], 0.0), writes=[c.R(onesA)])
    p.add("gpsimd", lambda e: e.memset(onesA.ap()[0:64, 0:1], 1.0), writes=[c.R(onesA)])
    p.add("gpsimd", lambda e: e.memset(onesA.ap()[64:128, 1:2], 1.0), writes=[c.R(onesA)])
    p.add("gpsimd", lambda e: e.memset(maskL[:, :], 1.0), writes=[c.R(maskL)])
    p.add("gpsimd", lambda e: e.affine_select(
        out=maskL[:, :], in_=maskL[:, :], compare_op=ALU.is_ge, fill=0.0,
        base=0, pattern=[[1, 128]], channel_multiplier=-1),
        reads=[c.R(maskL)], writes=[c.R(maskL)], late=True)

    xr_r = [c.R(x_fm, ic * T, (ic + 1) * T) for ic in range(DC)]

    PROJ = [("Wq", "bq", qd, "q"), ("Wk", "bk", kd, "k"),
            ("Wv", "bv", vd, "v"), ("Wg", "bg", ld_, "g")]
    for wn, bn, dest, kindp in PROJ:
        p.dma("gpsimd", lambda e, wn=wn: e.dma_start(
            out=wbuf.ap().rearrange("p (cc o) -> p cc o", o=D),
            in_=aps[wn].rearrange("(cc p) o -> p cc o", p=128)),
            writes=[c.R(wbuf)], group="l1b_w" + wn + uid)
        p.dma("gpsimd", lambda e, bn=bn: e.dma_start(
            out=biasb[:, :], in_=aps[bn][None, :].broadcast_to((128, D))),
            writes=[c.R(biasb)], group="l1b_w" + wn + uid)
        for t in range(TT):
            par = t % 2
            for oh in range(2):
                bank = oh

                def mmp(e, t=t, oh=oh, bank=bank):
                    for ic in range(DC):
                        ins = e.matmul(p.psum[bank][:, :],
                                       x_fm[:, ic * T + t * 128: ic * T + (t + 1) * 128],
                                       wbuf[:, ic * D + oh * 512: ic * D + (oh + 1) * 512],
                                       start=(ic == 0), stop=(ic == DC - 1))
                    return ins
                p.add("tensor", mmp, reads=xr_r + [c.R(wbuf)], writes=[p.pr(bank)],
                      late=True)
                seg = (par * D + oh * 512, par * D + (oh + 1) * 512)
                p.add("vector", lambda e, bank=bank, seg=seg: e.tensor_tensor(
                    out=ptile[:, seg[0]:seg[1]], in0=p.psum[bank][:, :],
                    in1=biasb[:, (seg[0] % D):(seg[0] % D) + 512], op=ALU.add),
                    reads=[p.pr(bank), c.R(biasb)],
                    writes=[PTr(seg[0], seg[1])], late=True)
            pseg = (par * D, (par + 1) * D)
            psrc = ptile[:, pseg[0]:pseg[1]]
            preg = PTr(pseg[0], pseg[1])
            bdst = btile[:, pseg[0]:pseg[1]]
            breg = BTr(pseg[0], pseg[1])
            if kindp == "q" or kindp == "v":
                p.add("scalar", lambda e, psrc=psrc, bdst=bdst: e.activation(
                    out=bdst, in_=psrc, func=AF.Identity),
                    reads=[preg], writes=[breg], late=True)
            elif kindp == "k":
                p.add("scalar", lambda e, psrc=psrc, bdst=bdst: e.activation(
                    out=bdst, in_=psrc, func=AF.Sigmoid),
                    reads=[preg], writes=[breg], late=True)
            else:
                p.add("scalar", lambda e, psrc=psrc, bdst=bdst: e.activation(
                    out=bdst, in_=psrc, func=AF.Abs),
                    reads=[preg], writes=[breg], late=True)
                p.add("scalar", lambda e, bdst=bdst: e.activation(
                    out=bdst, in_=bdst, func=AF.Exp, scale=-1.0),
                    reads=[breg], writes=[breg], late=True)
                p.add("scalar", lambda e, bdst=bdst: e.activation(
                    out=bdst, in_=bdst, func=AF.Ln, bias=1.0),
                    reads=[breg], writes=[breg], late=True)
                p.add("scalar", lambda e, psrc=psrc, bdst=bdst: e.activation(
                    out=ptile[:, 0:D] if False else psrc, in_=psrc, func=AF.Relu, scale=-1.0),
                    reads=[preg], writes=[preg], late=True)
                p.add("vector", lambda e, psrc=psrc, bdst=bdst: e.tensor_tensor(
                    out=bdst, in0=bdst, in1=psrc, op=ALU.add),
                    reads=[breg, preg], writes=[breg], late=True)
            if kindp == "g":
                p.dma("sync", lambda e, t=t, bdst=bdst: e.dma_start(
                    out=dest[t * 128:(t + 1) * 128, :], in_=bdst),
                    reads=[breg], writes=[("@dram" + kindp + uid, t, t + 1)],
                    group="l1b_sp" + kindp + uid)
            else:
                oreg = c.R(otile, 0, D)
                p.add("vector", lambda e, bdst=bdst: e.tensor_copy(otile[:, :], bdst),
                      reads=[breg], writes=[oreg], late=True)
                p.dma("sync", lambda e, t=t, dest=dest: e.dma_start(
                    out=dest[t * 128:(t + 1) * 128, :], in_=otile[:, :]),
                    reads=[oreg], writes=[("@dram" + kindp + uid, t, t + 1)],
                    group="l1b_sp" + kindp + uid)

    p.add("vector", lambda e: e.memset(S[:, :], 0.0), writes=[SRr(0, 8 * D)])
    p.add("vector", lambda e: e.memset(Sb[:, :], 0.0), writes=[c.R(wbuf)])
    p.dma("gpsimd", lambda e: e.dma_start(
        out=wob.ap().rearrange("p (cc o) -> p cc o", o=D),
        in_=aps["Wo"].rearrange("(cc p) o -> p cc o", p=128)),
        writes=[(_SA, _SO, _SO + 16384)], group="l1b_wo" + uid)

    dbg = aps.get("_dbg")

    for ch in range(TT):
        par = ch % 2
        for buf, dram, grp in [(qt, qd, "q"), (kt, kd, "k"), (vt, vd, "v"), (lt, ld_, "g")]:
            p.dma("sync", lambda e, buf=buf, dram=dram, ch=ch, par=par: e.dma_start(
                out=buf[:, par * D:(par + 1) * D], in_=dram[ch * 128:(ch + 1) * 128, :]),
                reads=[("@dram" + grp + uid, ch, ch + 1),
                       ("@dram" + grp + uid, TT - 1, TT)],
                writes=[rg(buf, par * D, (par + 1) * D)],
                group="l1b_ld" + grp + uid + str(par))
        qs = qt[:, par * D:(par + 1) * D]
        ks = kt[:, par * D:(par + 1) * D]
        vs = vt[:, par * D:(par + 1) * D]
        ls = lt[:, par * D:(par + 1) * D]
        qreg = rg(qt, par * D, (par + 1) * D)
        kreg = rg(kt, par * D, (par + 1) * D)
        vreg = rg(vt, par * D, (par + 1) * D)
        lreg = rg(lt, par * D, (par + 1) * D)
        for oh in range(2):
            def mmc(e, oh=oh, ls=ls):
                return e.matmul(p.psum[2 + oh][:, :], u64[:, :],
                                ls[:, oh * 512:(oh + 1) * 512], start=True, stop=True)
            p.add("tensor", mmc, reads=[c.R(u64), lreg], writes=[p.pr(2 + oh)], late=True)
            p.add("scalar", lambda e, oh=oh: e.activation(
                out=bp[:, oh * 512:(oh + 1) * 512], in_=p.psum[2 + oh][:, :],
                func=AF.Identity),
                reads=[p.pr(2 + oh)], writes=[c.R(bp, oh * 512, (oh + 1) * 512)],
                late=True)
        for nch in range(8):
            def mtot(e, nch=nch, ls=ls):
                e.matmul(p.psum[4][:, 2 * nch:2 * nch + 1], ls[:, nch * 128:(nch + 1) * 128],
                         onesA[:, 0:1], start=True, stop=True)
                return e.matmul(p.psum[4][:, 2 * nch + 1:2 * nch + 2],
                                ls[:, nch * 128:(nch + 1) * 128],
                                onesA[:, 1:2], start=False, stop=True)
            p.add("tensor", mtot, reads=[lreg, c.R(onesA)], writes=[p.pr(4)], late=True)
        p.add("scalar", lambda e: e.activation(
            out=stot[:, 0:16], in_=p.psum[4][:, 0:16], func=AF.Identity),
            reads=[p.pr(4)], writes=[c.R(stot, 0, 16)], late=True)
        p.add("vector", lambda e: e.tensor_tensor(
            out=stot[:, 16:24].rearrange("p (a b) -> p a b", b=1),
            in0=stot[:, 0:16].rearrange("p (a b) -> p a b", b=2)[:, :, 0:1],
            in1=stot[:, 0:16].rearrange("p (a b) -> p a b", b=2)[:, :, 1:2], op=ALU.add),
            reads=[c.R(stot, 0, 16)], writes=[c.R(stot, 16, 24)], late=True)
        p.add("scalar", lambda e: e.activation(
            out=fcol[:, 0:16], in_=stot[:, 0:16], func=AF.Exp, scale=-1.0),
            reads=[c.R(stot, 0, 16)], writes=[c.R(fcol, 0, 16)], late=True)
        p.add("scalar", lambda e: e.activation(
            out=fcol[:, 16:24], in_=stot[:, 16:24], func=AF.Exp, scale=-1.0),
            reads=[c.R(stot, 16, 24)], writes=[c.R(fcol, 16, 24)], late=True)
        p.add("scalar", lambda e: e.activation(
            out=ehq[:, :], in_=bp[:, :], func=AF.Exp, scale=-1.0),
            reads=[c.R(bp, 0, D)], writes=[c.R(ehq)], late=True)
        p.add("vector", lambda e, qs=qs: e.tensor_tensor(
            out=qhat[:, :], in0=qs, in1=ehq[:, :], op=ALU.mult),
            reads=[qreg, c.R(ehq)], writes=[c.R(qhat)], late=True)
        p.add("scalar", lambda e: e.activation(
            out=ehq[:, :], in_=bp[:, :], func=AF.Exp),
            reads=[c.R(bp, 0, D)], writes=[c.R(ehq)], late=True)
        p.add("vector", lambda e, ks=ks: e.tensor_tensor(
            out=khat[:, :], in0=ks, in1=ehq[:, :], op=ALU.mult),
            reads=[kreg, c.R(ehq)], writes=[c.R(khat)], late=True)
        p.add("vector", lambda e: e.tensor_copy(khat1[:, :], khat.ap()[64:128, :]),
              reads=[c.R(khat)], writes=[c.R(khat1)], late=True)
        p.add("vector", lambda e, vs=vs: e.tensor_copy(v1[:, :], vs.ap()[64:128, :] if False else vs[64:128, :]),
              reads=[vreg], writes=[c.R(v1)], late=True)
        for src, dst, dreg in [(qhat, qhf, c.R(qhf)), (khat, khf, c.R(khf))]:
            for hb in range(0, 8, 4):
                bank = 6 + (hb // 4)

                def tpq(e, src=src, hb=hb, bank=bank):
                    for j in range(4):
                        ins = e.transpose(p.psum[bank].ap()[:, j * 128:(j + 1) * 128],
                                          src[:, (hb + j) * 128:(hb + j + 1) * 128],
                                          identb[:, :])
                    return ins
                p.add("tensor", tpq, reads=[c.R(src), c.R(identb)], writes=[p.pr(bank)],
                      late=True)
                p.add("vector", lambda e, dst=dst, hb=hb, bank=bank: e.tensor_copy(
                    dst[:, hb * 128:(hb + 4) * 128], p.psum[bank].ap()[:, 0:512]),
                    reads=[p.pr(bank)], writes=[dreg], late=True)
        for ncb in range(8):
            p.add("vector", lambda e, ncb=ncb: e.tensor_copy(
                qtl[:, ncb * 128: ncb * 128 + 64], qhf[:, ncb * 128: ncb * 128 + 64]),
                reads=[c.R(qhf)], writes=[c.R(qtl)], late=True)
            p.add("vector", lambda e, ncb=ncb: e.tensor_scalar(
                out=qtl[:, ncb * 128 + 64: ncb * 128 + 128],
                in0=qhf[:, ncb * 128 + 64: ncb * 128 + 128],
                scalar1=fcol[:, 2 * ncb:2 * ncb + 1], scalar2=None, op0=ALU.mult),
                reads=[c.R(qhf), c.R(fcol, 0, 16)], writes=[c.R(qtl)], late=True)
        def matA(e):
            for ncb in range(8):
                e.matmul(p.psum[2].ap()[0:64, 0:64], khf[:, ncb * 128:ncb * 128 + 64],
                         qhf[:, ncb * 128:ncb * 128 + 64],
                         start=(ncb == 0), stop=(ncb == 7))
            for ncb in range(8):
                e.matmul(p.psum[2].ap()[0:64, 64:128], khf[:, ncb * 128:ncb * 128 + 64],
                         qtl[:, ncb * 128 + 64:ncb * 128 + 128],
                         start=(ncb == 0), stop=(ncb == 7))
            for ncb in range(8):
                e.matmul(p.psum[3].ap()[0:64, 0:64], khf[:, ncb * 128 + 64:ncb * 128 + 128],
                         qhf[:, ncb * 128:ncb * 128 + 64],
                         start=(ncb == 0), stop=(ncb == 7))
            for ncb in range(8):
                ins = e.matmul(p.psum[3].ap()[0:64, 64:128],
                               khf[:, ncb * 128 + 64:ncb * 128 + 128],
                               qhf[:, ncb * 128 + 64:ncb * 128 + 128],
                               start=(ncb == 0), stop=(ncb == 7))
            return ins
        p.add("tensor", matA, reads=[c.R(khf), c.R(qhf), c.R(qtl)],
              writes=[p.pr(2), p.pr(3)], late=True)
        p.add("vector", lambda e: e.tensor_tensor(
            out=att.ap()[0:64, :], in0=p.psum[2].ap()[0:64, 0:128],
            in1=maskL.ap()[0:64, :], op=ALU.mult),
            reads=[p.pr(2), c.R(maskL)], writes=[c.R(att)], late=True)
        p.add("vector", lambda e: e.tensor_tensor(
            out=att.ap()[64:128, :], in0=p.psum[3].ap()[0:64, 0:128],
            in1=maskL.ap()[64:128, :], op=ALU.mult),
            reads=[p.pr(3), c.R(maskL)], writes=[c.R(att)], late=True)
        for oh in range(2):
            def mo(e, oh=oh, vs=vs):
                for ncb in range(8):
                    e.matmul(p.psum[oh][:, :], qtl[:, ncb * 128:(ncb + 1) * 128],
                             Sb[:, ncb * D + oh * 512: ncb * D + (oh + 1) * 512],
                             start=(ncb == 0), stop=False)
                return e.matmul(p.psum[oh][:, :], att[:, :],
                                vs[:, oh * 512:(oh + 1) * 512], start=False, stop=True)
            p.add("tensor", mo, reads=[c.R(qtl), c.R(wbuf), c.R(att), vreg],
                  writes=[p.pr(oh)], late=True)
            p.add("vector", lambda e, oh=oh: e.tensor_copy(
                obuf[:, oh * 512:(oh + 1) * 512], p.psum[oh][:, :]),
                reads=[p.pr(oh)], writes=[c.R(obuf, oh * 512, (oh + 1) * 512)], late=True)
        for hb in range(0, 8, 4):
            bank = 6 + (hb // 4)

            def tpo2(e, hb=hb, bank=bank):
                for j in range(4):
                    ins = e.transpose(p.psum[bank].ap()[:, j * 128:(j + 1) * 128],
                                      obuf[:, (hb + j) * 128:(hb + j + 1) * 128],
                                      identb[:, :])
                return ins
            p.add("tensor", tpo2, reads=[c.R(obuf), c.R(identb)], writes=[p.pr(bank)],
                  late=True)
            p.add("vector", lambda e, hb=hb, bank=bank: e.tensor_copy(
                ofm[:, hb * 128:(hb + 4) * 128], p.psum[bank].ap()[:, 0:512]),
                reads=[p.pr(bank)], writes=[c.R(ofm)], late=True)
        for oh in range(2):
            def mwo(e, oh=oh):
                for ic in range(DC):
                    ins = e.matmul(p.psum[2 + oh][:, :], ofm[:, ic * 128:(ic + 1) * 128],
                                   wob[:, ic * D + oh * 512: ic * D + (oh + 1) * 512],
                                   start=(ic == 0), stop=(ic == DC - 1))
                return ins
            p.add("tensor", mwo, reads=[c.R(ofm), (_SA, _SO, _SO + 16384)], writes=[p.pr(2 + oh)],
                  late=True)
            xreg = c.R(xs, ch * D + oh * 512, ch * D + (oh + 1) * 512)
            p.add("vector", lambda e, oh=oh, ch=ch: e.tensor_tensor(
                out=stmp[:, :], in0=p.psum[2 + oh][:, :],
                in1=bob[:, oh * 512:(oh + 1) * 512], op=ALU.add),
                reads=[p.pr(2 + oh), c.R(bob)], writes=[c.R(stmp)], late=True)
            p.add("vector", lambda e, oh=oh, ch=ch: e.tensor_tensor(
                out=xs[:, ch * D + oh * 512: ch * D + (oh + 1) * 512],
                in0=xs[:, ch * D + oh * 512: ch * D + (oh + 1) * 512],
                in1=stmp[:, :], op=ALU.add),
                reads=[xreg, c.R(stmp)], writes=[xreg], late=True)
        if dbg is not None and ch == 0:
            for buf, nm in [(qt, "dq"), (kt, "dk"), (vt, "dv")]:
                p.dma("sync", lambda e, buf=buf, nm=nm: e.dma_start(
                    out=dbg[nm][0:128, :], in_=buf[:, 0:D]),
                    reads=[rg(buf, 0, D)], group="dbg" + nm)
            p.dma("sync", lambda e: e.dma_start(out=dbg["dl"][0:128, :], in_=lt[:, 0:D]),
                  reads=[rg(lt, 0, D)], group="dbgdl")
            p.dma("sync", lambda e: e.dma_start(out=dbg["att0"], in_=att[:, :]),
                  reads=[c.R(att)], group="dbga")
            p.dma("sync", lambda e: e.dma_start(out=dbg["o0"], in_=obuf[:, :]),
                  reads=[c.R(obuf, 0, D)], group="dbgo")
            p.dma("sync", lambda e: e.dma_start(out=dbg["bp0"], in_=bp[:, :]),
                  reads=[c.R(bp, 0, D)], group="dbgbp")
            p.dma("sync", lambda e: e.dma_start(out=dbg["qh0"], in_=qhf[:, :]),
                  reads=[c.R(qhf)], group="dbgqh")
        for ncb in range(8):
            for oh in range(2):
                def mu(e, ncb=ncb, oh=oh, vs=vs):
                    e.matmul(p.psum[4][:, :], khat.ap()[0:64, ncb * 128:(ncb + 1) * 128],
                             vs[0:64, oh * 512:(oh + 1) * 512], start=True, stop=True)
                    return e.matmul(p.psum[5][:, :], khat1[:, ncb * 128:(ncb + 1) * 128],
                                    v1[:, oh * 512:(oh + 1) * 512], start=True, stop=True)
                p.add("tensor", mu, reads=[c.R(khat), c.R(khat1), vreg, c.R(v1)],
                      writes=[p.pr(4), p.pr(5)], late=True)
                sseg = (ncb * D + oh * 512, ncb * D + (oh + 1) * 512)
                sreg = SRr(sseg[0], sseg[1])
                p.add("vector", lambda e, sseg=sseg, ncb=ncb: e.tensor_scalar(
                    out=S[:, sseg[0]:sseg[1]], in0=S[:, sseg[0]:sseg[1]],
                    scalar1=fcol[:, 16 + ncb:17 + ncb], scalar2=None, op0=ALU.mult),
                    reads=[sreg, c.R(fcol, 16, 24)], writes=[sreg], late=True)
                p.add("vector", lambda e, sseg=sseg, ncb=ncb: e.scalar_tensor_tensor(
                    out=S[:, sseg[0]:sseg[1]], in0=p.psum[4][:, :],
                    scalar=fcol[:, 16 + ncb:17 + ncb], in1=S[:, sseg[0]:sseg[1]],
                    op0=ALU.mult, op1=ALU.add),
                    reads=[p.pr(4), sreg, c.R(fcol, 16, 24)], writes=[sreg], late=True)
                p.add("vector", lambda e, sseg=sseg, ncb=ncb: e.scalar_tensor_tensor(
                    out=S[:, sseg[0]:sseg[1]], in0=p.psum[5][:, :],
                    scalar=fcol[:, 2 * ncb + 1:2 * ncb + 2], in1=S[:, sseg[0]:sseg[1]],
                    op0=ALU.mult, op1=ALU.add),
                    reads=[p.pr(5), sreg, c.R(fcol, 0, 16)], writes=[sreg], late=True)
                p.add("vector", lambda e, sseg=sseg: e.tensor_copy(
                    Sb[:, sseg[0]:sseg[1]], S[:, sseg[0]:sseg[1]]),
                    reads=[sreg], writes=[rg(wbuf, sseg[0], sseg[1])], late=True)


def build_L1_attn_b(dbg=False):
    nc = bass.Bass("TRN2", target_bir_lowering=False)
    x = nc.dram_tensor("x", [T, D], F32, kind="ExternalInput").ap()
    aps = {}
    for wn in ["Wq", "Wk", "Wv", "Wg", "Wo"]:
        aps[wn] = nc.dram_tensor(wn, [D, D], F32, kind="ExternalInput").ap()
    for bn in ["bq", "bk", "bv", "bg", "bo"]:
        aps[bn] = nc.dram_tensor(bn, [D], F32, kind="ExternalInput").ap()
    if dbg:
        aps["_dbg"] = {
            "dq": nc.dram_tensor("dq", [T, D], BF16, kind="ExternalOutput").ap(),
            "dk": nc.dram_tensor("dk", [T, D], BF16, kind="ExternalOutput").ap(),
            "dv": nc.dram_tensor("dv", [T, D], BF16, kind="ExternalOutput").ap(),
            "dl": nc.dram_tensor("dl", [T, D], F32, kind="ExternalOutput").ap(),
            "att0": nc.dram_tensor("att0", [128, 128], BF16, kind="ExternalOutput").ap(),
            "o0": nc.dram_tensor("o0", [128, D], BF16, kind="ExternalOutput").ap(),
            "bp0": nc.dram_tensor("bp0", [128, D], F32, kind="ExternalOutput").ap(),
            "qh0": nc.dram_tensor("qh0", [128, D], BF16, kind="ExternalOutput").ap(),
        }
    y = nc.dram_tensor("y", [T, D], F32, kind="ExternalOutput").ap()
    with ExitStack() as stack:
        p = Prog(nc, stack)
        c = Ctx(p)
        emit_identity(p, c)
        emit_load_x(p, c, x)
        emit_ln_to_fm(p, c)
        emit_l1attnb(p, c, aps)
        emit_store_xs(p, c, y)
        with nc.Block() as block:
            p.emit(block)
    return nc



def emit_l0attnb(p, c, aps, uid=""):
    xs, stats, lnbuf, ident, identb = c.xs, c.stats, c.lnbuf, c.ident, c.identb
    nc = p.nc
    NH, HP, NS = 32, 64, 128
    DI = 2048
    xn_fm = c.x_fm
    _XF = "x_fm"
    xcd = nc.dram_tensor("l0b_xcd" + uid, [T, DI], BF16, kind="Internal").ap()
    szd = nc.dram_tensor("l0b_szd" + uid, [T, DI], BF16, kind="Internal").ap()

    nat = p.sb("l0b_nat" + uid, [128, 2 * D], BF16)
    wslc = p.sb("l0b_wslc" + uid, [128, 2 * DC * 128], BF16)
    xcf = p.sb("l0b_xcf" + uid, [128, T], F32)
    xcb = p.sb("l0b_xcb" + uid, [128, T], BF16)
    B_fm = p.sb("l0b_Bfm" + uid, [128, T], BF16)
    C_fm = p.sb("l0b_Cfm" + uid, [128, T], BF16)
    dtb = p.sb("l0b_dtb" + uid, [128, TT * NH], F32)
    lgb = p.sb("l0b_lgb" + uid, [128, TT * NH], F32)
    btm = p.sb("l0b_btm" + uid, [128, NH], F32)
    btotb = p.sb("l0b_btot" + uid, [128, 2 * NH], F32)
    hcols = p.sb("l0b_hcols" + uid, [128, 4 * NH], F32)
    u128 = p.sb("l0b_u128" + uid, [128, 128], F32)
    msk01 = p.sb("l0b_msk" + uid, [128, 128], F32)
    onesc = p.sb("l0b_ones" + uid, [128, 2], F32)
    onesrow = p.sb("l0b_onesr" + uid, [1, 128], F32)
    btrow = p.sb("l0b_btr" + uid, [1, NH], F32)
    brep = p.sb("l0b_brep" + uid, [128, 2 * 128], F32)
    Mt = p.sb("l0b_Mt" + uid, [128, 2 * 128], BF16)
    dtmp = p.sb("l0b_dtmp" + uid, [128, 2 * 128], F32)
    CBm = p.sb("l0b_CBm" + uid, [128, 128], F32)
    B_tm = p.sb("l0b_Btm" + uid, [128, 128], BF16)
    Bhat = p.sb("l0b_Bhat" + uid, [128, 2 * 128], BF16)
    t64 = p.sb("l0b_t64" + uid, [128, 2 * 64], F32)
    xh_tm = p.sb("l0b_xht" + uid, [128, 2 * DI], BF16)
    ybuf = p.sb("l0b_ybuf" + uid, [128, DI], BF16)
    szt = p.sb("l0b_szt" + uid, [128, 2 * DI], BF16)
    gfm = p.sb("l0b_gfm" + uid, [128, DI], BF16)
    S = p.sb("l0b_S" + uid, [128, NH * HP], F32)
    Sbf = p.sb("l0b_Sbf" + uid, [128, NH * HP], BF16)
    nwb = p.sb("l0b_nwb" + uid, [128, D], F32)
    cwb = p.sb("l0b_cwb" + uid, [128, 16 * 4], F32)
    cbb = p.sb("l0b_cbb" + uid, [128, 16], F32)
    eA = p.sb("l0b_eA" + uid, [128, 2 * NH], F32)
    dtbias = p.sb("l0b_dtbias" + uid, [128, NH], F32)
    tst = p.sb("l0b_tst" + uid, [128, 4], F32)

    p.add("gpsimd", lambda e: e.memset(u128[:, :], 1.0), writes=[c.R(u128)])
    p.add("gpsimd", lambda e: e.affine_select(
        out=u128[:, :], in_=u128[:, :], compare_op=ALU.is_ge, fill=0.0,
        base=0, pattern=[[1, 128]], channel_multiplier=-1),
        reads=[c.R(u128)], writes=[c.R(u128)], late=True)
    p.add("gpsimd", lambda e: e.memset(msk01[:, :], 1.0), writes=[c.R(msk01)])
    p.add("gpsimd", lambda e: e.affine_select(
        out=msk01[:, :], in_=msk01[:, :], compare_op=ALU.is_ge, fill=0.0,
        base=0, pattern=[[1, 128]], channel_multiplier=-1),
        reads=[c.R(msk01)], writes=[c.R(msk01)], late=True)
    p.add("gpsimd", lambda e: e.memset(onesc[:, :], 1.0), writes=[c.R(onesc)])
    p.add("gpsimd", lambda e: e.memset(onesrow[:, :], 1.0), writes=[c.R(onesrow)])
    p.dma("gpsimd", lambda e: e.dma_start(
        out=nwb[:, :], in_=aps["norm_w"][None, :].broadcast_to((128, D))),
        writes=[c.R(nwb)], group="l0b_nw" + uid)
    p.dma("gpsimd", lambda e: e.dma_start(
        out=cwb.ap().rearrange("p (cc k) -> p cc k", k=4),
        in_=aps["conv_w"].rearrange("(cc p) k -> p cc k", p=128)),
        writes=[c.R(cwb)], group="l0b_cw" + uid)
    p.dma("gpsimd", lambda e: e.dma_start(
        out=cbb[:, :], in_=aps["conv_b"].rearrange("(cc p) -> p cc", p=128),
        allow_slow_non_contiguous=True),
        writes=[c.R(cbb)], group="l0b_cb" + uid)
    if aps.get("_dbg") is not None:
        p.dma("sync", lambda e: e.dma_start(out=aps["_dbg"]["cwbE"], in_=cwb[:, :]),
              reads=[c.R(cwb)], group="dbg_cwbE")
    p.dma("gpsimd", lambda e: e.dma_start(
        out=eA[:, 0:NH], in_=aps["A_log"][None, :].broadcast_to((128, NH))),
        writes=[c.R(eA, 0, NH)], group="l0b_eA" + uid)
    p.add("scalar", lambda e: e.activation(
        out=eA[:, 0:NH], in_=eA[:, 0:NH], func=AF.Exp),
        reads=[c.R(eA, 0, NH)], writes=[c.R(eA, 0, NH)], late=True)
    p.dma("gpsimd", lambda e: e.dma_start(
        out=eA[:, NH:2 * NH], in_=aps["D"][None, :].broadcast_to((128, NH))),
        writes=[c.R(eA, NH, 2 * NH)], group="l0b_D" + uid)
    p.dma("gpsimd", lambda e: e.dma_start(
        out=dtbias[:, :], in_=aps["dt_bias"][None, :].broadcast_to((128, NH))),
        writes=[c.R(dtbias)], group="l0b_dtb" + uid)
    p.add("vector", lambda e: e.memset(S[:, :], 0.0), writes=[c.R(S)])
    p.add("vector", lambda e: e.memset(Sbf[:, :], 0.0), writes=[c.R(Sbf)])

    for t in range(TT):
        xreg = c.R(xs, t * D, (t + 1) * D)
        ureg = c.R(lnbuf, 0, D)
        p.add("scalar", lambda e, t=t: e.activation(
            out=lnbuf[:, :], in_=xs[:, t * D:(t + 1) * D], func=AF.Identity,
            scale=stats[:, t:t + 1], bias=stats[:, TT + t:TT + t + 1]),
            reads=[xreg, c.R(stats, 0, 2 * TT)], writes=[ureg], late=True)
        p.add("vector", lambda e, t=t: e.tensor_tensor(
            out=xs[:, t * D:(t + 1) * D], in0=xs[:, t * D:(t + 1) * D],
            in1=lnbuf[:, :], op=ALU.add),
            reads=[xreg, ureg], writes=[xreg], late=True)
        p.add("scalar", lambda e: e.activation(
            out=xcf[:, 0:D], in_=lnbuf[:, :], func=AF.Square, accum_out=tst[:, 0:1]),
            reads=[ureg], writes=[c.R(xcf, 0, D), c.R(tst, 0, 1)], late=True)
        p.add("vector", lambda e: e.tensor_scalar(
            out=tst[:, 0:1], in0=tst[:, 0:1], scalar1=1.0 / D, scalar2=1e-6,
            op0=ALU.mult, op1=ALU.add),
            reads=[c.R(tst, 0, 1)], writes=[c.R(tst, 0, 1)], late=True)
        p.add("scalar", lambda e: e.activation(
            out=tst[:, 0:1], in_=tst[:, 0:1], func=AF.Ln),
            reads=[c.R(tst, 0, 1)], writes=[c.R(tst, 0, 1)], late=True)
        p.add("scalar", lambda e: e.activation(
            out=tst[:, 1:2], in_=tst[:, 0:1], func=AF.Exp, scale=-0.5),
            reads=[c.R(tst, 0, 1)], writes=[c.R(tst, 1, 2)], late=True)
        p.add("scalar", lambda e: e.activation(
            out=xcf[:, 0:D], in_=lnbuf[:, :], func=AF.Identity, scale=tst[:, 1:2]),
            reads=[ureg, c.R(tst, 1, 2)], writes=[c.R(xcf, 0, D)], late=True)
        p.add("vector", lambda e: e.tensor_tensor(
            out=xcf[:, 0:D], in0=xcf[:, 0:D], in1=nwb[:, :], op=ALU.mult),
            reads=[c.R(xcf, 0, D), c.R(nwb)], writes=[c.R(xcf, 0, D)])
        emit_tm_tile_to_fm(p, c, xcf, c.R(xcf, 0, D), xn_fm, t, pref="l0b" + uid)

    xn_r = [(_XF, 2 * ic * T, 2 * (ic + 1) * T) for ic in range(DC)]

    def wT_slab(oc_idx, r0, rows, slot):
        sreg = rg(wslc, slot * DC * 128, (slot + 1) * DC * 128)
        nreg = c.R(nat, (slot % 2) * D, (slot % 2 + 1) * D)
        p.dma("gpsimd", lambda e, r0=r0, rows=rows, slot=slot: e.dma_start(
            out=nat[:, (slot % 2) * D:(slot % 2) * D + D][0:rows, :],
            in_=aps["in_proj_w"][r0:r0 + rows, :]),
            writes=[nreg], group="l0b_nat" + uid + str(slot % 2))
        for ic in range(DC):
            p.dma("sync", lambda e, ic=ic, rows=rows, slot=slot: e.dma_start_transpose(
                wslc[:, slot * DC * 128 + ic * 128: slot * DC * 128 + ic * 128 + rows],
                nat[0:rows, (slot % 2) * D + ic * 128:(slot % 2) * D + (ic + 1) * 128]),
                reads=[nreg], writes=[sreg], group="l0b_wsl" + uid + str(slot % 2))
        return sreg

    for oc in range(16):
        slot = oc % 2
        sreg = wT_slab(oc, 2048 + oc * 128, 128, slot)
        for tg in range(4):
            bank = tg % 2

            def mmx(e, tg=tg, slot=slot, bank=bank):
                for ic in range(DC):
                    ins = e.matmul(p.psum[bank][:, :],
                                   wslc[:, slot * DC * 128 + ic * 128: slot * DC * 128 + (ic + 1) * 128],
                                   xn_fm[:, ic * T + tg * 512: ic * T + (tg + 1) * 512],
                                   start=(ic == 0), stop=(ic == DC - 1))
                return ins
            p.add("tensor", mmx, reads=xn_r + [sreg], writes=[p.pr(bank)], late=True)
            p.add("scalar", lambda e, tg=tg, bank=bank: e.activation(
                out=xcf[:, tg * 512:(tg + 1) * 512], in_=p.psum[bank][:, :],
                func=AF.Identity),
                reads=[p.pr(bank)], writes=[c.R(xcf, tg * 512, (tg + 1) * 512)],
                late=True)
        if aps.get("_dbg") is not None and oc == 0:
            p.dma("sync", lambda e: e.dma_start(out=aps["_dbg"]["xcf0"], in_=xcf[:, :]),
                  reads=[c.R(xcf, 0, T)], group="dbg_xcf0")
            p.dma("sync", lambda e: e.dma_start(out=aps["_dbg"]["cwbL"], in_=cwb[:, :]),
                  reads=[c.R(cwb)], group="dbg_cwbL")
        accf = p.sb("l0b_accf" + uid, [128, T], F32)
        p.add("vector", lambda e, oc=oc: e.tensor_scalar(
            out=accf[:, :], in0=xcf[:, :], scalar1=cwb[:, oc * 4 + 3:oc * 4 + 4],
            scalar2=None, op0=ALU.mult),
            reads=[c.R(xcf, 0, T), c.R(cwb)], writes=[c.R(accf)], late=True)
        for k in range(3):
            sh = 3 - k
            p.add("vector", lambda e, k=k, sh=sh, oc=oc: e.scalar_tensor_tensor(
                out=accf[:, sh:T], in0=xcf[:, 0:T - sh],
                scalar=cwb[:, oc * 4 + k:oc * 4 + k + 1], in1=accf[:, sh:T],
                op0=ALU.mult, op1=ALU.add),
                reads=[c.R(xcf, 0, T), c.R(cwb), c.R(accf)], writes=[c.R(accf)],
                late=True)
        p.add("scalar", lambda e, oc=oc: e.activation(
            out=xcb[:, :], in_=accf[:, :], func=AF.Silu, bias=cbb[:, oc:oc + 1]),
            reads=[c.R(accf), c.R(cbb)], writes=[c.R(xcb)], late=True)
        if aps.get("_dbg") is not None and oc == 0:
            p.dma("sync", lambda e: e.dma_start(out=aps["_dbg"]["acc0"], in_=accf[:, :]),
                  reads=[c.R(accf)], group="dbg_acc0")
            p.dma("sync", lambda e: e.dma_start(out=aps["_dbg"]["xcb0"], in_=xcb[:, :]),
                  reads=[c.R(xcb)], group="dbg_xcb0")
        for tch in range(0, TT, 4):
            bank = 6 + (tch // 4) % 2

            def tpx(e, tch=tch, bank=bank):
                for j in range(4):
                    ins = e.transpose(p.psum[bank].ap()[:, j * 128:(j + 1) * 128],
                                      xcb[:, (tch + j) * 128:(tch + j + 1) * 128],
                                      identb[:, :])
                return ins
            p.add("tensor", tpx, reads=[c.R(xcb), c.R(identb)], writes=[p.pr(bank)],
                  late=True)
            p.add("vector", lambda e, tch=tch, bank=bank: e.tensor_copy(
                gfm[:, (tch % 8) * 128:(tch % 8) * 128 + 512] if False else
                xh_tm[:, (tch % 8) * 128:(tch % 8) * 128 + 512],
                p.psum[bank].ap()[:, 0:512]),
                reads=[p.pr(bank)],
                writes=[rg(xh_tm, (tch % 8) * 128, (tch % 8) * 128 + 512)],
                late=True)
            for j in range(4):
                p.dma("sync", lambda e, tch=tch, j=j, oc=oc: e.dma_start(
                    out=xcd[(tch + j) * 128:(tch + j + 1) * 128, oc * 128:(oc + 1) * 128],
                    in_=xh_tm[:, (tch % 8 + j) * 128:(tch % 8 + j + 1) * 128]),
                    reads=[rg(xh_tm, (tch % 8 + j) * 128, (tch % 8 + j + 1) * 128)],
                    writes=[("@l0bxcd" + uid, oc * TT + tch + j, oc * TT + tch + j + 1)],
                    group="l0b_spx" + uid)

    for nm, dst, r0 in [("B", B_fm, 4096), ("C", C_fm, 4224)]:
        slot = 0 if nm == "B" else 1
        sreg = wT_slab(0, r0, 128, slot)
        for tg in range(4):
            bank = tg % 2

            def mmb(e, tg=tg, slot=slot, bank=bank):
                for ic in range(DC):
                    ins = e.matmul(p.psum[bank][:, :],
                                   wslc[:, slot * DC * 128 + ic * 128: slot * DC * 128 + (ic + 1) * 128],
                                   xn_fm[:, ic * T + tg * 512: ic * T + (tg + 1) * 512],
                                   start=(ic == 0), stop=(ic == DC - 1))
                return ins
            p.add("tensor", mmb, reads=xn_r + [sreg], writes=[p.pr(bank)], late=True)
            p.add("vector", lambda e, tg=tg, bank=bank, dst=dst: e.tensor_copy(
                dst[:, tg * 512:(tg + 1) * 512], p.psum[bank][:, :]),
                reads=[p.pr(bank)], writes=[c.R(dst, tg * 512, (tg + 1) * 512)],
                late=True)

    sreg = wT_slab(0, 4352, 32, 0)
    for t in range(TT):
        bank = t % 2

        def mmdt(e, t=t, bank=bank):
            for ic in range(DC):
                ins = e.matmul(p.psum[bank][:, 0:32],
                               xn_fm[:, ic * T + t * 128: ic * T + (t + 1) * 128],
                               wslc[:, ic * 128: ic * 128 + 32],
                               start=(ic == 0), stop=(ic == DC - 1))
            return ins
        p.add("tensor", mmdt, reads=xn_r + [sreg], writes=[p.pr(bank)], late=True)
        p.add("scalar", lambda e, t=t, bank=bank: e.activation(
            out=dtb[:, t * NH:(t + 1) * NH], in_=p.psum[bank][:, 0:32], func=AF.Identity),
            reads=[p.pr(bank)], writes=[c.R(dtb, t * NH, (t + 1) * NH)], late=True)
    allr = c.R(dtb, 0, TT * NH)
    allg = c.R(lgb, 0, TT * NH)
    p.add("vector", lambda e: e.tensor_tensor(
        out=dtb[:, :].rearrange("p (t h) -> p t h", h=NH), in0=dtb[:, :].rearrange("p (t h) -> p t h", h=NH),
        in1=dtbias[:, :].rearrange("p (a h) -> p a h", a=1).broadcast_to((128, TT, NH)),
        op=ALU.add),
        reads=[allr, c.R(dtbias)], writes=[allr], late=True)
    p.add("scalar", lambda e: e.activation(
        out=lgb[:, :], in_=dtb[:, :], func=AF.Abs),
        reads=[allr], writes=[allg], late=True)
    p.add("scalar", lambda e: e.activation(
        out=lgb[:, :], in_=lgb[:, :], func=AF.Exp, scale=-1.0),
        reads=[allg], writes=[allg], late=True)
    p.add("scalar", lambda e: e.activation(
        out=lgb[:, :], in_=lgb[:, :], func=AF.Ln, bias=1.0),
        reads=[allg], writes=[allg], late=True)
    p.add("scalar", lambda e: e.activation(
        out=dtb[:, :], in_=dtb[:, :], func=AF.Relu),
        reads=[allr], writes=[allr], late=True)
    p.add("vector", lambda e: e.tensor_tensor(
        out=dtb[:, :], in0=dtb[:, :], in1=lgb[:, :], op=ALU.add),
        reads=[allr, allg], writes=[allr], late=True)
    p.add("vector", lambda e: e.tensor_tensor(
        out=lgb[:, :].rearrange("p (t h) -> p t h", h=NH),
        in0=dtb[:, :].rearrange("p (t h) -> p t h", h=NH),
        in1=eA[:, 0:NH].rearrange("p (a h) -> p a h", a=1).broadcast_to((128, TT, NH)),
        op=ALU.mult),
        reads=[allr, c.R(eA, 0, NH)], writes=[allg], late=True)

    for oc in range(16):
        slot = oc % 2
        sreg = wT_slab(oc, 0 + oc * 128, 128, slot)
        for tg in range(4):
            bank = tg % 2

            def mmz(e, tg=tg, slot=slot, bank=bank):
                for ic in range(DC):
                    ins = e.matmul(p.psum[bank][:, :],
                                   wslc[:, slot * DC * 128 + ic * 128: slot * DC * 128 + (ic + 1) * 128],
                                   xn_fm[:, ic * T + tg * 512: ic * T + (tg + 1) * 512],
                                   start=(ic == 0), stop=(ic == DC - 1))
                return ins
            p.add("tensor", mmz, reads=xn_r + [sreg], writes=[p.pr(bank)], late=True)
            p.add("scalar", lambda e, tg=tg, bank=bank: e.activation(
                out=xcb[:, tg * 512:(tg + 1) * 512], in_=p.psum[bank][:, :], func=AF.Silu),
                reads=[p.pr(bank)], writes=[c.R(xcb, tg * 512, (tg + 1) * 512)],
                late=True)
        for tch in range(0, TT, 4):
            bank = 6 + (tch // 4) % 2

            def tpz(e, tch=tch, bank=bank):
                for j in range(4):
                    ins = e.transpose(p.psum[bank].ap()[:, j * 128:(j + 1) * 128],
                                      xcb[:, (tch + j) * 128:(tch + j + 1) * 128],
                                      identb[:, :])
                return ins
            p.add("tensor", tpz, reads=[c.R(xcb), c.R(identb)], writes=[p.pr(bank)],
                  late=True)
            p.add("vector", lambda e, tch=tch, bank=bank: e.tensor_copy(
                xh_tm[:, (tch % 8) * 128:(tch % 8) * 128 + 512],
                p.psum[bank].ap()[:, 0:512]),
                reads=[p.pr(bank)],
                writes=[rg(xh_tm, (tch % 8) * 128, (tch % 8) * 128 + 512)],
                late=True)
            for j in range(4):
                p.dma("sync", lambda e, tch=tch, j=j, oc=oc: e.dma_start(
                    out=szd[(tch + j) * 128:(tch + j + 1) * 128, oc * 128:(oc + 1) * 128],
                    in_=xh_tm[:, (tch % 8 + j) * 128:(tch % 8 + j + 1) * 128]),
                    reads=[rg(xh_tm, (tch % 8 + j) * 128, (tch % 8 + j + 1) * 128)],
                    writes=[("@l0bszd" + uid, oc * TT + tch + j, oc * TT + tch + j + 1)],
                    group="l0b_spz" + uid)

    wopT = c.x_fm
    for oc in range(DC):
        slot = oc % 2
        nreg = c.R(nat, slot * D, (slot + 1) * D)
        p.dma("gpsimd", lambda e, oc=oc, slot=slot: e.dma_start(
            out=nat.ap()[:, slot * D:(slot + 1) * D].rearrange("p (a i) -> p a i", a=1)[:, 0, :],
            in_=aps["out_proj_w"][oc * 128:(oc + 1) * 128, 0:1024]),
            writes=[nreg], group="l0b_wo1" + uid + str(slot))
        for ic in range(8):
            p.dma2("sync", lambda e, oc=oc, ic=ic, slot=slot: e.dma_start_transpose(
                wopT.ap()[:, ic * 1024 + oc * 128: ic * 1024 + (oc + 1) * 128],
                nat[:, slot * D + ic * 128: slot * D + (ic + 1) * 128]),
                reads=[nreg], writes=[(_XF, 2 * (ic * 1024 + oc * 128), 2 * (ic * 1024 + (oc + 1) * 128))],
                group="l0b_woT" + uid)
    nat2 = p.sb("l0b_nat2" + uid, [128, 2 * D], BF16)
    for oc in range(DC):
        slot = oc % 2
        nreg = c.R(nat2, slot * D, (slot + 1) * D)
        p.dma("gpsimd", lambda e, oc=oc, slot=slot: e.dma_start(
            out=nat2[:, slot * D:(slot + 1) * D],
            in_=aps["out_proj_w"][oc * 128:(oc + 1) * 128, 1024:2048]),
            writes=[nreg], group="l0b_wo2" + uid + str(slot))
        for ic in range(8):
            p.dma2("sync", lambda e, oc=oc, ic=ic, slot=slot: e.dma_start_transpose(
                wopT.ap()[:, (8 + ic) * 1024 + oc * 128: (8 + ic) * 1024 + (oc + 1) * 128],
                nat2[:, slot * D + ic * 128: slot * D + (ic + 1) * 128]),
                reads=[nreg],
                writes=[(_XF, 2 * ((8 + ic) * 1024 + oc * 128), 2 * ((8 + ic) * 1024 + (oc + 1) * 128))],
                group="l0b_woT" + uid)
    wop_r = (_XF, 0, 2 * 16 * 1024)

    dbg = aps.get("_dbg")
    if dbg is not None:
        p.dma("sync", lambda e: e.dma_start(out=dbg["dtb"], in_=dtb[:, :]),
              reads=[allr], group="dbg_dtb")
        p.dma("sync", lambda e: e.dma_start(out=dbg["lgb"], in_=lgb[:, :]),
              reads=[allg], group="dbg_lgb")
        p.dma("sync", lambda e: e.dma_start(out=dbg["Bfm"], in_=B_fm[:, :]),
              reads=[c.R(B_fm)], group="dbg_Bfm")
        p.dma("sync", lambda e: e.dma_start(out=dbg["Cfm"], in_=C_fm[:, :]),
              reads=[c.R(C_fm)], group="dbg_Cfm")

    for ch in range(TT):
        par = ch % 2
        xseg = (par * DI, (par + 1) * DI)
        xh_r = rg(xh_tm, xseg[0], xseg[1])
        p.dma2("sync", lambda e, ch=ch, par=par: e.dma_start(
            out=xh_tm[:, par * DI:(par + 1) * DI],
            in_=xcd[ch * 128:(ch + 1) * 128, :]),
            reads=[("@l0bxcd" + uid, 15 * TT + 15, 15 * TT + 16)],
            writes=[xh_r], group="l0b_ldx" + uid + str(par))
        p.dma2("sync", lambda e, ch=ch, par=par: e.dma_start(
            out=szt[:, par * DI:(par + 1) * DI],
            in_=szd[ch * 128:(ch + 1) * 128, :]),
            reads=[("@l0bszd" + uid, 15 * TT + 15, 15 * TT + 16)],
            writes=[rg(szt, xseg[0], xseg[1])], group="l0b_ldz" + uid + str(par))
        lgs = lgb[:, ch * NH:(ch + 1) * NH]
        lgr = c.R(lgb, ch * NH, (ch + 1) * NH)
        def mcs(e, lgs=lgs):
            return e.matmul(p.psum[2][:, 0:NH], u128[:, :], lgs, start=True, stop=True)
        p.add("tensor", mcs, reads=[c.R(u128), lgr], writes=[p.pr(2)], late=True)
        p.add("scalar", lambda e: e.activation(
            out=btm[:, :], in_=p.psum[2][:, 0:NH], func=AF.Identity),
            reads=[p.pr(2)], writes=[c.R(btm)], late=True)

        def mts(e, lgs=lgs):
            return e.matmul(p.psum[2].ap()[0:1, 64:64 + NH], onesc[:, 0:1], lgs,
                            start=True, stop=True)
        p.add("tensor", mts, reads=[lgr, c.R(onesc)], writes=[p.pr(2)], late=True)
        p.add("scalar", lambda e: e.activation(
            out=btrow[:, :], in_=p.psum[2].ap()[0:1, 64:64 + NH], func=AF.Identity),
            reads=[p.pr(2)], writes=[c.R(btrow)], late=True)

        def mbc2(e):
            return e.matmul(p.psum[3][:, 0:NH], onesrow[:, :], btrow[:, :],
                            start=True, stop=True)
        p.add("tensor", mbc2, reads=[c.R(onesrow), c.R(btrow)],
              writes=[p.pr(3)], late=True)
        p.add("scalar", lambda e: e.activation(
            out=btotb[:, 0:NH], in_=p.psum[3][:, 0:NH], func=AF.Identity),
            reads=[p.pr(3)], writes=[c.R(btotb, 0, NH)], late=True)
        p.add("vector", lambda e: e.tensor_scalar(
            out=btotb[:, NH:2 * NH], in0=btotb[:, 0:NH], scalar1=-1.0, scalar2=None,
            op0=ALU.mult),
            reads=[c.R(btotb, 0, NH)], writes=[c.R(btotb, NH, 2 * NH)], late=True)
        def mcb(e, ch=ch):
            return e.matmul(p.psum[4][:, 0:128], B_fm[:, ch * 128:(ch + 1) * 128],
                            C_fm[:, ch * 128:(ch + 1) * 128], start=True, stop=True)
        p.add("tensor", mcb, reads=[c.R(B_fm), c.R(C_fm)], writes=[p.pr(4)], late=True)
        p.add("vector", lambda e: e.tensor_tensor(
            out=CBm[:, :], in0=p.psum[4][:, 0:128], in1=msk01[:, :], op=ALU.mult),
            reads=[p.pr(4), c.R(msk01)], writes=[c.R(CBm)], late=True)
        p.add("scalar", lambda e: e.activation(
            out=hcols[:, 0:NH], in_=btm[:, :], func=AF.Exp, scale=-1.0),
            reads=[c.R(btm)], writes=[c.R(hcols, 0, NH)], late=True)
        p.add("vector", lambda e: e.tensor_tensor(
            out=hcols[:, NH:2 * NH], in0=btm[:, :], in1=btotb[:, 0:NH], op=ALU.subtract),
            reads=[c.R(btm), c.R(btotb, 0, NH)], writes=[c.R(hcols, NH, 2 * NH)],
            late=True)
        p.add("scalar", lambda e: e.activation(
            out=hcols[:, NH:2 * NH], in_=hcols[:, NH:2 * NH], func=AF.Exp),
            reads=[c.R(hcols, NH, 2 * NH)], writes=[c.R(hcols, NH, 2 * NH)], late=True)
        p.add("vector", lambda e, ch=ch: e.tensor_tensor(
            out=hcols[:, 2 * NH:3 * NH], in0=hcols[:, NH:2 * NH],
            in1=dtb[:, ch * NH:(ch + 1) * NH], op=ALU.mult),
            reads=[c.R(hcols, NH, 2 * NH), allr], writes=[c.R(hcols, 2 * NH, 3 * NH)],
            late=True)
        p.add("scalar", lambda e: e.activation(
            out=hcols[:, 3 * NH:4 * NH], in_=btotb[:, 0:NH], func=AF.Exp, scale=-1.0),
            reads=[c.R(btotb, 0, NH)], writes=[c.R(hcols, 3 * NH, 4 * NH)], late=True)
        def tpb(e, ch=ch):
            return e.transpose(p.psum[6].ap()[:, 0:128],
                               B_fm[:, ch * 128:(ch + 1) * 128], identb[:, :])
        p.add("tensor", tpb, reads=[c.R(B_fm), c.R(identb)], writes=[p.pr(6)], late=True)
        p.add("vector", lambda e: e.tensor_copy(B_tm[:, :], p.psum[6].ap()[:, 0:128]),
              reads=[p.pr(6)], writes=[c.R(B_tm)], late=True)
        for h in range(NH):
            w = h % 2
            W0, W1 = w * 128, (w + 1) * 128
            bcol = btm[:, h:h + 1]
            dtc = dtb[:, ch * NH + h: ch * NH + h + 1]
            brep_s = brep[:, W0:W1]
            brep_r = c.R(brep, W0, W1)
            dtmp_s = dtmp[:, W0:W1]
            dtmp_r = c.R(dtmp, W0, W1)
            Mt_s = Mt[:, W0:W1]
            Mt_r = c.R(Mt, W0, W1)
            Bh_s = Bhat[:, W0:W1]
            Bh_r = c.R(Bhat, W0, W1)
            t64_s = t64[:, w * 64:(w + 1) * 64]
            t64_r = c.R(t64, w * 64, (w + 1) * 64)
            bI, bA, bS = (0, 1, 5) if w == 0 else (2, 3, 5)
            p.add("vector", lambda e, bcol=bcol, brep_s=brep_s, brep_r=brep_r:
                  e.tensor_copy(brep_s, bcol.broadcast_to((128, 128))),
                  reads=[c.R(btm)], writes=[brep_r], late=True)

            def tpr(e, brep_s=brep_s, w=w):
                return e.transpose(p.psum[4].ap()[:, w * 128:(w + 1) * 128], brep_s,
                                   ident[:, :])
            p.add("tensor", tpr, reads=[brep_r, c.R(ident)], writes=[p.pr(4)],
                  late=True)
            p.add("vector", lambda e, bcol=bcol, dtmp_s=dtmp_s, w=w:
                  e.scalar_tensor_tensor(
                      out=dtmp_s, in0=p.psum[4].ap()[:, w * 128:(w + 1) * 128],
                      scalar=bcol, in1=msk01[:, :], op0=ALU.subtract, op1=ALU.mult),
                  reads=[p.pr(4), c.R(btm), c.R(msk01)], writes=[dtmp_r], late=True)
            p.add("scalar", lambda e, dtmp_s=dtmp_s: e.activation(
                out=dtmp_s, in_=dtmp_s, func=AF.Exp, scale=-1.0),
                reads=[dtmp_r], writes=[dtmp_r], late=True)
            p.add("vector", lambda e, dtc=dtc, dtmp_s=dtmp_s, Mt_s=Mt_s:
                  e.scalar_tensor_tensor(
                      out=Mt_s, in0=dtmp_s, scalar=dtc, in1=CBm[:, :],
                      op0=ALU.mult, op1=ALU.mult),
                  reads=[dtmp_r, allr, c.R(CBm)], writes=[Mt_r], late=True)

            def myi(e, ch=ch, h=h, bI=bI):
                return e.matmul(p.psum[bI][:, 0:HP], C_fm[:, ch * 128:(ch + 1) * 128],
                                Sbf[:, h * HP:(h + 1) * HP], start=True, stop=True)
            p.add("tensor", myi, reads=[c.R(C_fm), c.R(Sbf, h * HP, (h + 1) * HP)],
                  writes=[p.pr(bI)], late=True)

            def mya(e, h=h, par=par, Mt_s=Mt_s, bA=bA):
                return e.matmul(p.psum[bA][:, 0:HP], Mt_s,
                                xh_tm[:, par * DI + h * HP: par * DI + (h + 1) * HP],
                                start=True, stop=True)
            p.add("tensor", mya, reads=[Mt_r, xh_r], writes=[p.pr(bA)], late=True)
            p.add("scalar", lambda e, h=h, t64_s=t64_s, bI=bI: e.activation(
                out=t64_s, in_=p.psum[bI][:, 0:HP], func=AF.Identity,
                scale=hcols[:, h:h + 1]),
                reads=[p.pr(bI), c.R(hcols, 0, NH)], writes=[t64_r], late=True)
            p.add("vector", lambda e, t64_s=t64_s, bA=bA: e.tensor_tensor(
                out=t64_s, in0=t64_s, in1=p.psum[bA][:, 0:HP], op=ALU.add),
                reads=[t64_r, p.pr(bA)], writes=[t64_r], late=True)
            p.add("vector", lambda e, h=h, par=par, t64_s=t64_s: e.scalar_tensor_tensor(
                out=ybuf[:, h * HP:(h + 1) * HP],
                in0=xh_tm[:, par * DI + h * HP: par * DI + (h + 1) * HP],
                scalar=eA[:, NH + h:NH + h + 1], in1=t64_s,
                op0=ALU.mult, op1=ALU.add),
                reads=[xh_r, c.R(eA, NH, 2 * NH), t64_r],
                writes=[c.R(ybuf, h * HP, (h + 1) * HP)], late=True)
            p.add("vector", lambda e, h=h, Bh_s=Bh_s: e.tensor_scalar(
                out=Bh_s, in0=B_tm[:, :], scalar1=hcols[:, 2 * NH + h:2 * NH + h + 1],
                scalar2=None, op0=ALU.mult),
                reads=[c.R(B_tm), c.R(hcols, 2 * NH, 3 * NH)], writes=[Bh_r], late=True)

            def mst(e, h=h, par=par, Bh_s=Bh_s, w=w):
                return e.matmul(p.psum[5].ap()[:, w * 256:w * 256 + HP], Bh_s,
                                xh_tm[:, par * DI + h * HP: par * DI + (h + 1) * HP],
                                start=True, stop=True)
            p.add("tensor", mst, reads=[Bh_r, xh_r], writes=[p.pr(5)], late=True)
            sreg2 = c.R(S, h * HP, (h + 1) * HP)
            p.add("vector", lambda e, h=h: e.tensor_scalar(
                out=S[:, h * HP:(h + 1) * HP], in0=S[:, h * HP:(h + 1) * HP],
                scalar1=hcols[:, 3 * NH + h:3 * NH + h + 1], scalar2=None, op0=ALU.mult),
                reads=[sreg2, c.R(hcols, 3 * NH, 4 * NH)], writes=[sreg2], late=True)
            p.add("vector", lambda e, h=h, w=w: e.tensor_tensor(
                out=S[:, h * HP:(h + 1) * HP], in0=S[:, h * HP:(h + 1) * HP],
                in1=p.psum[5].ap()[:, w * 256:w * 256 + HP], op=ALU.add),
                reads=[sreg2, p.pr(5)], writes=[sreg2], late=True)
            p.add("vector", lambda e, h=h: e.tensor_copy(
                Sbf[:, h * HP:(h + 1) * HP], S[:, h * HP:(h + 1) * HP]),
                reads=[sreg2], writes=[c.R(Sbf, h * HP, (h + 1) * HP)], late=True)
        if dbg is not None and ch == 0:
            p.dma("sync", lambda e: e.dma_start(out=dbg["xh0"], in_=xh_tm[:, 0:DI]),
                  reads=[xh_r], group="dbg_xh0")
            p.dma("sync", lambda e: e.dma_start(out=dbg["sz0"], in_=szt[:, 0:DI]),
                  reads=[rg(szt, 0, DI)], group="dbg_sz0")
            p.dma("sync", lambda e: e.dma_start(out=dbg["y0"], in_=ybuf[:, :]),
                  reads=[c.R(ybuf, 0, DI)], group="dbg_y0")
            p.dma("sync", lambda e: e.dma_start(out=dbg["btm0"], in_=btm[:, :]),
                  reads=[c.R(btm)], group="dbg_btm0")
            p.dma("sync", lambda e: e.dma_start(out=dbg["CBm0"], in_=CBm[:, :]),
                  reads=[c.R(CBm)], group="dbg_CBm0")
            p.dma("sync", lambda e: e.dma_start(out=dbg["S0"], in_=S[:, :]),
                  reads=[c.R(S, 0, NH * HP)], group="dbg_S0")
        p.add("vector", lambda e, par=par: e.tensor_tensor(
            out=ybuf[:, :], in0=ybuf[:, :], in1=szt[:, par * DI:(par + 1) * DI],
            op=ALU.mult),
            reads=[c.R(ybuf, 0, DI), rg(szt, xseg[0], xseg[1])],
            writes=[c.R(ybuf, 0, DI)], late=True)
        for hb in range(0, 16, 4):
            bank = 6 + (hb // 4) % 2

            def tpg(e, hb=hb, bank=bank):
                for j in range(4):
                    ins = e.transpose(p.psum[bank].ap()[:, j * 128:(j + 1) * 128],
                                      ybuf[:, (hb + j) * 128:(hb + j + 1) * 128],
                                      identb[:, :])
                return ins
            p.add("tensor", tpg, reads=[c.R(ybuf, 0, DI), c.R(identb)],
                  writes=[p.pr(bank)], late=True)
            p.add("vector", lambda e, hb=hb, bank=bank: e.tensor_copy(
                gfm[:, hb * 128:(hb + 4) * 128], p.psum[bank].ap()[:, 0:512]),
                reads=[p.pr(bank)], writes=[c.R(gfm, hb * 128, (hb + 4) * 128)],
                late=True)
        for oh in range(2):
            def mwo(e, oh=oh):
                for ic in range(16):
                    ins = e.matmul(p.psum[2 + oh][:, :], gfm[:, ic * 128:(ic + 1) * 128],
                                   wopT.ap()[:, ic * 1024 + oh * 512: ic * 1024 + oh * 512 + 512],
                                   start=(ic == 0), stop=(ic == 15))
                return ins
            p.add("tensor", mwo, reads=[c.R(gfm, 0, DI), wop_r], writes=[p.pr(2 + oh)],
                  late=True)
            xreg = c.R(xs, ch * D + oh * 512, ch * D + (oh + 1) * 512)
            p.add("vector", lambda e, oh=oh, ch=ch: e.tensor_tensor(
                out=xs[:, ch * D + oh * 512: ch * D + (oh + 1) * 512],
                in0=xs[:, ch * D + oh * 512: ch * D + (oh + 1) * 512],
                in1=p.psum[2 + oh][:, :], op=ALU.add),
                reads=[xreg, p.pr(2 + oh)], writes=[xreg], late=True)


def build_L0_attn_b(dbg=False):
    nc = bass.Bass("TRN2", target_bir_lowering=False)
    x = nc.dram_tensor("x", [T, D], F32, kind="ExternalInput").ap()
    aps = {
        "norm_w": nc.dram_tensor("norm_w", [D], F32, kind="ExternalInput").ap(),
        "in_proj_w": nc.dram_tensor("in_proj_w", [4384, D], F32, kind="ExternalInput").ap(),
        "conv_w": nc.dram_tensor("conv_w", [2048, 4], F32, kind="ExternalInput").ap(),
        "conv_b": nc.dram_tensor("conv_b", [2048], F32, kind="ExternalInput").ap(),
        "A_log": nc.dram_tensor("A_log", [32], F32, kind="ExternalInput").ap(),
        "D": nc.dram_tensor("D", [32], F32, kind="ExternalInput").ap(),
        "dt_bias": nc.dram_tensor("dt_bias", [32], F32, kind="ExternalInput").ap(),
        "out_proj_w": nc.dram_tensor("out_proj_w", [D, 2048], F32, kind="ExternalInput").ap(),
    }
    if dbg:
        aps["_dbg"] = {
            "dtb": nc.dram_tensor("ddtb", [128, 512], F32, kind="ExternalOutput").ap(),
            "lgb": nc.dram_tensor("dlgb", [128, 512], F32, kind="ExternalOutput").ap(),
            "Bfm": nc.dram_tensor("dBfm", [128, 2048], BF16, kind="ExternalOutput").ap(),
            "Cfm": nc.dram_tensor("dCfm", [128, 2048], BF16, kind="ExternalOutput").ap(),
            "xh0": nc.dram_tensor("dxh0", [128, 2048], BF16, kind="ExternalOutput").ap(),
            "sz0": nc.dram_tensor("dsz0", [128, 2048], BF16, kind="ExternalOutput").ap(),
            "y0": nc.dram_tensor("dy0", [128, 2048], BF16, kind="ExternalOutput").ap(),
            "btm0": nc.dram_tensor("dbtm0", [128, 32], F32, kind="ExternalOutput").ap(),
            "CBm0": nc.dram_tensor("dCBm0", [128, 128], F32, kind="ExternalOutput").ap(),
            "S0": nc.dram_tensor("dS0", [128, 2048], F32, kind="ExternalOutput").ap(),
            "xcf0": nc.dram_tensor("dxcf0", [128, 2048], F32, kind="ExternalOutput").ap(),
            "acc0": nc.dram_tensor("dacc0", [128, 2048], F32, kind="ExternalOutput").ap(),
            "xcb0": nc.dram_tensor("dxcb0", [128, 2048], BF16, kind="ExternalOutput").ap(),
            "cwbE": nc.dram_tensor("dcwbE", [128, 64], F32, kind="ExternalOutput").ap(),
            "cwbL": nc.dram_tensor("dcwbL", [128, 64], F32, kind="ExternalOutput").ap(),
        }
    y = nc.dram_tensor("y", [T, D], F32, kind="ExternalOutput").ap()
    with ExitStack() as stack:
        p = Prog(nc, stack)
        c = Ctx(p)
        emit_identity(p, c)
        emit_load_x(p, c, x)
        emit_ln_stats_only(p, c)
        emit_l0attnb(p, c, aps)
        emit_store_xs(p, c, y)
        with nc.Block() as block:
            p.emit(block)
    return nc


def rope_tables():
    s, half = T, D // 2
    theta = 1.0 / 10000.0 ** (np.arange(half, dtype=np.float64) / half)
    ang = np.arange(s, dtype=np.float64)[:, None] * theta[None, :]
    return (np.cos(ang).astype(np.float32), np.sin(ang).astype(np.float32))


def build_L0_attn_a():
    nc = bass.Bass("TRN2", target_bir_lowering=False)
    x = nc.dram_tensor("x", [T, D], F32, kind="ExternalInput").ap()
    aps = {
        "W_q": nc.dram_tensor("W_q", [D, D], F32, kind="ExternalInput").ap(),
        "W_k": nc.dram_tensor("W_k", [256, D], F32, kind="ExternalInput").ap(),
        "W_v": nc.dram_tensor("W_v", [256, D], F32, kind="ExternalInput").ap(),
        "W_o": nc.dram_tensor("W_o", [D, D], F32, kind="ExternalInput").ap(),
        "rope_cos": nc.dram_tensor("rope_cos", [T, 512], F32, kind="ExternalInput").ap(),
        "rope_sin": nc.dram_tensor("rope_sin", [T, 512], F32, kind="ExternalInput").ap(),
    }
    y = nc.dram_tensor("y", [T, D], F32, kind="ExternalOutput").ap()
    with ExitStack() as stack:
        p = Prog(nc, stack)
        c = Ctx(p)
        emit_identity(p, c)
        emit_load_x(p, c, x)
        emit_ln_stats_only(p, c)
        emit_l0attna(p, c, aps)
        emit_store_xs(p, c, y)
        with nc.Block() as block:
            p.emit(block)
    return nc


def build_L0_ffn():
    nc = bass.Bass("TRN2", target_bir_lowering=False)
    x = nc.dram_tensor("x", [T, D], F32, kind="ExternalInput").ap()
    aps = {
        "w_shared_in": nc.dram_tensor("w_shared_in", [D, 2048], F32, kind="ExternalInput").ap(),
        "w_shared_out": nc.dram_tensor("w_shared_out", [1024, D], F32, kind="ExternalInput").ap(),
        "w1_shared": nc.dram_tensor("w1_shared", [D, 2048], F32, kind="ExternalInput").ap(),
        "w2": nc.dram_tensor("w2", [8, 1024, D], F32, kind="ExternalInput").ap(),
        "w_gate": nc.dram_tensor("w_gate", [D, 8], F32, kind="ExternalInput").ap(),
        "gate_bias": nc.dram_tensor("gate_bias", [8], F32, kind="ExternalInput").ap(),
    }
    y = nc.dram_tensor("y", [T, D], F32, kind="ExternalOutput").ap()

    with ExitStack() as stack:
        p = Prog(nc, stack)
        c = Ctx(p)
        emit_identity(p, c)
        emit_load_x(p, c, x)
        emit_ln_to_fm(p, c)
        emit_l0ffn_full(p, c, aps)
        emit_store_xs(p, c, y)
        with nc.Block() as block:
            p.emit(block)
    return nc


def build_fused():
    nc = bass.Bass("TRN2", target_bir_lowering=False)
    x = nc.dram_tensor("x", [T, D], F32, kind="ExternalInput").ap()
    y = nc.dram_tensor("y", [T, D], F32, kind="ExternalOutput").ap()

    def dr(pref, name, shape):
        return nc.dram_tensor(pref + name, shape, F32, kind="ExternalInput").ap()

    a_l0a = {n: dr("l0a_", n, s) for n, s in [
        ("W_q", [D, D]), ("W_k", [256, D]), ("W_v", [256, D]), ("W_o", [D, D]),
        ("rope_cos", [T, 512]), ("rope_sin", [T, 512])]}
    a_l0f = {n: dr("l0f_", n, s) for n, s in [
        ("w_shared_in", [D, 2048]), ("w_shared_out", [1024, D]),
        ("w1_shared", [D, 2048]), ("w2", [8, 1024, D]), ("w_gate", [D, 8]),
        ("gate_bias", [8])]}
    a_l0b = {n: dr("l0b_", n, s) for n, s in [
        ("norm_w", [D]), ("in_proj_w", [4384, D]), ("conv_w", [2048, 4]),
        ("conv_b", [2048]), ("A_log", [32]), ("D", [32]), ("dt_bias", [32]),
        ("out_proj_w", [D, 2048])]}
    a_l1a = {n: dr("l1a_", n, s) for n, s in [
        ("wq1", [16, D, 64]), ("wk1", [16, D, 64]), ("wq2", [16, D, 64]),
        ("wk2", [16, D, 64]), ("wv", [16, D, 64]), ("ln_gamma", [D]),
        ("ln_beta", [D]), ("lamb", [1])]}
    a_l1f = {n: dr("l1f_", n, s) for n, s in [
        ("gamma1", [D]), ("Wq", [D, D]), ("bq", [D]), ("Wdown", [D, 64]),
        ("bdown", [64]), ("Wup", [64, 2048]), ("bup", [2048]), ("Wo", [D, D]),
        ("bo", [D]), ("gamma2", [D]), ("Wg", [D, 8]), ("bg", [8]),
        ("We", [8, D, D])]}
    a_l1b = {n: dr("l1b_", n, s) for n, s in [
        ("Wq", [D, D]), ("bq", [D]), ("Wk", [D, D]), ("bk", [D]),
        ("Wv", [D, D]), ("bv", [D]), ("Wg", [D, D]), ("bg", [D]),
        ("Wo", [D, D]), ("bo", [D])]}

    with ExitStack() as stack:
        p = Prog(nc, stack)
        p.pooling = True
        p.arena_bytes = 105216
        c = Ctx(p)
        emit_identity(p, c)
        emit_load_x(p, c, x)
        p.begin_scope()
        emit_ln_stats_only(p, c)
        emit_l0attna(p, c, a_l0a)
        p.begin_scope()
        emit_ln_to_fm(p, c)
        emit_l0ffn_full(p, c, a_l0f)
        p.begin_scope()
        emit_ln_stats_only(p, c)
        emit_l0attnb(p, c, a_l0b)
        p.begin_scope()
        emit_ln_to_fm(p, c)
        emit_l0ffn_full(p, c, a_l0f)
        p.begin_scope()
        emit_ln_to_fm(p, c)
        emit_l1attna(p, c, a_l1a)
        p.begin_scope()
        emit_ln_stats_only(p, c)
        emit_l1ffn(p, c, a_l1f)
        p.begin_scope()
        emit_ln_to_fm(p, c)
        emit_l1attnb(p, c, a_l1b)
        p.begin_scope()
        emit_ln_stats_only(p, c)
        emit_l1ffn(p, c, a_l1f)
        emit_store_xs(p, c, y)
        with nc.Block() as block:
            p.emit(block)
    return nc


_CACHE = {}


def _fused_prog():
    if "f" not in _CACHE:
        _CACHE["f"] = build_fused()
        _CACHE["rope"] = rope_tables()
    return _CACHE["f"]


_IN_MAP = [
    ("l0a_", [("W_q", "L0_attn_a_W_q"), ("W_k", "L0_attn_a_W_k"),
              ("W_v", "L0_attn_a_W_v"), ("W_o", "L0_attn_a_W_o")]),
    ("l0f_", [("w_shared_in", "L0_ffn_w_shared_in"), ("w_shared_out", "L0_ffn_w_shared_out"),
              ("w1_shared", "L0_ffn_w1_shared"), ("w2", "L0_ffn_w2"),
              ("w_gate", "L0_ffn_w_gate"), ("gate_bias", "L0_ffn_gate_bias")]),
    ("l0b_", [("norm_w", "L0_attn_b_norm_w"), ("in_proj_w", "L0_attn_b_in_proj_w"),
              ("conv_w", "L0_attn_b_conv_w"), ("conv_b", "L0_attn_b_conv_b"),
              ("A_log", "L0_attn_b_A_log"), ("D", "L0_attn_b_D"),
              ("dt_bias", "L0_attn_b_dt_bias"), ("out_proj_w", "L0_attn_b_out_proj_w")]),
    ("l1a_", [("wq1", "L1_attn_a_wq1"), ("wk1", "L1_attn_a_wk1"),
              ("wq2", "L1_attn_a_wq2"), ("wk2", "L1_attn_a_wk2"), ("wv", "L1_attn_a_wv"),
              ("ln_gamma", "L1_attn_a_ln_gamma"), ("ln_beta", "L1_attn_a_ln_beta")]),
    ("l1f_", [("gamma1", "L1_ffn_gamma1"), ("Wq", "L1_ffn_Wq"), ("bq", "L1_ffn_bq"),
              ("Wdown", "L1_ffn_Wdown"), ("bdown", "L1_ffn_bdown"),
              ("Wup", "L1_ffn_Wup"), ("bup", "L1_ffn_bup"), ("Wo", "L1_ffn_Wo"),
              ("bo", "L1_ffn_bo"), ("gamma2", "L1_ffn_gamma2"), ("Wg", "L1_ffn_Wg"),
              ("bg", "L1_ffn_bg"), ("We", "L1_ffn_We")]),
    ("l1b_", [("Wq", "L1_attn_b_Wq"), ("bq", "L1_attn_b_bq"), ("Wk", "L1_attn_b_Wk"),
              ("bk", "L1_attn_b_bk"), ("Wv", "L1_attn_b_Wv"), ("bv", "L1_attn_b_bv"),
              ("Wg", "L1_attn_b_Wg"), ("bg", "L1_attn_b_bg"), ("Wo", "L1_attn_b_Wo"),
              ("bo", "L1_attn_b_bo")]),
]


def kernel(**z):
    nc = _fused_prog()
    cos, sin = _CACHE["rope"]
    x = np.ascontiguousarray(np.asarray(z["L0_attn_a_X"], np.float32))
    common = {}
    for pref, keys in _IN_MAP:
        for a, b in keys:
            common[pref + a] = np.ascontiguousarray(np.asarray(z[b], np.float32))
    common["l0a_rope_cos"] = cos
    common["l0a_rope_sin"] = sin
    common["l1a_lamb"] = np.asarray(z["L1_attn_a_lamb"], np.float32).reshape(1)
    in_maps = [dict(common, x=np.ascontiguousarray(x[i])) for i in range(NCORES)]
    res = run_bass_kernel_spmd(nc, in_maps, core_ids=list(range(NCORES)))
    return np.stack([np.asarray(res.results[i]["y"]) for i in range(NCORES)]).astype(np.float32)



def _progs():
    if "p" not in _CACHE:
        _CACHE["p"] = {
            "l0a": build_L0_attn_a(),
            "l0f": build_L0_ffn(),
            "l0b": build_L0_attn_b(),
            "l1a": build_L1_attn_a(),
            "l1f": build_L1_ffn(),
            "l1b": build_L1_attn_b(),
        }
        _CACHE["rope"] = rope_tables()
    return _CACHE["p"]
